# version 10 (final)
# baseline (speedup 1.0000x reference)
import math
import contextlib
import numpy as np
import ml_dtypes
import concourse.bass as bass
import concourse.mybir as mybir
from concourse.bass_utils import run_bass_kernel_spmd

F32 = mybir.dt.float32
BF16 = mybir.dt.bfloat16
AF = mybir.ActivationFunctionType
ALU = mybir.AluOpType
AX = mybir.AxisListType

D = 2048
FC = 16
HQ, HKV, HD = 16, 8, 128
NE0, F0 = 8, 1024
NE1, E1 = 6, 1024
H1 = 8
TWO_PI = 6.283185307179586
LN10K = math.log(10000.0)
PI32 = float(np.float32(np.pi))
_C1 = float(np.float32(np.ldexp(np.round(np.ldexp(TWO_PI, 9)), -9)))
_C2 = float(np.float32(np.ldexp(np.round(np.ldexp(TWO_PI - _C1, 21)), -21)))
_C3 = float(np.float32(TWO_PI - _C1 - _C2))


def _oc_major(w):
    i, o = w.shape
    return np.ascontiguousarray(
        w.reshape(i // 128, 128, o // 128, 128).transpose(1, 2, 0, 3).reshape(128, -1))


def _fc_major(w):
    i, o = w.shape
    return np.ascontiguousarray(
        w.reshape(i // 128, 128, o).transpose(1, 0, 2).reshape(128, -1))


def _bias_cols(b):
    o = b.shape[0]
    return np.ascontiguousarray(b.reshape(o // 128, 128).T)


def host_prepare(inputs, TO, ncores):
    f32 = lambda k: np.asarray(inputs[k], np.float32)
    x = f32("L0_attn_hidden_states").reshape(-1, D)
    sh = {}
    sh["wq0"] = _oc_major(f32("L0_attn_Wq").T)
    sh["wk0"] = _oc_major(f32("L0_attn_Wk").T)
    sh["wv0"] = _oc_major(f32("L0_attn_Wv").T)
    sh["wo0"] = _oc_major(f32("L0_attn_Wo").T)
    sh["biases"] = np.ascontiguousarray(np.concatenate(
        [_bias_cols(f32("L0_attn_bq")), _bias_cols(f32("L0_attn_bk")),
         _bias_cols(f32("L0_attn_bv")), _bias_cols(f32("L0_attn_bo")),
         _bias_cols(f32("L0_ffn_rms_weight")), _bias_cols(f32("L1_ffn_bsu")),
         _bias_cols(f32("L1_ffn_bsa")), _bias_cols(f32("L1_ffn_bsd"))], 1))
    sh["rt0"] = np.ascontiguousarray(f32("L0_attn_rand_matrix").T)
    sh["rw0"] = _fc_major(f32("L0_ffn_router_w"))
    for e in range(NE0):
        sh[f"wg0_{e}"] = _oc_major(f32("L0_ffn_w_gate")[e])
        sh[f"wu0_{e}"] = _oc_major(f32("L0_ffn_w_up")[e])
        sh[f"wd0_{e}"] = _oc_major(f32("L0_ffn_w_down")[e])
    sh["swg"] = _oc_major(f32("L0_ffn_sw_gate"))
    sh["swu"] = _oc_major(f32("L0_ffn_sw_up"))
    sh["swd"] = _oc_major(f32("L0_ffn_sw_down"))
    sh["taunu"] = np.array([[float(f32("L1_attn_tau").reshape(-1)[0]),
                             float(f32("L1_attn_nu").reshape(-1)[0])]], np.float32)
    for j in range(2):
        sh[f"wgate1_{j}"] = _fc_major(f32("L1_ffn_Wg")[j])
        sh[f"bg1_{j}"] = np.ascontiguousarray(f32("L1_ffn_bg")[j].reshape(8, 1))
    for n in range(NE1):
        sh[f"wup1_{n}"] = _oc_major(f32("L1_ffn_Wup")[n])
        sh[f"wsw1_{n}"] = _oc_major(f32("L1_ffn_Wsw")[n])
        sh[f"wdn1_{n}"] = _oc_major(f32("L1_ffn_Wdn")[n])
    sh["bup1"] = np.ascontiguousarray(
        np.concatenate([_bias_cols(f32("L1_ffn_bup")[n]) for n in range(NE1)], 1))
    sh["bsw1"] = np.ascontiguousarray(
        np.concatenate([_bias_cols(f32("L1_ffn_bsw")[n]) for n in range(NE1)], 1))
    sh["bdn1"] = np.ascontiguousarray(f32("L1_ffn_bdn"))
    sh["wsu1"] = _oc_major(f32("L1_ffn_Wsu"))
    sh["wsa1"] = _oc_major(f32("L1_ffn_Wsa"))
    sh["wsd1"] = _oc_major(f32("L1_ffn_Wsd"))
    wq1, wk1, wv1 = f32("L1_attn_wq").T, f32("L1_attn_wk").T, f32("L1_attn_wv").T
    maps = []
    for c in range(ncores):
        xo = x[c * TO:(c + 1) * TO]
        m = dict(sh)
        m["x_fm"] = np.ascontiguousarray(
            xo.T.reshape(FC, 128, TO).transpose(1, 0, 2).reshape(128, FC * TO))
        pos0 = (c % 2) * TO
        m["posrow"] = np.arange(pos0, pos0 + TO, dtype=np.float32).reshape(1, TO)
        hf = c % 2
        m["halfsel"] = np.array([[1.0 - hf, float(hf)]], np.float32)
        hs = slice(hf * H1 * HD, (hf + 1) * H1 * HD)
        m["wq1"] = _oc_major(wq1[:, hs])
        m["wk1"] = _oc_major(wk1[:, hs])
        m["wv1"] = _oc_major(wv1[:, hs])
        maps.append(m)
    return maps


ENGS = ("tensor", "vector", "scalar", "gpsimd", "sync")


class P:
    def __init__(self, nc, sems):
        self.nc = nc
        self.sems = sems
        self.streams = {e: [] for e in ENGS}
        self.val = {n: 0 for n in sems}
        self.wm = {e: {n: 0 for n in sems} for e in ENGS}
        self.own = {"tensor": "pe", "vector": "dve", "scalar": "act",
                    "gpsimd": "gpc", "sync": None}

    def raw(self, eng, fn):
        self.streams[eng].append(fn)

    def wait(self, eng, tok):
        if tok is None:
            return
        sem, v = tok
        if v <= self.wm[eng][sem]:
            return
        self.wm[eng][sem] = v
        self.raw(eng, lambda E, h=self.sems[sem], v=v: E.wait_ge(h, v))

    def op(self, eng, fn, waits=(), inc=None, n=1, selfwait=True, ring=False):
        own = self.own[eng]
        if selfwait and own is not None and self.val[own] > 0:
            self.wait(eng, (own, self.val[own]))
        if inc is None:
            inc = own
        if ring and inc is not False and self.val[inc] > 0:
            self.wait(eng, (inc, self.val[inc]))
        for t in waits:
            self.wait(eng, t)
        if inc is False:
            self.raw(eng, fn)
            return None
        self.val[inc] += n
        self.raw(eng, lambda E, fn=fn, h=self.sems[inc], n=n: fn(E).then_inc(h, n))
        return (inc, self.val[inc])

    def tok(self, sem):
        return (sem, self.val[sem])


def cdiv(a, b):
    return (a + b - 1) // b


def build_program(TO, ncores, dbg=()):
    SEQ = 2 * TO
    TB = TO // 128
    SB = SEQ // 128
    NT5 = cdiv(TO, 512)
    TCH = min(256, TO)
    NCH = TO // TCH
    nc = bass.Bass("TRN2", target_bir_lowering=False, debug=False)

    def din(name, shape, dtype=F32):
        return nc.dram_tensor(name, list(shape), dtype, kind="ExternalInput").ap()

    I = {"x_fm": din("x_fm", (128, FC * TO)), "posrow": din("posrow", (1, TO)),
         "biases": din("biases", (128, 104)), "halfsel": din("halfsel", (1, 2))}
    B_BQ, B_BK, B_BV, B_BO, B_RMS, B_BSU, B_BSA, B_BSD = 0, 16, 24, 32, 48, 64, 80, 88
    for nm, shp in [("wq0", (128, 16 * D)), ("wk0", (128, 8 * D)), ("wv0", (128, 8 * D)),
                    ("wo0", (128, 16 * D)), ("rt0", (128, 64)), ("rw0", (128, FC * 8)),
                    ("swg", (128, 8 * D)), ("swu", (128, 8 * D)), ("swd", (128, 16 * F0)),
                    ("taunu", (1, 2)), ("wq1", (128, 8 * D)), ("wk1", (128, 8 * D)),
                    ("wv1", (128, 8 * D)), ("wsu1", (128, 16 * D)),
                    ("wsa1", (128, 8 * E1)), ("wsd1", (128, 16 * D)),
                    ("bup1", (128, NE1 * 8)), ("bsw1", (128, NE1 * 8)),
                    ("bdn1", (NE1, D))]:
        I[nm] = din(nm, shp)
    for e in range(NE0):
        I[f"wg0_{e}"] = din(f"wg0_{e}", (128, 8 * D))
        I[f"wu0_{e}"] = din(f"wu0_{e}", (128, 8 * D))
        I[f"wd0_{e}"] = din(f"wd0_{e}", (128, 16 * F0))
    for j in range(2):
        I[f"wgate1_{j}"] = din(f"wgate1_{j}", (128, FC * 8))
        I[f"bg1_{j}"] = din(f"bg1_{j}", (8, 1))
    for n in range(NE1):
        I[f"wup1_{n}"] = din(f"wup1_{n}", (128, 8 * D))
        I[f"wsw1_{n}"] = din(f"wsw1_{n}", (128, 8 * E1))
        I[f"wdn1_{n}"] = din(f"wdn1_{n}", (128, 16 * E1))

    y = nc.dram_tensor("y", [TO, D], F32, kind="ExternalOutput").ap()

    def dt_(name, shape, dtype):
        return nc.dram_tensor(name, list(shape), dtype)

    xn0_d = dt_("xn0_d", (128, FC * TO), BF16)
    hbf_d = dt_("hbf_d", (128, FC * TO), BF16)
    a0_d = dt_("a0_d", (128, FC * TO), BF16)
    x1_d = dt_("x1_d", (128, FC * TO), F32)
    xt_d = dt_("xt_d", (128, FC * TO), BF16)
    moe_d = dt_("moe_d", (128, FC * TO), F32)
    wfm_d = dt_("wfm_d", (8, TO), F32)
    wfm1_d = dt_("wfm1_d", (8, TO), F32)
    xn1_own = dt_("xn1_own", (128, FC * TO), BF16)
    xn1_full = dt_("xn1_full", (2, 128, FC * TO), BF16)
    q1_d = dt_("q1_d", (H1, 128, SEQ), BF16)
    k1_d = dt_("k1_d", (H1, 128, SEQ), BF16)
    v1_d = dt_("v1_d", (SB, 128, H1 * HD), BF16)
    mixo_d = dt_("mixo_d", (128, HQ * TO), BF16)
    mixo_full = dt_("mixo_full", (2, 128, HQ * TO), BF16)
    att_own = dt_("att_own", (H1, 128, SEQ), BF16)
    att_full = dt_("att_full", (2, H1, 128, SEQ), BF16)

    SBT = {}
    def sbuf(name, shape, dtype=F32):
        SBT[name] = nc.alloc_sbuf_tensor(name, list(shape), dtype)
        return SBT[name]

    costab = sbuf("costab", (128, TO))
    sintab = sbuf("sintab", (128, TO))
    ident = sbuf("ident", (128, 128), BF16)
    identf = sbuf("identf", (128, 128), F32)
    onesbf = sbuf("onesbf", (128, 128), BF16)
    onesdiv = sbuf("onesdiv", (128, 128), BF16)
    onesfr = sbuf("onesfr", (1, 128))
    prot = sbuf("prot", (128, 128), BF16)
    rt0 = sbuf("s_rt0", (128, 64), BF16)
    rw0 = sbuf("s_rw0", (128, FC * 8), BF16)
    biases = sbuf("s_biases", (128, 104))
    bup1 = sbuf("s_bup1", (128, NE1 * 8))
    bsw1 = sbuf("s_bsw1", (128, NE1 * 8))
    bdn1 = sbuf("s_bdn1", (8, D))
    w6 = sbuf("w6", (8, 512))
    bg1 = sbuf("s_bg1", (8, 2))
    cst = sbuf("cst", (128, 8))
    taunu = sbuf("s_taunu", (128, 2))
    hsel = sbuf("hsel", (128, 2))
    lrow2 = sbuf("lrow2", (1, 512))
    lrow = sbuf("lrow", (1, 512))
    NWS = 5
    ws = [sbuf(f"ws{i}", (128, 2048), BF16) for i in range(NWS)]
    bufA = sbuf("bufA", (128, 32768), BF16)
    bufB = sbuf("bufB", (128, 16384), BF16)
    bufC = sbuf("bufC", (128, 2048))
    xch = sbuf("xch", (128, FC * 512))
    stage = sbuf("stage", (128, 1024), BF16)
    tmp = [sbuf(f"tmp{i}", (128, 512)) for i in range(4)]
    tmpi = sbuf("tmpi", (128, 512), mybir.dt.int32)
    cmask = sbuf("cmask", (128, 128), BF16)
    stgf = sbuf("stgf", (128, 1024))
    vhp = sbuf("vhp", (8, 32 * 128), BF16)

    pb = [nc.alloc_psum_tensor(f"pb{i}", [128, 512], F32) for i in range(7)]
    pT = nc.alloc_psum_tensor("pT", [128, 1024], BF16)

    stackx = contextlib.ExitStack()
    NWSEM = NWS
    semnames = (["pe", "dve", "act", "gpc", "cc", "xd", "xd2", "sp", "od", "yd", "gd"] +
                [f"wd{i}" for i in range(NWSEM)])
    sems = {n: stackx.enter_context(nc.semaphore(n)) for n in semnames}
    p = P(nc, sems)

    def mm(waits, out_ap, lhsT_ap, rhs_ap, start=True, stop=True, tp=None):
        return p.op("tensor",
                    lambda E: E.matmul(out_ap, lhsT_ap, rhs_ap, start=start, stop=stop,
                                       tile_position=tp),
                    waits=waits, inc=("pe" if stop else False), selfwait=False)

    def tr(waits, out_ap, in_ap, idap):
        return p.op("tensor", lambda E: E.transpose(out_ap, in_ap, idap),
                    waits=waits, inc="pe", selfwait=False)

    def dve(fn, waits=()):
        return p.op("vector", fn, waits=waits)

    def act(fn, waits=()):
        return p.op("scalar", fn, waits=waits)

    def gp(fn, waits=()):
        return p.op("gpsimd", fn, waits=waits)

    def sy(fn, waits=(), inc="xd"):
        return p.op("sync", fn, waits=waits, inc=inc, n=16, selfwait=False, ring=True)

    def gdma(fn, waits=(), inc="gd"):
        return p.op("gpsimd", fn, waits=waits, inc=inc, n=16, selfwait=False, ring=True)

    wst = {"free": [None] * NWS, "seq": 0}

    def wload(src_ap, cols):
        i = wst["seq"] % NWS
        wst["seq"] += 1
        t = p.op("gpsimd",
                 lambda E, i=i, cols=cols, src_ap=src_ap:
                 E.dma_start(out=ws[i][:, :cols], in_=src_ap),
                 waits=([wst["free"][i]] if wst["free"][i] else []),
                 inc=f"wd{i}", n=16, selfwait=False, ring=True)
        return i, t

    def wfree(i, tok):
        wst["free"][i] = tok

    bank_rd = [None] * 7
    pT_rd = [None]

    def engtoks():
        return [p.tok("pe"), p.tok("dve"), p.tok("act")]

    dbg_aps = {}
    def dbg_out(name, src_ap, waits):
        if name in dbg:
            ap = nc.dram_tensor("dbg_" + name, list(src_ap.shape), src_ap.dtype,
                                kind="ExternalOutput").ap()
            dbg_aps[name] = ap
            sy(lambda E: E.dma_start(out=ap, in_=src_ap), waits=waits, inc="od")

    t_b = sy(lambda E: E.dma_start(out=biases[:, :], in_=I["biases"]))
    t_tn0 = sy(lambda E: E.dma_start(out=taunu[:1, :2], in_=I["taunu"]))
    t_hs0 = sy(lambda E: E.dma_start(out=hsel[:1, :2], in_=I["halfsel"]))
    t_bup = sy(lambda E: E.dma_start(out=bup1[:, :], in_=I["bup1"]))
    t_bsw = sy(lambda E: E.dma_start(out=bsw1[:, :], in_=I["bsw1"]))
    t_bg0 = sy(lambda E: E.dma_start(out=bg1[:8, 0:1], in_=I["bg1_0"]))
    t_bg1 = sy(lambda E: E.dma_start(out=bg1[:8, 1:2], in_=I["bg1_1"]))
    t_bdn = sy(lambda E: E.dma_start(out=bdn1[:NE1, :], in_=I["bdn1"]), inc="xd2")
    t_rt = gdma(lambda E: E.dma_start(out=rt0[:, :], in_=I["rt0"]))
    t_rw = gdma(lambda E: E.dma_start(out=rw0[:, :], in_=I["rw0"]))
    t_ones = dve(lambda E: E.memset(onesbf[:, :], 1.0))
    t_odiv = dve(lambda E: E.memset(onesdiv[:, :], 1.0 / 2048.0))
    t_ofr = dve(lambda E: E.memset(onesfr[:, :], 1.0))
    t_c = dve(lambda E: E.memset(cst[:, :], 0.0))
    t_c0 = dve(lambda E: E.memset(cst[:, 0:1], 1e-5), waits=[t_c])
    t_c2 = dve(lambda E: E.memset(cst[:, 2:3], 1e-6), waits=[t_c])
    t_of = dve(lambda E: E.memset(stgf[:, :128], 1.0))
    t_n1 = dve(lambda E: E.memset(stgf[:, 128:256], -1.0))
    t_id = gp(lambda E: E.affine_select(ident[:, :], onesbf[:, :], pattern=[[1, 128]],
                                        compare_op=ALU.is_equal, fill=0.0, base=0,
                                        channel_multiplier=-1), waits=[t_ones])
    t_idf = gp(lambda E: E.affine_select(identf[:, :], stgf[:, :128], pattern=[[1, 128]],
                                         compare_op=ALU.is_equal, fill=0.0, base=0,
                                         channel_multiplier=-1), waits=[t_of])
    t_cmi = gp(lambda E: E.iota(stgf[:, 256:384], pattern=[[1, 128]], base=0,
                                channel_multiplier=-1,
                                allow_small_or_imprecise_dtypes=True))
    t_cm = dve(lambda E: E.tensor_scalar(cmask[:, :], stgf[:, 256:384], 0.5, None,
                                         ALU.is_lt), waits=[t_cmi])
    t_pa = gp(lambda E: E.affine_select(prot[:, :], onesbf[:, :], pattern=[[1, 128]],
                                        compare_op=ALU.is_equal, fill=0.0, base=-64,
                                        channel_multiplier=-1), waits=[t_ones])
    t_pb = gp(lambda E: E.affine_select(stage[:, :128], stgf[:, 128:256],
                                        pattern=[[1, 128]], compare_op=ALU.is_equal,
                                        fill=0.0, base=64, channel_multiplier=-1),
              waits=[t_n1])
    t_prot = dve(lambda E: E.tensor_add(prot[:, :], prot[:, :], stage[:, :128]),
                 waits=[t_pa, t_pb])
    t_tnm = mm([t_tn0, t_ofr, bank_rd[6]], pb[6][:, :2], onesfr[:1, :],
               taunu[:1, :2])
    t_tn = dve(lambda E: E.tensor_copy(taunu[:, :2], pb[6][:, :2]), waits=[t_tnm])
    t_hsm = mm([t_hs0, t_ofr, t_tn], pb[6][:, 2:4], onesfr[:1, :], hsel[:1, :2])
    t_hsel = dve(lambda E: E.tensor_copy(hsel[:, :2], pb[6][:, 2:4]), waits=[t_hsm])
    bank_rd[6] = t_hsel

    xn0v = xn0_d.ap().rearrange("p (c t) -> p c t", t=TO)
    hbfv = hbf_d.ap().rearrange("p (c t) -> p c t", t=TO)
    a0v = a0_d.ap().rearrange("p (c t) -> p c t", t=TO)
    x1v = x1_d.ap().rearrange("p (c t) -> p c t", t=TO)
    xfmv = I["x_fm"].rearrange("p (c t) -> p c t", t=TO)
    xn1v = xn1_own.ap().rearrange("p (c t) -> p c t", t=TO)
    xtv = xt_d.ap().rearrange("p (c t) -> p c t", t=TO)

    def ln_chunk(src_view, t0, w5, out_view, wait_src):
        t_xc = sy(lambda E: E.dma_start(
            out=xch[:, :FC * w5].rearrange("p (c t) -> p c t", t=w5),
            in_=src_view[:, :, t0:t0 + w5]),
            waits=([wait_src] if wait_src else []) + engtoks())
        xv = xch[:, :FC * w5].rearrange("p (c t) -> p c t", t=w5)
        cps, sqs = [], []
        for fc in range(FC):
            cps.append(dve(lambda E, fc=fc: E.tensor_copy(
                bufA[:, fc * w5:(fc + 1) * w5], xv[:, fc, :]), waits=[t_xc]))
            sqs.append(act(lambda E, fc=fc: E.activation(
                bufB[:, fc * w5:(fc + 1) * w5], xv[:, fc, :], AF.Square),
                waits=[t_xc]))
        for fc in range(FC):
            mm([cps[fc], t_odiv] + ([bank_rd[0]] if fc == 0 else []),
               pb[0][:, :w5], onesdiv[:, :], bufA[:, fc * w5:(fc + 1) * w5],
               start=(fc == 0), stop=(fc == FC - 1))
            mm([sqs[fc]] + ([bank_rd[1]] if fc == 0 else []),
               pb[1][:, :w5], onesdiv[:, :], bufB[:, fc * w5:(fc + 1) * w5],
               start=(fc == 0), stop=(fc == FC - 1))
        t_mm = p.tok("pe")
        t_m2 = act(lambda E: E.activation(stgf[:, :w5], pb[0][:, :w5], AF.Square),
                   waits=[t_mm])
        t_var = dve(lambda E: E.tensor_sub(tmp[0][:, :w5], pb[1][:, :w5],
                                           stgf[:, :w5]), waits=[t_mm, t_m2])
        bank_rd[1] = t_var
        t_ln = act(lambda E: E.activation(tmp[1][:, :w5], tmp[0][:, :w5], AF.Ln,
                                          bias=cst[:, 0:1]), waits=[t_var, t_c0])
        t_rstd = act(lambda E: E.activation(tmp[1][:, :w5], tmp[1][:, :w5], AF.Exp,
                                            scale=-0.5), waits=[t_ln])
        t_xnl = None
        for fc in range(FC):
            t_d = dve(lambda E, fc=fc: E.tensor_sub(tmp[2][:, :w5], xv[:, fc, :],
                                                    pb[0][:, :w5]),
                      waits=[t_mm, t_rstd, t_xc])
            t_xnl = dve(lambda E, fc=fc: E.tensor_mul(
                bufA[:, fc * w5:(fc + 1) * w5], tmp[2][:, :w5], tmp[1][:, :w5]),
                waits=[t_d])
        bank_rd[0] = t_xnl
        t_st = sy(lambda E: E.dma_start(
            out=out_view[:, :, t0:t0 + w5],
            in_=bufA[:, :FC * w5].rearrange("p (c t) -> p c t", t=w5)),
            waits=[t_xnl], inc="sp")
        return t_xnl, t_st

    for th in range(NT5):
        t0 = th * 512
        w5 = min(512, TO - t0)
        t_xnl, _ = ln_chunk(xfmv, t0, w5, xn0v, None)
        sqs = []
        for fc in range(FC):
            sqs.append(act(lambda E, fc=fc: E.activation(
                bufB[:, fc * w5:(fc + 1) * w5], bufA[:, fc * w5:(fc + 1) * w5],
                AF.Square), waits=[t_xnl]))
        for fc in range(FC):
            mm([sqs[fc]] + ([bank_rd[2]] if fc == 0 else []),
               pb[2][:, :w5], onesdiv[:, :], bufB[:, fc * w5:(fc + 1) * w5],
               start=(fc == 0), stop=(fc == FC - 1))
        t_mm2 = p.tok("pe")
        t_l2 = act(lambda E: E.activation(tmp[0][:, :w5], pb[2][:, :w5], AF.Ln,
                                          bias=cst[:, 2:3]), waits=[t_mm2, t_c2])
        t_r2 = act(lambda E: E.activation(tmp[0][:, :w5], tmp[0][:, :w5], AF.Exp,
                                          scale=-0.5), waits=[t_l2])
        bank_rd[2] = t_r2
        t_hl = None
        for fc in range(FC):
            t_hl = dve(lambda E, fc=fc: E.scalar_tensor_tensor(
                bufB[:, fc * w5:(fc + 1) * w5], bufA[:, fc * w5:(fc + 1) * w5],
                biases[:, B_RMS + fc:B_RMS + fc + 1], tmp[0][:, :w5],
                ALU.mult, ALU.mult), waits=[t_r2, t_b])
        sy(lambda E, t0=t0, w5=w5: E.dma_start(
            out=hbfv[:, :, t0:t0 + w5],
            in_=bufB[:, :FC * w5].rearrange("p (c t) -> p c t", t=w5)),
            waits=[t_hl], inc="sp")
    t_ln_all = p.tok("sp")
    dbg_out("xn0", xn0_d.ap(), [t_ln_all])
    dbg_out("hbf", hbf_d.ap(), [t_ln_all])

    t_ior = gp(lambda E: E.iota(
        lrow[:1, :128].rearrange("o (a b) -> o a b", b=64), pattern=[[0, 2], [1, 64]],
        base=0, channel_multiplier=0, allow_small_or_imprecise_dtypes=True))
    t_if = act(lambda E: E.activation(lrow[:1, 128:256], lrow[:1, :128], AF.Exp,
                                      scale=-LN10K / 64.0), waits=[t_ior])
    for j5 in range(NT5):
        a = j5 * 512
        w5 = min(512, TO - a)
        t_posc = sy(lambda E, a=a, w5=w5: E.dma_start(out=lrow2[:1, :w5],
                                                      in_=I["posrow"][:, a:a + w5]),
                    waits=[p.tok("pe")])
        t_am = mm([t_if, t_posc, bank_rd[3]], pb[3][:, :w5], lrow[:1, 128:256],
                  lrow2[:1, :w5])
        t_y = dve(lambda E, w5=w5: E.tensor_scalar(tmp[0][:, :w5], pb[3][:, :w5],
                                                   1.0 / TWO_PI, 24.5, ALU.mult,
                                                   ALU.add), waits=[t_am])
        t_ki = dve(lambda E, w5=w5: E.tensor_copy(tmpi[:, :w5], tmp[0][:, :w5]),
                   waits=[t_y])
        t_kf = dve(lambda E, w5=w5: E.tensor_copy(tmp[2][:, :w5], tmpi[:, :w5]),
                   waits=[t_ki])
        t_d = dve(lambda E, w5=w5: E.tensor_sub(tmp[0][:, :w5], tmp[0][:, :w5],
                                                tmp[2][:, :w5]), waits=[t_kf])
        t_g = dve(lambda E, w5=w5: E.tensor_scalar(tmp[0][:, :w5], tmp[0][:, :w5],
                                                   0.0, None, ALU.is_ge),
                  waits=[t_d])
        t_kk = dve(lambda E, w5=w5: E.scalar_tensor_tensor(
            tmp[0][:, :w5], tmp[0][:, :w5], -25.0, tmp[2][:, :w5], ALU.add, ALU.add),
            waits=[t_g])
        t_r1 = dve(lambda E, w5=w5: E.scalar_tensor_tensor(
            tmp[1][:, :w5], tmp[0][:, :w5], -_C1, pb[3][:, :w5], ALU.mult, ALU.add),
            waits=[t_kk, t_am])
        bank_rd[3] = t_r1
        t_r2 = dve(lambda E, w5=w5: E.scalar_tensor_tensor(
            tmp[1][:, :w5], tmp[0][:, :w5], -_C2, tmp[1][:, :w5], ALU.mult, ALU.add),
            waits=[t_r1])
        t_red2 = dve(lambda E, w5=w5: E.scalar_tensor_tensor(
            tmp[1][:, :w5], tmp[0][:, :w5], -_C3, tmp[1][:, :w5], ALU.mult, ALU.add),
            waits=[t_r2])
        act(lambda E, a=a, w5=w5: E.activation(sintab[:, a:a + w5], tmp[1][:, :w5],
                                               AF.Sin), waits=[t_red2])
        t_s2 = act(lambda E, w5=w5: E.activation(tmp[2][:, :w5], tmp[1][:, :w5],
                                                 AF.Sin, scale=0.5), waits=[t_red2])
        t_sq = act(lambda E, w5=w5: E.activation(tmp[2][:, :w5], tmp[2][:, :w5],
                                                 AF.Square), waits=[t_s2])
        dve(lambda E, a=a, w5=w5: E.tensor_scalar(costab[:, a:a + w5], tmp[2][:, :w5],
                                                  -2.0, 1.0, ALU.mult, ALU.add),
            waits=[t_sq])
    t_tabs = [p.tok("dve"), p.tok("act")]
    dbg_out("sintab", sintab[:, :], t_tabs)
    dbg_out("costab", costab[:, :], t_tabs)

    for tb in range(TB):
        tt0 = tb * 128
        t_h = sy(lambda E, tt0=tt0: E.dma_start(
            out=bufB[:, :FC * 128].rearrange("p (c t) -> p c t", t=128),
            in_=hbfv[:, :, tt0:tt0 + 128]), waits=[t_ln_all] + engtoks())
        hv = bufB[:, :FC * 128].rearrange("p (c t) -> p c t", t=128)
        for fc in range(FC):
            mm([t_h, t_rw] + ([bank_rd[4]] if fc == 0 else []), pb[4][:8, :128],
               rw0[:, fc * 8:(fc + 1) * 8], hv[:, fc, :],
               start=(fc == 0), stop=(fc == FC - 1))
        t_lg = p.tok("pe")
        t_lb = dve(lambda E: E.tensor_copy(stgf[:8, :128], pb[4][:8, :128]),
                   waits=[t_lg])
        bank_rd[4] = t_lb
        t_tr = tr([t_lb, t_idf, bank_rd[5]], pb[5][:, :8], stgf[:8, :128],
                  identf[:8, :8])
        t_tm = dve(lambda E: E.tensor_copy(stgf[:, 512:520], pb[5][:, :8]),
                   waits=[t_tr])
        bank_rd[5] = t_tm
        lg = stgf[:, 512:520]
        t_v1 = dve(lambda E: E.reduce_max(stgf[:, 520:521], lg, axis=AX.X),
                   waits=[t_tm])
        t_e1 = dve(lambda E: E.tensor_scalar(tmp[0][:, :8], lg, stgf[:, 520:521],
                                             None, ALU.is_ge), waits=[t_v1])
        t_x2 = dve(lambda E: E.scalar_tensor_tensor(
            tmp[1][:, :8], tmp[0][:, :8], -1e30, lg, ALU.mult, ALU.add),
            waits=[t_e1])
        t_v2 = dve(lambda E: E.reduce_max(stgf[:, 521:522], tmp[1][:, :8], axis=AX.X),
                   waits=[t_x2])
        t_e2 = dve(lambda E: E.tensor_scalar(tmp[1][:, :8], tmp[1][:, :8],
                                             stgf[:, 521:522], None, ALU.is_ge),
                   waits=[t_v2])
        t_dl = dve(lambda E: E.tensor_sub(stgf[:, 522:523], stgf[:, 520:521],
                                          stgf[:, 521:522]), waits=[t_v2])
        t_th = act(lambda E: E.activation(stgf[:, 523:524], stgf[:, 522:523],
                                          AF.Tanh, scale=0.5), waits=[t_dl])
        t_w1 = dve(lambda E: E.tensor_scalar(stgf[:, 523:524], stgf[:, 523:524],
                                             0.5, 0.5, ALU.mult, ALU.add),
                   waits=[t_th])
        t_w2 = dve(lambda E: E.tensor_scalar(stgf[:, 524:525], stgf[:, 523:524],
                                             -1.0, 1.0, ALU.mult, ALU.add),
                   waits=[t_w1])
        t_a1 = dve(lambda E: E.tensor_scalar(tmp[0][:, :8], tmp[0][:, :8],
                                             stgf[:, 523:524], None, ALU.mult),
                   waits=[t_w1, t_e1])
        t_wall = dve(lambda E: E.scalar_tensor_tensor(
            stgf[:, 528:536], tmp[1][:, :8], stgf[:, 524:525], tmp[0][:, :8],
            ALU.mult, ALU.add), waits=[t_w2, t_a1, t_e2])
        t_tw = tr([t_wall, t_idf, bank_rd[6]], pb[6][:8, :128], stgf[:, 528:536],
                  identf[:, :128])
        t_we = dve(lambda E: E.tensor_copy(stgf[:8, 640:768], pb[6][:8, :128]),
                   waits=[t_tw, p.tok("sp")])
        bank_rd[6] = t_we
        sy(lambda E, tt0=tt0: E.dma_start(out=wfm_d.ap()[:8, tt0:tt0 + 128],
                                          in_=stgf[:8, 640:768]),
           waits=[t_we], inc="sp")
    t_wfst = p.tok("sp")
    dbg_out("wfm", wfm_d.ap(), [t_wfst])


    O_QSB, O_QP, O_KP, O_VSB = 0, 24 * TCH, 40 * TCH, 48 * TCH
    O_MIXO, O_INVN, O_XN = 0, 16 * TCH, 40 * TCH
    for ch in range(NCH):
        c0 = ch * TCH
        t_xn = sy(lambda E, c0=c0: E.dma_start(
            out=bufB[:, O_XN:O_XN + FC * TCH].rearrange("p (c t) -> p c t", t=TCH),
            in_=xn0v[:, :, c0:c0 + TCH]), waits=[t_ln_all] + engtoks())
        xnv = bufB[:, O_XN:O_XN + FC * TCH].rearrange("p (c t) -> p c t", t=TCH)
        for pname, wnm, nh, bcol, obase in [
                ("q", "wq0", HQ, B_BQ, O_QSB),
                ("k", "wk0", HKV, B_BK, O_QSB + HQ * TCH),
                ("v", "wv0", HKV, B_BV, O_VSB)]:
            for hh in range(nh):
                si, t_w = wload(I[wnm][:, hh * D:(hh + 1) * D], D)
                wtv = ws[si][:, :D].rearrange("p (c o) -> p c o", o=128)
                bank = pb[hh % 2]
                for fc in range(FC):
                    mm([t_w, t_xn] + ([bank_rd[hh % 2]] if fc == 0 else []),
                       bank[:, :TCH], wtv[:, fc, :], xnv[:, fc, :],
                       start=(fc == 0), stop=(fc == FC - 1))
                t_mm = p.tok("pe")
                wfree(si, t_mm)
                dst = bufA[:, obase + hh * TCH: obase + (hh + 1) * TCH]
                t_ev = act(lambda E, dst=dst, bank=bank, bcol=bcol, hh=hh:
                           E.activation(dst, bank[:, :TCH], AF.Identity,
                                        bias=biases[:, bcol + hh:bcol + hh + 1]),
                           waits=[t_mm, t_b])
                bank_rd[hh % 2] = t_ev
                if pname != "v":
                    hidx = hh if pname == "q" else HQ + hh
                    t_sq = act(lambda E, dst=dst: E.activation(
                        stage[:, :TCH], dst, AF.Square), waits=[t_ev])
                    t_n2 = mm([t_sq, t_ones, bank_rd[2]], pb[2][:, :TCH],
                              onesbf[:, :], stage[:, :TCH])
                    t_mx = dve(lambda E: E.tensor_scalar(
                        tmp[0][:, :TCH], pb[2][:, :TCH], 1e-12, None, ALU.max),
                        waits=[t_n2])
                    bank_rd[2] = t_mx
                    t_l = act(lambda E: E.activation(tmp[0][:, :TCH], tmp[0][:, :TCH],
                                                     AF.Ln), waits=[t_mx])
                    act(lambda E, hidx=hidx: E.activation(
                        bufB[:, O_INVN + hidx * TCH: O_INVN + (hidx + 1) * TCH],
                        tmp[0][:, :TCH], AF.Exp, scale=-0.5), waits=[t_l])
        t_proj = [p.tok("act"), p.tok("dve")]

        for hh in range(HQ + HKV):
            src = bufA[:, O_QSB + hh * TCH: O_QSB + (hh + 1) * TCH]
            t_rot = mm(t_proj + [t_prot, bank_rd[2]], pb[2][:, :TCH], prot[:, :], src)
            t_qc = dve(lambda E, src=src, c0=c0: E.tensor_mul(
                tmp[0][:, :TCH], src, costab[:, c0:c0 + TCH]),
                waits=t_tabs + t_proj)
            t_rs = dve(lambda E, c0=c0: E.tensor_mul(
                tmp[1][:, :TCH], pb[2][:, :TCH], sintab[:, c0:c0 + TCH]),
                waits=[t_rot] + t_tabs)
            bank_rd[2] = t_rs
            t_qr = dve(lambda E: E.tensor_add(stage[:, :TCH], tmp[0][:, :TCH],
                                              tmp[1][:, :TCH]), waits=[t_qc, t_rs])
            t_p0 = mm([t_qr, t_rt, bank_rd[2]], pb[2][:64, :TCH], rt0[:, :],
                      stage[:, :TCH])
            t_ps = dve(lambda E, hh=hh: E.tensor_mul(
                tmp[0][:64, :TCH], pb[2][:64, :TCH],
                bufB[:64, O_INVN + hh * TCH: O_INVN + (hh + 1) * TCH]),
                waits=[t_p0])
            bank_rd[2] = t_ps
            t_y = dve(lambda E: E.tensor_scalar(tmp[1][:64, :TCH], tmp[0][:64, :TCH],
                                                1.0 / TWO_PI, 24.5, ALU.mult, ALU.add),
                      waits=[t_ps])
            t_ki = dve(lambda E: E.tensor_copy(tmpi[:64, :TCH], tmp[1][:64, :TCH]),
                       waits=[t_y])
            t_kf = dve(lambda E: E.tensor_copy(tmp[3][:64, :TCH], tmpi[:64, :TCH]),
                       waits=[t_ki])
            t_d = dve(lambda E: E.tensor_sub(tmp[1][:64, :TCH], tmp[1][:64, :TCH],
                                             tmp[3][:64, :TCH]), waits=[t_kf])
            t_g = dve(lambda E: E.tensor_scalar(tmp[1][:64, :TCH], tmp[1][:64, :TCH],
                                                0.0, None, ALU.is_ge), waits=[t_d])
            t_k = dve(lambda E: E.scalar_tensor_tensor(
                tmp[1][:64, :TCH], tmp[1][:64, :TCH], -25.0, tmp[3][:64, :TCH],
                ALU.add, ALU.add), waits=[t_g])
            t_q1r = dve(lambda E: E.scalar_tensor_tensor(
                tmp[2][:64, :TCH], tmp[1][:64, :TCH], -_C1, tmp[0][:64, :TCH],
                ALU.mult, ALU.add), waits=[t_k])
            t_q2r = dve(lambda E: E.scalar_tensor_tensor(
                tmp[2][:64, :TCH], tmp[1][:64, :TCH], -_C2, tmp[2][:64, :TCH],
                ALU.mult, ALU.add), waits=[t_q1r])
            t_red = dve(lambda E: E.scalar_tensor_tensor(
                tmp[0][:64, :TCH], tmp[1][:64, :TCH], -_C3, tmp[2][:64, :TCH],
                ALU.mult, ALU.add), waits=[t_q2r])
            dq = (bufA[:, O_QP + hh * TCH: O_QP + (hh + 1) * TCH] if hh < HQ else
                  bufA[:, O_KP + (hh - HQ) * TCH: O_KP + (hh - HQ + 1) * TCH])
            act(lambda E, dq=dq: E.activation(dq[:64, :], tmp[0][:64, :TCH], AF.Sin),
                waits=[t_red])
            t_s2 = act(lambda E: E.activation(tmp[3][:64, :TCH], tmp[0][:64, :TCH],
                                              AF.Sin, scale=0.5), waits=[t_red])
            t_sqq = act(lambda E: E.activation(tmp[3][:64, :TCH], tmp[3][:64, :TCH],
                                               AF.Square), waits=[t_s2])
            dve(lambda E, dq=dq: E.tensor_scalar(dq[64:128, :], tmp[3][:64, :TCH],
                                                 -2.0, 1.0, ALU.mult, ALU.add),
                waits=[t_sqq])
        t_feats = [p.tok("dve"), p.tok("act")]

        qpv = bufA[:, O_QP:O_QP + HQ * TCH].rearrange("p (h t) -> p h t", t=TCH)
        kpv = bufA[:, O_KP:O_KP + HKV * TCH].rearrange("p (h t) -> p h t", t=TCH)
        t_mm2_prev = None
        for sb32 in range(TCH // 32):
            tl0 = sb32 * 32
            for i in range(HKV):
                t_t = tr(t_feats + [t_id, pT_rd[0]], pT[:32, :128],
                         bufA[:, O_VSB + i * TCH + tl0: O_VSB + i * TCH + tl0 + 32],
                         ident[:, :])
                t_e = dve(lambda E: E.tensor_scalar(stage[:32, :128], pT[:32, :128],
                                                    2.0 / 128.0, None, ALU.mult),
                          waits=[t_t, p.tok("gd")])
                pT_rd[0] = t_e
                gdma(lambda E, i=i: E.dma_start(out=vhp[i:i + 1, :],
                                                in_=stage[:32, :128]),
                     waits=[t_e] + ([t_mm2_prev] if t_mm2_prev else []))
            t_vhp = p.tok("gd")
            for sl in range(32):
                s = tl0 + sl
                mm(t_feats + ([bank_rd[0]] if sl == 0 else []),
                   pb[0][:8, sl * 16:sl * 16 + 16], kpv[:, :, s], qpv[:, :, s])
            t_mm1 = p.tok("pe")
            t_ae = dve(lambda E: E.tensor_copy(stage[:8, 512:1024],
                                               pb[0][:8, :512]), waits=[t_mm1])
            bank_rd[0] = t_ae
            for sl in range(32):
                mm([t_ae, t_vhp] + ([bank_rd[1]] if sl == 0 else []),
                   pb[1][:, sl * 16:sl * 16 + 16],
                   vhp[:8, sl * 128:(sl + 1) * 128],
                   stage[:8, 512 + sl * 16:512 + sl * 16 + 16])
            t_mm2 = p.tok("pe")
            t_mm2_prev = t_mm2
            t_me = dve(lambda E, tl0=tl0: E.tensor_copy(
                bufB[:, O_MIXO:O_MIXO + HQ * TCH].rearrange(
                    "p (h t) -> p t h", h=HQ)[:, tl0: tl0 + 32, :],
                pb[1][:, :512].rearrange("p (t h) -> p t h", h=16)),
                waits=[t_mm2])
            bank_rd[1] = t_me
        t_mix = [p.tok("dve")]
        sy(lambda E, c0=c0: E.dma_start(
            out=mixo_d.ap().rearrange("p (h t) -> p h t", t=TO)[:, :, c0:c0 + TCH],
            in_=bufB[:, O_MIXO:O_MIXO + HQ * TCH].rearrange("p (h t) -> p h t",
                                                            t=TCH)),
            waits=t_mix, inc="sp")
    t_mixd = p.tok("sp")

    groups = [[2 * i, 2 * i + 1] for i in range(max(1, ncores // 2))]
    t_cc3 = p.op("gpsimd", lambda E: E.collective_compute(
        "AllGather", ALU.bypass, replica_groups=groups,
        ins=[mixo_d.ap().opt()], outs=[mixo_full.ap().opt()]),
        waits=[t_mixd], inc="cc", selfwait=False)

    HBLK = SEQ // 16
    for i in range(TO // HBLK):
        S2T = 2 * TO
        t_lA = sy(lambda E, i=i: E.dma_start(
            out=bufA[:, :S2T].rearrange("p (r t) -> p r t", r=2),
            in_=mixo_full.ap().rearrange("r p (h t) -> r p h t", t=TO)
            [:, :, i, :].rearrange("r p t -> p r t")),
            waits=[t_cc3] + engtoks())
        t_lB = sy(lambda E, i=i: E.dma_start(
            out=bufA[:, S2T:2 * S2T].rearrange("p (r t) -> p r t", r=2),
            in_=mixo_full.ap().rearrange("r p (h t) -> r p h t", t=TO)
            [:, :, 8 + i, :].rearrange("r p t -> p r t")),
            waits=[t_cc3] + engtoks(), inc="xd2")
        t_se = dve(lambda E: E.tensor_scalar(bufA[:, 2 * S2T:3 * S2T],
                                             bufA[:, :S2T],
                                             hsel[:, 0:1], None, ALU.mult),
                   waits=[t_lA, t_hsel])
        t_cb = dve(lambda E: E.scalar_tensor_tensor(
            bufA[:, 2 * S2T:3 * S2T], bufA[:, S2T:2 * S2T], hsel[:, 1:2],
            bufA[:, 2 * S2T:3 * S2T], ALU.mult, ALU.add), waits=[t_lB, t_se])
        t_pm = dve(lambda E, i=i: E.tensor_copy(
            bufB[:, :HQ * TO].rearrange("p (h t) -> p h t", t=TO)
            [:, :, i * HBLK:(i + 1) * HBLK],
            bufA[:, 2 * S2T:3 * S2T].rearrange("p (u h) -> p h u", h=16)),
            waits=[t_cb])
    t_a0in = p.tok("dve")
    for oc in range(FC):
        si, t_w = wload(I["wo0"][:, oc * D:(oc + 1) * D], D)
        wtv = ws[si][:, :D].rearrange("p (c o) -> p c o", o=128)
        for j5 in range(NT5):
            a5 = j5 * 512
            w5 = min(512, TO - a5)
            bank = pb[j5 % 2]
            for fc in range(FC):
                mm([t_w, t_a0in] + ([bank_rd[j5 % 2]] if fc == 0 else []),
                   bank[:, :w5], wtv[:, fc, :],
                   bufB[:, fc * TO + a5: fc * TO + a5 + w5],
                   start=(fc == 0), stop=(fc == FC - 1))
            t_mm = p.tok("pe")
            t_ev = act(lambda E, bank=bank, oc=oc, w5=w5: E.activation(
                stage[:, :w5], bank[:, :w5], AF.Identity,
                bias=biases[:, B_BO + oc:B_BO + oc + 1]),
                waits=[t_mm, t_b, p.tok("sp")])
            bank_rd[j5 % 2] = t_ev
            sy(lambda E, oc=oc, a5=a5, w5=w5: E.dma_start(
                out=a0v[:, oc, a5:a5 + w5], in_=stage[:, :w5]),
                waits=[t_ev], inc="sp")
        wfree(si, p.tok("pe"))
    t_a0 = p.tok("sp")
    dbg_out("a0", a0_d.ap(), [t_a0])

    A_HB, A_HE, A_WBC = 16384, 24576, 28672
    moev = moe_d.ap().rearrange("p (c t) -> p c t", t=TO)
    s4st = {"spill": None, "guctr": 0, "spills": []}

    def mk_group(a5, w5):
        hvv = bufA[:, A_HB:A_HB + FC * w5].rearrange("p (c t) -> p c t", t=w5)
        st = {"t_hb": None, "t_wbc": None, "t_h": None}
        units = []

        def u_load():
            st["t_hb"] = sy(lambda E: E.dma_start(
                out=bufA[:, A_HB:A_HB + FC * w5].rearrange("p (c t) -> p c t",
                                                           t=w5),
                in_=hbfv[:, :, a5:a5 + w5]),
                waits=[t_ln_all, p.tok("pe")], inc="xd2")
        units.append(u_load)

        def u_wbc(e):
            t_rr = sy(lambda E, e=e: E.dma_start(
                out=lrow[:1, :w5], in_=wfm_d.ap()[e:e + 1, a5:a5 + w5]),
                waits=[t_wfst, p.tok("pe")], inc="xd2")
            t_bc = mm([t_rr, t_ofr, bank_rd[5]], pb[5][:, :w5], onesfr[:1, :],
                      lrow[:1, :w5])
            st["t_wbc"] = dve(lambda E: E.tensor_copy(
                bufA[:, A_WBC:A_WBC + w5], pb[5][:, :w5]), waits=[t_bc])
            bank_rd[5] = st["t_wbc"]

        def u_gu(wg_ap, wu_ap, oc8, scaled):
            bg_ = [3, 4][s4st["guctr"] % 2]
            bu_ = [5, 6][s4st["guctr"] % 2]
            s4st["guctr"] += 1
            sg, t_wg = wload(wg_ap, D)
            su, t_wu = wload(wu_ap, D)
            wgv = ws[sg][:, :D].rearrange("p (c o) -> p c o", o=128)
            wuv = ws[su][:, :D].rearrange("p (c o) -> p c o", o=128)
            for fc in range(FC):
                mm([t_wg, st["t_hb"]] + ([bank_rd[bg_]] if fc == 0 else []),
                   pb[bg_][:, :w5], wgv[:, fc, :], hvv[:, fc, :],
                   start=(fc == 0), stop=(fc == FC - 1))
                mm([t_wu] + ([bank_rd[bu_]] if fc == 0 else []),
                   pb[bu_][:, :w5], wuv[:, fc, :], hvv[:, fc, :],
                   start=(fc == 0), stop=(fc == FC - 1))
            t_gu = p.tok("pe")
            wfree(sg, t_gu)
            wfree(su, t_gu)
            t_s = act(lambda E, bg_=bg_: E.activation(bufC[:, :w5],
                                                      pb[bg_][:, :w5], AF.Silu),
                      waits=[t_gu, p.tok("dve")])
            bank_rd[bg_] = t_s
            t_u = act(lambda E, bu_=bu_: E.activation(bufC[:, 512:512 + w5],
                                                      pb[bu_][:, :w5],
                                                      AF.Identity),
                      waits=[t_gu])
            bank_rd[bu_] = t_u
            dst = bufA[:, A_HE + oc8 * 512: A_HE + oc8 * 512 + w5]
            if not scaled:
                st["t_h"] = dve(lambda E, dst=dst: E.tensor_mul(
                    dst, bufC[:, :w5], bufC[:, 512:512 + w5]), waits=[t_s, t_u])
            else:
                t_h0 = dve(lambda E: E.tensor_mul(
                    bufC[:, 1024:1024 + w5], bufC[:, :w5], bufC[:, 512:512 + w5]),
                    waits=[t_s, t_u])
                st["t_h"] = dve(lambda E, dst=dst: E.tensor_mul(
                    dst, bufC[:, 1024:1024 + w5], bufA[:, A_WBC:A_WBC + w5]),
                    waits=[t_h0, st["t_wbc"]])

        def u_down(w_ap, oc, fresh):
            bi = [3, 4][oc % 2]
            sd, t_d = wload(w_ap, F0)
            wdv = ws[sd][:, :F0].rearrange("p (c o) -> p c o", o=128)
            for fcf in range(8):
                mm([t_d, st["t_h"]] + ([bank_rd[bi]] if fcf == 0 else []),
                   pb[bi][:, :w5], wdv[:, fcf, :],
                   bufA[:, A_HE + fcf * 512: A_HE + fcf * 512 + w5],
                   start=(fcf == 0), stop=(fcf == 7))
            t_dn = p.tok("pe")
            wfree(sd, t_dn)
            if fresh:
                t_ac = dve(lambda E, oc=oc, bi=bi: E.tensor_copy(
                    xch[:, oc * w5:(oc + 1) * w5], pb[bi][:, :w5]),
                    waits=[t_dn] + ([s4st["spill"]] if s4st["spill"] else []))
            else:
                t_ac = dve(lambda E, oc=oc, bi=bi: E.tensor_add(
                    xch[:, oc * w5:(oc + 1) * w5],
                    xch[:, oc * w5:(oc + 1) * w5], pb[bi][:, :w5]),
                    waits=[t_dn])
            bank_rd[bi] = t_ac

        for e in range(NE0):
            units.append(lambda e=e: u_wbc(e))
            for oc8 in range(8):
                units.append(lambda e=e, oc8=oc8: u_gu(
                    I[f"wg0_{e}"][:, oc8 * D:(oc8 + 1) * D],
                    I[f"wu0_{e}"][:, oc8 * D:(oc8 + 1) * D], oc8, True))
            for oc in range(FC):
                units.append(lambda e=e, oc=oc: u_down(
                    I[f"wd0_{e}"][:, oc * F0:(oc + 1) * F0], oc, e == 0))
        for oc8 in range(8):
            units.append(lambda oc8=oc8: u_gu(
                I["swg"][:, oc8 * D:(oc8 + 1) * D],
                I["swu"][:, oc8 * D:(oc8 + 1) * D], oc8, False))
        for oc in range(FC):
            units.append(lambda oc=oc: u_down(
                I["swd"][:, oc * F0:(oc + 1) * F0], oc, False))

        def u_spill():
            t_acc = p.tok("dve")
            s4st["spill"] = sy(lambda E: E.dma_start(
                out=moev[:, :, a5:a5 + w5],
                in_=xch[:, :FC * w5].rearrange("p (c t) -> p c t", t=w5)),
                waits=[t_acc], inc="sp")
            s4st["spills"].append(s4st["spill"])
        units.append(u_spill)
        return units

    s4_units = []
    for th5 in range(NT5):
        s4_units += mk_group(th5 * 512, min(512, TO - th5 * 512))
    for u in s4_units:
        u()

    for th5 in range(NT5):
        a5 = th5 * 512
        w5 = min(512, TO - a5)
        for fc in range(FC):
            t_xl = sy(lambda E, fc=fc, a5=a5, w5=w5: E.dma_start(
                out=stgf[:, :w5], in_=xfmv[:, fc, a5:a5 + w5]),
                inc="xd2", waits=engtoks())
            t_al = sy(lambda E, fc=fc, a5=a5, w5=w5: E.dma_start(
                out=stage[:, :w5], in_=a0v[:, fc, a5:a5 + w5]),
                waits=[t_a0] + engtoks())
            t_ml = sy(lambda E, fc=fc, a5=a5, w5=w5: E.dma_start(
                out=tmp[1][:, :w5], in_=moev[:, fc, a5:a5 + w5]),
                waits=[s4st["spills"][th5]] + engtoks())
            t_s1 = dve(lambda E, w5=w5: E.tensor_add(
                tmp[0][:, :w5], stgf[:, :w5], stage[:, :w5]),
                waits=[t_xl, t_al])
            t_s2 = dve(lambda E, w5=w5: E.tensor_add(
                stgf[:, 512:512 + w5], tmp[0][:, :w5], tmp[1][:, :w5]),
                waits=[t_s1, t_ml, p.tok("sp")])
            sy(lambda E, fc=fc, a5=a5, w5=w5: E.dma_start(
                out=x1v[:, fc, a5:a5 + w5], in_=stgf[:, 512:512 + w5]),
                waits=[t_s2], inc="sp")
    t_x1 = p.tok("sp")
    dbg_out("x1", x1_d.ap(), [t_x1])

    for th in range(NT5):
        t0 = th * 512
        w5 = min(512, TO - t0)
        ln_chunk(x1v, t0, w5, xn1v, t_x1)
    t_xn1 = p.tok("sp")
    dbg_out("xn1", xn1_own.ap(), [t_xn1])

    t_cc1 = p.op("gpsimd", lambda E: E.collective_compute(
        "AllGather", ALU.bypass, replica_groups=groups,
        ins=[xn1_own.ap().opt()], outs=[xn1_full.ap().opt()]),
        waits=[t_xn1], inc="cc", selfwait=False)

    t_xf = sy(lambda E: E.dma_start(
        out=bufA[:, :2 * FC * TO].rearrange("p (r n) -> p r n", r=2),
        in_=xn1_full.ap().rearrange("r p n -> p r n")),
        waits=[t_cc1] + engtoks())

    def xn1tile(fc, j5):
        r = (j5 * 512) // TO
        tt = (j5 * 512) % TO
        base = r * FC * TO + fc * TO + tt
        return bufA[:, base: base + 512]

    for wi, (wnm, dnm) in enumerate([("wq1", q1_d), ("wk1", k1_d), ("wv1", v1_d)]):
        for hh in range(H1):
            si, t_w = wload(I[wnm][:, hh * D:(hh + 1) * D], D)
            wtv = ws[si][:, :D].rearrange("p (c o) -> p c o", o=128)
            for j5 in range(SEQ // 512):
                bank = pb[j5 % 2]
                for fc in range(FC):
                    mm([t_w, t_xf] + ([bank_rd[j5 % 2]] if fc == 0 else []),
                       bank[:, :512], wtv[:, fc, :], xn1tile(fc, j5),
                       start=(fc == 0), stop=(fc == FC - 1))
                t_mm = p.tok("pe")
                t_ev = dve(lambda E, hh=hh, j5=j5, bank=bank: E.tensor_copy(
                    bufB[:, hh * SEQ + j5 * 512: hh * SEQ + j5 * 512 + 512],
                    bank[:, :512]), waits=[t_mm, p.tok("sp")])
                bank_rd[j5 % 2] = t_ev
            wfree(si, p.tok("pe"))
        t_evs = p.tok("dve")
        for hh in range(H1):
            qh = bufB[:, hh * SEQ:(hh + 1) * SEQ]
            for j5 in range(SEQ // 512):
                sl = qh[:, j5 * 512:(j5 + 1) * 512]
                t_sq = act(lambda E, sl=sl: E.activation(stage[:, :512], sl,
                                                         AF.Square), waits=[t_evs])
                t_n2 = mm([t_sq, t_ones, bank_rd[2]], pb[2][:, :512], onesbf[:, :],
                          stage[:, :512])
                t_mx = dve(lambda E: E.tensor_scalar(tmp[0][:, :512], pb[2][:, :512],
                                                     1e-38, None, ALU.max),
                           waits=[t_n2])
                bank_rd[2] = t_mx
                t_l = act(lambda E: E.activation(tmp[0][:, :512], tmp[0][:, :512],
                                                 AF.Ln), waits=[t_mx])
                t_n = act(lambda E: E.activation(tmp[0][:, :512], tmp[0][:, :512],
                                                 AF.Exp, scale=0.5), waits=[t_l])
                t_pe2 = dve(lambda E: E.tensor_scalar(tmp[0][:, :512],
                                                      tmp[0][:, :512], 1e-8, None,
                                                      ALU.add), waits=[t_n])
                t_l2 = act(lambda E: E.activation(tmp[0][:, :512], tmp[0][:, :512],
                                                  AF.Ln), waits=[t_pe2])
                t_iv = act(lambda E: E.activation(tmp[0][:, :512], tmp[0][:, :512],
                                                  AF.Exp, scale=-1.0), waits=[t_l2])
                if wi == 1:
                    dve(lambda E, sl=sl: E.scalar_tensor_tensor(
                        sl, sl, taunu[:, 0:1], tmp[0][:, :512], ALU.mult, ALU.mult),
                        waits=[t_iv, t_tn])
                else:
                    dve(lambda E, sl=sl: E.tensor_mul(sl, sl, tmp[0][:, :512]),
                        waits=[t_iv])
        t_nrm = p.tok("dve")
        if wi < 2:
            sy(lambda E, dnm=dnm: E.dma_start(
                out=dnm.ap().rearrange("h p t -> p h t"),
                in_=bufB[:, :H1 * SEQ].rearrange("p (h t) -> p h t", h=H1)),
                waits=[t_nrm], inc="sp")
        else:
            for tb2 in range(SB):
                for hh in range(H1):
                    tr([t_nrm, t_id] + ([pT_rd[0]] if hh == 0 else []),
                       pT[:, hh * 128:(hh + 1) * 128],
                       bufB[:, hh * SEQ + tb2 * 128: hh * SEQ + tb2 * 128 + 128],
                       ident[:, :])
                t_tt = p.tok("pe")
                t_pe3 = dve(lambda E: E.tensor_copy(stage[:, :1024], pT[:, :1024]),
                            waits=[t_tt, p.tok("sp")])
                pT_rd[0] = t_pe3
                sy(lambda E, tb2=tb2: E.dma_start(out=v1_d.ap()[tb2],
                                                  in_=stage[:, :1024]),
                   waits=[t_pe3], inc="sp")
    t_qkv1 = p.tok("sp")
    dbg_out("q1", q1_d.ap(), [t_qkv1])
    dbg_out("k1", k1_d.ap(), [t_qkv1])
    dbg_out("v1", v1_d.ap(), [t_qkv1])

    E0pe = p.tok("pe")
    E0act = p.tok("act")
    hslot_rd = [[E0pe], [E0pe]]
    qslot_rd = [[E0pe, t_qkv1], [E0pe, t_qkv1]]
    dgslot_rd = [[E0pe, t_qkv1], [E0pe, t_qkv1]]
    outslot_rd = [[t_qkv1], [t_qkv1]]
    eslot_rd = [[E0pe, t_qkv1], [E0pe, t_qkv1]]
    etslot_rd = [[E0pe], [E0pe]]
    dstg_rd = [[t_qkv1], [t_qkv1]]
    pz_rd = [[E0act], [E0act]]
    hload = [None, None]
    HSLOT = SEQ + SB * 128
    iters8 = [(h, qb) for h in range(H1) for qb in range(SB)]

    def phaseA(i):
        h, qb = iters8[i]
        ip = i % 2
        hp = h % 2
        abase = hp * HSLOT
        if qb == 0:
            t_kl = sy(lambda E, h=h, abase=abase: E.dma_start(
                out=bufA[:, abase:abase + SEQ], in_=k1_d.ap()[h]),
                waits=[t_qkv1] + hslot_rd[hp])
            t_vl = sy(lambda E, h=h, abase=abase: E.dma_start(
                out=bufA[:, abase + SEQ:abase + HSLOT].rearrange(
                    "p (b t) -> p b t", b=SB),
                in_=v1_d.ap()[:, :, h * 128:(h + 1) * 128].rearrange(
                    "b p t -> p b t")),
                waits=[t_qkv1] + hslot_rd[hp], inc="xd2")
            hload[hp] = (t_kl, t_vl)
        t_kl, t_vl = hload[hp]
        qoff = ip * 128
        t_ql = sy(lambda E, h=h, qb=qb, qoff=qoff: E.dma_start(
            out=stage[:, qoff:qoff + 128],
            in_=q1_d.ap()[h][:, qb * 128:(qb + 1) * 128]),
            waits=[t_qkv1] + qslot_rd[ip])
        pcol = ip * 8
        t_z = dve(lambda E, pcol=pcol: E.memset(tmp[3][:, pcol:pcol + 8], 0.0),
                  waits=pz_rd[ip])
        nk = qb + 1
        ebase = ip * 4096
        t_sums = [t_z]
        t_mk_last = None
        for kc in range(cdiv(nk * 128, 512)):
            k0 = kc * 512
            wk = min(512, nk * 128 - k0)
            bi = 2 * ip + kc % 2
            bank = pb[bi]
            t_sc = mm([t_ql, t_kl, bank_rd[bi]], bank[:, :wk],
                      stage[:, qoff:qoff + 128],
                      bufA[:, abase + k0:abase + k0 + wk])
            ndiag = 128 if k0 + wk == nk * 128 else 0
            wfull = wk - ndiag
            if wfull > 0:
                t_ex = act(lambda E, bank=bank, k0=k0, wfull=wfull, kc=kc,
                           ebase=ebase, pcol=pcol:
                           E.activation(bufB[:, ebase + k0:ebase + k0 + wfull],
                                        bank[:, :wfull], AF.Exp,
                                        accum_out=tmp[3][:, pcol + kc:
                                                         pcol + kc + 1]),
                           waits=[t_sc, t_z] + eslot_rd[ip])
                t_sums.append(t_ex)
                bank_rd[bi] = t_ex
            if ndiag:
                soff = ip * 128
                t_ed = act(lambda E, bank=bank, wfull=wfull, soff=soff:
                           E.activation(stgf[:, soff:soff + 128],
                                        bank[:, wfull:wfull + 128], AF.Exp),
                           waits=[t_sc] + dstg_rd[ip])
                bank_rd[bi] = t_ed
                t_mk_last = (t_ed, k0, wfull)
        qslot_rd[ip] = [p.tok("pe")]
        return {"h": h, "qb": qb, "ip": ip, "nk": nk, "ebase": ebase,
                "abase": abase, "pcol": pcol, "t_sums": t_sums, "t_vl": t_vl,
                "t_z": t_z, "diag": t_mk_last}

    def phaseB(C):
        h, qb, ip = C["h"], C["qb"], C["ip"]
        nk, ebase, abase, pcol = C["nk"], C["ebase"], C["abase"], C["pcol"]
        t_sums = C["t_sums"]
        t_ed, k0d, wfd = C["diag"]
        t_mk = dve(lambda E, k0=k0d, wfull=wfd, ebase=ebase, ip=ip:
                   E.tensor_mul(bufB[:, ebase + k0 + wfull:
                                     ebase + k0 + wfull + 128],
                                stgf[:, ip * 128:ip * 128 + 128], cmask[:, :]),
                   waits=[t_ed, t_cm] + eslot_rd[ip])
        dstg_rd[ip] = [t_mk]
        t_ds = dve(lambda E, k0=k0d, wfull=wfd, ebase=ebase, pcol=pcol:
                   E.reduce_sum(tmp[3][:, pcol + 7:pcol + 8],
                                bufB[:, ebase + k0 + wfull:
                                     ebase + k0 + wfull + 128],
                                axis=AX.X), waits=[t_mk, C["t_z"]])
        t_sums = t_sums + [t_ds]
        pz_rd[ip] = list(t_sums)
        rcol = 3 * ip
        t_rs = dve(lambda E, pcol=pcol, rcol=rcol: E.reduce_sum(
            tmp[2][:, rcol:rcol + 1], tmp[3][:, pcol:pcol + 8], axis=AX.X),
            waits=t_sums)
        t_iv = dve(lambda E, rcol=rcol: E.reciprocal(
            tmp[2][:, rcol + 1:rcol + 2], tmp[2][:, rcol:rcol + 1]),
            waits=[t_rs])
        t_sc2 = dve(lambda E, rcol=rcol: E.tensor_scalar(
            tmp[2][:, rcol + 2:rcol + 3], tmp[2][:, rcol + 1:rcol + 2],
            taunu[:, 1:2], None, ALU.mult), waits=[t_iv, t_tn])
        dgoff = 256 + ip * 256
        t_dg = dve(lambda E, rcol=rcol, dgoff=dgoff: E.tensor_scalar(
            stage[:, dgoff:dgoff + 128], ident[:, :],
            tmp[2][:, rcol + 2:rcol + 3], None, ALU.mult),
            waits=[t_sc2, t_id] + dgslot_rd[ip])
        t_pe4s = []
        for kb4 in range(cdiv(nk, 4)):
            nkb = min(4, nk - kb4 * 4)
            for kk in range(nkb):
                kb = kb4 * 4 + kk
                mm([t_dg] + ([bank_rd[5 + kb4 % 2]] if kk == 0 else []),
                   pb[5 + kb4 % 2][:, kk * 128:(kk + 1) * 128],
                   bufB[:, ebase + kb * 128:ebase + (kb + 1) * 128],
                   stage[:, dgoff:dgoff + 128])
            t_et = p.tok("pe")
            t_pe4 = dve(lambda E, kb4=kb4, nkb=nkb, ebase=ebase: E.tensor_copy(
                bufB[:, ebase + 2048 + kb4 * 512:
                     ebase + 2048 + kb4 * 512 + nkb * 128],
                pb[5 + kb4 % 2][:, :nkb * 128]),
                waits=[t_et] + etslot_rd[ip])
            etslot_rd[ip] = []
            bank_rd[5 + kb4 % 2] = t_pe4
            t_pe4s.append(t_pe4)
        t_eTend = p.tok("pe")
        dgslot_rd[ip] = [t_eTend]
        eslot_rd[ip] = [t_eTend]
        pvb = 4 if ip == 0 else 2
        for kb in range(nk):
            mm(t_pe4s + [C["t_vl"]] + ([bank_rd[pvb]] if kb == 0 else []),
               pb[pvb][:, :128],
               bufA[:, abase + SEQ + kb * 128:abase + SEQ + (kb + 1) * 128],
               bufB[:, ebase + 2048 + kb * 128:ebase + 2048 + (kb + 1) * 128],
               start=(kb == 0), stop=(kb == nk - 1))
        t_pv = p.tok("pe")
        etslot_rd[ip] = [t_pv]
        if qb == SB - 1:
            hslot_rd[h % 2] = [t_pv]
        ooff = 384 + ip * 256
        t_oe = dve(lambda E, ooff=ooff, pvb=pvb: E.tensor_copy(
            stage[:, ooff:ooff + 128], pb[pvb][:, :128]),
            waits=[t_pv] + outslot_rd[ip])
        bank_rd[pvb] = t_oe
        t_od = sy(lambda E, h=h, qb=qb, ooff=ooff: E.dma_start(
            out=att_own.ap()[h][:, qb * 128:(qb + 1) * 128],
            in_=stage[:, ooff:ooff + 128]), waits=[t_oe], inc="sp")
        outslot_rd[ip] = [t_od]

    ctx8 = phaseA(0)
    for i8 in range(len(iters8)):
        nxt = phaseA(i8 + 1) if i8 + 1 < len(iters8) else None
        phaseB(ctx8)
        ctx8 = nxt
    t_att = p.tok("sp")
    dbg_out("att", att_own.ap(), [t_att])

    t_cc2 = p.op("gpsimd", lambda E: E.collective_compute(
        "AllGather", ALU.bypass, replica_groups=groups,
        ins=[att_own.ap().opt()], outs=[att_full.ap().opt()]),
        waits=[t_att], inc="cc", selfwait=False)

    t_xt0 = gdma(lambda E: E.dma_start(out=xt_d.ap()[:, :], in_=xn1_own.ap()[:, :]),
                 waits=[t_xn1])
    t_round_done = t_xt0
    t_w6all = [None]
    for rnd in range(2):
        for j5 in range(NT5):
            a5 = j5 * 512
            w5 = min(512, TO - a5)
            t_xl = sy(lambda E, a5=a5, w5=w5: E.dma_start(
                out=bufB[:, :FC * w5].rearrange("p (c t) -> p c t", t=w5),
                in_=xtv[:, :, a5:a5 + w5]), waits=[t_round_done] + engtoks())
            xvv = bufB[:, :FC * w5].rearrange("p (c t) -> p c t", t=w5)
            sg, t_wg = wload(I[f"wgate1_{rnd}"], FC * 8)
            for fc in range(FC):
                mm([t_wg, t_xl] + ([bank_rd[4]] if fc == 0 else []), pb[4][:8, :w5],
                   ws[sg][:, fc * 8:(fc + 1) * 8], xvv[:, fc, :],
                   start=(fc == 0), stop=(fc == FC - 1))
            t_lg = p.tok("pe")
            wfree(sg, t_lg)
            t_le = act(lambda E, w5=w5, rnd=rnd: E.activation(
                stgf[:8, :w5], pb[4][:8, :w5], AF.Identity,
                bias=bg1[:8, rnd:rnd + 1]), waits=[t_lg, t_bg0, t_bg1])
            bank_rd[4] = t_le
            for tb2 in range(w5 // 128):
                tt0 = a5 + tb2 * 128
                t_tr2 = tr([t_le, t_idf, bank_rd[5]], pb[5][:, :8],
                           stgf[:8, tb2 * 128:(tb2 + 1) * 128], identf[:8, :8])
                t_tm = dve(lambda E: E.tensor_copy(stgf[:, 512:520], pb[5][:, :8]),
                           waits=[t_tr2])
                bank_rd[5] = t_tm
                lg = stgf[:, 512:520]
                t_m1 = dve(lambda E: E.tensor_reduce(stgf[:, 520:521], lg, AX.X,
                                                     ALU.min), waits=[t_tm])
                t_e1 = dve(lambda E: E.tensor_scalar(tmp[0][:, :8], lg,
                                                     stgf[:, 520:521], None,
                                                     ALU.is_le), waits=[t_m1])
                t_xu = dve(lambda E: E.scalar_tensor_tensor(
                    tmp[1][:, :8], tmp[0][:, :8], 1e30, lg, ALU.mult, ALU.add),
                    waits=[t_e1])
                t_m2 = dve(lambda E: E.tensor_reduce(stgf[:, 521:522], tmp[1][:, :8],
                                                     AX.X, ALU.min), waits=[t_xu])
                t_top = dve(lambda E: E.tensor_scalar(tmp[0][:, :8], lg,
                                                      stgf[:, 521:522], None,
                                                      ALU.is_gt),
                            waits=[t_m2, t_tm])
                t_mx = dve(lambda E: E.reduce_max(stgf[:, 522:523], lg, axis=AX.X),
                           waits=[t_tm])
                t_mn = dve(lambda E: E.tensor_scalar(stgf[:, 523:524],
                                                     stgf[:, 522:523], -1.0, None,
                                                     ALU.mult), waits=[t_mx])
                t_ee = act(lambda E: E.activation(tmp[1][:, :8], lg, AF.Exp,
                                                  bias=stgf[:, 523:524]),
                           waits=[t_mn])
                t_em = dve(lambda E: E.tensor_mul(tmp[1][:, :8], tmp[1][:, :8],
                                                  tmp[0][:, :8]),
                           waits=[t_ee, t_top])
                t_ss = dve(lambda E: E.reduce_sum(stgf[:, 524:525], tmp[1][:, :8],
                                                  axis=AX.X), waits=[t_em])
                t_rc = dve(lambda E: E.reciprocal(stgf[:, 525:526], stgf[:, 524:525]),
                           waits=[t_ss])
                t_wv = dve(lambda E: E.tensor_scalar(tmp[1][:, :8], tmp[1][:, :8],
                                                     stgf[:, 525:526], None,
                                                     ALU.mult), waits=[t_rc])
                t_tw2 = tr([t_wv, t_idf, bank_rd[6]], pb[6][:8, :128],
                           tmp[1][:, :8], identf[:, :128])
                t_we2 = dve(lambda E: E.tensor_copy(stgf[:8, 640:768],
                                                    pb[6][:8, :128]),
                            waits=[t_tw2, p.tok("sp")])
                bank_rd[6] = t_we2
                sy(lambda E, tt0=tt0: E.dma_start(
                    out=wfm1_d.ap()[:8, tt0:tt0 + 128], in_=stgf[:8, 640:768]),
                    waits=[t_we2], inc="sp")
        t_gate = p.tok("dve")
        t_w1st = p.tok("sp")

        for th5 in range(NT5):
            a5 = th5 * 512
            w5 = min(512, TO - a5)
            t_xl = sy(lambda E, a5=a5, w5=w5: E.dma_start(
                out=bufB[:, :FC * w5].rearrange("p (c t) -> p c t", t=w5),
                in_=xtv[:, :, a5:a5 + w5]),
                waits=[t_round_done, t_gate] + engtoks())
            xvv = bufB[:, :FC * w5].rearrange("p (c t) -> p c t", t=w5)
            t_res = sy(lambda E, a5=a5, w5=w5: E.dma_start(
                out=bufA[:, :FC * w5].rearrange("p (c t) -> p c t", t=w5),
                in_=xn1v[:, :, a5:a5 + w5]), waits=[t_xn1] + engtoks(), inc="xd2")
            t_w6 = sy(lambda E, a5=a5, w5=w5: E.dma_start(
                out=w6[:NE1, :w5], in_=wfm1_d.ap()[:NE1, a5:a5 + w5]),
                waits=[t_w1st, p.tok("pe")], inc="xd2")
            first = [True]
            O_H1b, O_H2b = FC * 512, FC * 512 + 8 * 512
            for n in range(NE1):
                t_rr = sy(lambda E, n=n, a5=a5, w5=w5: E.dma_start(
                    out=lrow[:1, :w5], in_=wfm1_d.ap()[n:n + 1, a5:a5 + w5]),
                    waits=[t_w1st, p.tok("pe"), p.tok("dve")])
                t_bc = mm([t_rr, t_ofr, bank_rd[3]], pb[3][:, :w5], onesfr[:1, :],
                          lrow[:1, :w5])
                t_wbc = dve(lambda E, w5=w5: E.tensor_copy(bufC[:, :w5],
                                                           pb[3][:, :w5]),
                            waits=[t_bc])
                bank_rd[3] = t_wbc
                for oc8 in range(8):
                    bu_ = [0, 4][oc8 % 2]
                    sw, t_w = wload(I[f"wup1_{n}"][:, oc8 * D:(oc8 + 1) * D], D)
                    wv_ = ws[sw][:, :D].rearrange("p (c o) -> p c o", o=128)
                    for fc in range(FC):
                        mm([t_w, t_xl] + ([bank_rd[bu_]] if fc == 0 else []),
                           pb[bu_][:, :w5], wv_[:, fc, :], xvv[:, fc, :],
                           start=(fc == 0), stop=(fc == FC - 1))
                    t_mmu = p.tok("pe")
                    wfree(sw, t_mmu)
                    t_h1 = act(lambda E, oc8=oc8, n=n, w5=w5, bu_=bu_: E.activation(
                        bufB[:, O_H1b + oc8 * 512: O_H1b + oc8 * 512 + w5],
                        pb[bu_][:, :w5], AF.Identity,
                        bias=bup1[:, n * 8 + oc8:n * 8 + oc8 + 1]),
                        waits=[t_mmu, t_bup])
                    bank_rd[bu_] = t_h1
                t_h1a = p.tok("act")
                for oc8 in range(8):
                    bw_ = [1, 5][oc8 % 2]
                    sw, t_w = wload(I[f"wsw1_{n}"][:, oc8 * E1:(oc8 + 1) * E1], E1)
                    wv_ = ws[sw][:, :E1].rearrange("p (c o) -> p c o", o=128)
                    for fcf in range(8):
                        mm([t_w, t_h1a] + ([bank_rd[bw_]] if fcf == 0 else []),
                           pb[bw_][:, :w5], wv_[:, fcf, :],
                           bufB[:, O_H1b + fcf * 512: O_H1b + fcf * 512 + w5],
                           start=(fcf == 0), stop=(fcf == 7))
                    t_mmw = p.tok("pe")
                    wfree(sw, t_mmw)
                    t_sl = act(lambda E, oc8=oc8, n=n, w5=w5, bw_=bw_: E.activation(
                        stage[:, :w5], pb[bw_][:, :w5], AF.Silu,
                        bias=bsw1[:, n * 8 + oc8:n * 8 + oc8 + 1]),
                        waits=[t_mmw, t_bsw, p.tok("dve")])
                    bank_rd[bw_] = t_sl
                    t_h0 = dve(lambda E, oc8=oc8, w5=w5: E.tensor_mul(
                        tmp[0][:, :w5], stage[:, :w5],
                        bufB[:, O_H1b + oc8 * 512: O_H1b + oc8 * 512 + w5]),
                        waits=[t_sl])
                    dve(lambda E, oc8=oc8, w5=w5: E.tensor_mul(
                        bufB[:, O_H2b + oc8 * 512: O_H2b + oc8 * 512 + w5],
                        tmp[0][:, :w5], bufC[:, :w5]), waits=[t_h0, t_wbc])
                t_hx = p.tok("dve")
                for oc in range(FC):
                    bi = 2 + (oc % 2)
                    bank = pb[bi]
                    sd, t_d = wload(I[f"wdn1_{n}"][:, oc * E1:(oc + 1) * E1], E1)
                    wdv = ws[sd][:, :E1].rearrange("p (c o) -> p c o", o=128)
                    for fcf in range(8):
                        mm([t_d, t_hx] + ([bank_rd[bi]] if fcf == 0 else []),
                           bank[:, :w5], wdv[:, fcf, :],
                           bufB[:, O_H2b + fcf * 512: O_H2b + fcf * 512 + w5],
                           start=(fcf == 0), stop=(fcf == 7))
                    t_dn = p.tok("pe")
                    wfree(sd, t_dn)
                    if first[0]:
                        t_ac = dve(lambda E, oc=oc, w5=w5, bank=bank: E.tensor_copy(
                            xch[:, oc * w5:(oc + 1) * w5], bank[:, :w5]),
                            waits=[t_dn])
                    else:
                        t_ac = dve(lambda E, oc=oc, w5=w5, bank=bank: E.tensor_add(
                            xch[:, oc * w5:(oc + 1) * w5],
                            xch[:, oc * w5:(oc + 1) * w5], bank[:, :w5]),
                            waits=[t_dn])
                    bank_rd[bi] = t_ac
                first[0] = False
            for oc in range(FC):
                sw, t_w = wload(I["wsu1"][:, oc * D:(oc + 1) * D], D)
                wv_ = ws[sw][:, :D].rearrange("p (c o) -> p c o", o=128)
                bi = [0, 4][oc % 2]
                for fc in range(FC):
                    mm([t_w, t_xl] + ([bank_rd[bi]] if fc == 0 else []),
                       pb[bi][:, :w5], wv_[:, fc, :], xvv[:, fc, :],
                       start=(fc == 0), stop=(fc == FC - 1))
                t_mmu = p.tok("pe")
                wfree(sw, t_mmu)
                t_su = act(lambda E, oc=oc, w5=w5, bi=bi: E.activation(
                    bufB[:, O_H1b + oc * 512: O_H1b + oc * 512 + w5]
                    if oc < 8 else
                    bufB[:, O_H2b + (oc - 8) * 512: O_H2b + (oc - 8) * 512 + w5],
                    pb[bi][:, :w5], AF.Identity,
                    bias=biases[:, B_BSU + oc:B_BSU + oc + 1]),
                    waits=[t_mmu, t_b])
                bank_rd[bi] = t_su
            t_sua = p.tok("act")
            for s in range(2):
                base = O_H1b if s == 0 else O_H2b
                for oc8 in range(8):
                    bw_ = [1, 5][oc8 % 2]
                    sw, t_w = wload(I["wsa1"][:, oc8 * E1:(oc8 + 1) * E1], E1)
                    wv_ = ws[sw][:, :E1].rearrange("p (c o) -> p c o", o=128)
                    for fcf in range(8):
                        mm([t_w, t_sua] + ([bank_rd[bw_]] if fcf == 0 else []),
                           pb[bw_][:, :w5], wv_[:, fcf, :],
                           bufB[:, base + fcf * 512: base + fcf * 512 + w5],
                           start=(fcf == 0), stop=(fcf == 7))
                    t_mmw = p.tok("pe")
                    wfree(sw, t_mmw)
                    t_sl = act(lambda E, oc8=oc8, w5=w5, bw_=bw_: E.activation(
                        stage[:, :w5], pb[bw_][:, :w5], AF.Silu,
                        bias=biases[:, B_BSA + oc8:B_BSA + oc8 + 1]),
                        waits=[t_mmw, t_b, p.tok("dve")])
                    bank_rd[bw_] = t_sl
                    dve(lambda E, s=s, oc8=oc8, base=base, w5=w5: E.tensor_mul(
                        bufB[:, (s * 8 + oc8) * 512: (s * 8 + oc8) * 512 + w5],
                        stage[:, :w5],
                        bufB[:, base + oc8 * 512: base + oc8 * 512 + w5]),
                        waits=[t_sl])
            t_hsh = p.tok("dve")
            for oc in range(FC):
                bi = 2 + (oc % 2)
                bank = pb[bi]
                sd, t_d = wload(I["wsd1"][:, oc * D:(oc + 1) * D], D)
                wdv = ws[sd][:, :D].rearrange("p (c o) -> p c o", o=128)
                for fcf in range(FC):
                    mm([t_d, t_hsh] + ([bank_rd[bi]] if fcf == 0 else []),
                       bank[:, :w5], wdv[:, fcf, :],
                       bufB[:, fcf * 512: fcf * 512 + w5],
                       start=(fcf == 0), stop=False)
                mm([t_bdn, t_w6], bank[:, :w5], bdn1[:NE1, oc * 128:(oc + 1) * 128],
                   w6[:NE1, :w5], start=False, stop=True)
                t_dn = p.tok("pe")
                wfree(sd, t_dn)
                t_ac = dve(lambda E, oc=oc, w5=w5, bank=bank: E.tensor_add(
                    xch[:, oc * w5:(oc + 1) * w5], xch[:, oc * w5:(oc + 1) * w5],
                    bank[:, :w5]), waits=[t_dn])
                bank_rd[bi] = t_ac
            t_wr = None
            for fc in range(FC):
                t_n1r = dve(lambda E, fc=fc, w5=w5: E.scalar_tensor_tensor(
                    stage[:, 512:512 + w5], xch[:, fc * w5:(fc + 1) * w5],
                    biases[:, B_BSD + fc:B_BSD + fc + 1],
                    bufA[:, fc * w5:(fc + 1) * w5], ALU.add, ALU.add),
                    waits=[t_res, t_b, p.tok("sp")])
                t_wr = sy(lambda E, fc=fc, a5=a5, w5=w5: E.dma_start(
                    out=xtv[:, fc, a5:a5 + w5], in_=stage[:, 512:512 + w5]),
                    waits=[t_n1r], inc="sp")
        t_round_done = p.tok("sp")
    t_ffn = t_round_done

    for tb in range(TB):
        tt0 = tb * 128
        for fc in range(FC):
            gh, hl = fc // 8, fc % 8
            t_x1l = sy(lambda E, fc=fc, tt0=tt0: E.dma_start(
                out=stgf[:, :128], in_=x1v[:, fc, tt0:tt0 + 128]),
                waits=[t_x1] + engtoks())
            t_xtl = sy(lambda E, fc=fc, tt0=tt0: E.dma_start(
                out=stage[:, :128], in_=xtv[:, fc, tt0:tt0 + 128]),
                waits=[t_ffn] + engtoks(), inc="xd2")
            t_aL = sy(lambda E, gh=gh, hl=hl, tt0=tt0: E.dma_start(
                out=stage[:, 128:256],
                in_=att_full.ap()[gh, hl][:, tt0:tt0 + 128]),
                waits=[t_cc2] + engtoks(), inc="od")
            t_aH = sy(lambda E, gh=gh, hl=hl, tt0=tt0: E.dma_start(
                out=stage[:, 256:384],
                in_=att_full.ap()[gh, hl][:, TO + tt0:TO + tt0 + 128]),
                waits=[t_cc2] + engtoks(), inc="yd")
            t_s1 = dve(lambda E: E.tensor_scalar(tmp[0][:, :128], stage[:, 128:256],
                                                 hsel[:, 0:1], None, ALU.mult),
                       waits=[t_aL, t_hsel])
            t_s2 = dve(lambda E: E.scalar_tensor_tensor(
                tmp[0][:, :128], stage[:, 256:384], hsel[:, 1:2], tmp[0][:, :128],
                ALU.mult, ALU.add), waits=[t_aH, t_s1])
            t_s3 = dve(lambda E: E.tensor_add(tmp[1][:, :128], stgf[:, :128],
                                              stage[:, :128]),
                       waits=[t_x1l, t_xtl])
            t_s4 = dve(lambda E: E.tensor_add(stgf[:, 512:640], tmp[0][:, :128],
                                              tmp[1][:, :128]),
                       waits=[t_s2, t_s3, p.tok("pe")])
            bi = 2 + (fc % 2)
            t_tr3 = tr([t_s4, t_idf, bank_rd[bi]], pb[bi][:, :128],
                       stgf[:, 512:640], identf[:, :])
            t_ev3 = dve(lambda E, fc=fc, bi=bi: E.tensor_copy(
                bufC[:, fc * 128:(fc + 1) * 128], pb[bi][:, :128]),
                waits=[t_tr3, p.tok("yd")])
            bank_rd[bi] = t_ev3
        t_row = p.tok("dve")
        sy(lambda E, tt0=tt0: E.dma_start(out=y[tt0:tt0 + 128, :],
                                          in_=bufC[:, :2048]),
           waits=[t_row], inc="yd")
    p.raw("sync", lambda E, h=sems["yd"], v=p.val["yd"]: E.wait_ge(h, v))
    if p.val["od"] > 0:
        p.raw("sync", lambda E, h=sems["od"], v=p.val["od"]: E.wait_ge(h, v))
    p.raw("sync", lambda E, h=sems["sp"], v=p.val["sp"]: E.wait_ge(h, v))

    with nc.Block() as block:
        @block.tensor
        def _(E):
            for fn in p.streams["tensor"]:
                fn(E)

        @block.vector
        def _(E):
            for fn in p.streams["vector"]:
                fn(E)

        @block.scalar
        def _(E):
            for fn in p.streams["scalar"]:
                fn(E)

        @block.gpsimd
        def _(E):
            for fn in p.streams["gpsimd"]:
                fn(E)

        @block.sync
        def _(E):
            for fn in p.streams["sync"]:
                fn(E)

    return nc, stackx, dbg_aps


def kernel(**inputs):
    TO = 1024
    ncores = 8
    nc, stackx, _ = build_program(TO, ncores)
    maps = host_prepare(inputs, TO, ncores)
    res = run_bass_kernel_spmd(nc, maps, core_ids=list(range(ncores)))
    out = np.concatenate([np.asarray(r["y"]).reshape(TO, D) for r in res.results], 0)
    return out.reshape(4, 2048, D)


# version 13
# speedup vs baseline: 1.0559x; 1.0559x over previous
import math
import contextlib
import numpy as np
import ml_dtypes
import concourse.bass as bass
import concourse.mybir as mybir
from concourse.bass_utils import run_bass_kernel_spmd

F32 = mybir.dt.float32
BF16 = mybir.dt.bfloat16
AF = mybir.ActivationFunctionType
ALU = mybir.AluOpType
AX = mybir.AxisListType

D = 2048
FC = 16
HQ, HKV, HD = 16, 8, 128
NE0, F0 = 8, 1024
NE1, E1 = 6, 1024
H1 = 8
TWO_PI = 6.283185307179586
LN10K = math.log(10000.0)
PI32 = float(np.float32(np.pi))
_C1 = float(np.float32(np.ldexp(np.round(np.ldexp(TWO_PI, 9)), -9)))
_C2 = float(np.float32(np.ldexp(np.round(np.ldexp(TWO_PI - _C1, 21)), -21)))
_C3 = float(np.float32(TWO_PI - _C1 - _C2))


def _oc_major(w):
    i, o = w.shape
    return np.ascontiguousarray(
        w.reshape(i // 128, 128, o // 128, 128).transpose(1, 2, 0, 3).reshape(128, -1))


def _fc_major(w):
    i, o = w.shape
    return np.ascontiguousarray(
        w.reshape(i // 128, 128, o).transpose(1, 0, 2).reshape(128, -1))


def _bias_cols(b):
    o = b.shape[0]
    return np.ascontiguousarray(b.reshape(o // 128, 128).T)


def host_prepare(inputs, TO, ncores):
    f32 = lambda k: np.asarray(inputs[k], np.float32)
    x = f32("L0_attn_hidden_states").reshape(-1, D)
    sh = {}
    sh["wq0"] = _oc_major(f32("L0_attn_Wq").T)
    sh["wk0"] = _oc_major(f32("L0_attn_Wk").T)
    sh["wv0"] = _oc_major(f32("L0_attn_Wv").T)
    sh["wo0"] = _oc_major(f32("L0_attn_Wo").T)
    sh["biases"] = np.ascontiguousarray(np.concatenate(
        [_bias_cols(f32("L0_attn_bq")), _bias_cols(f32("L0_attn_bk")),
         _bias_cols(f32("L0_attn_bv")), _bias_cols(f32("L0_attn_bo")),
         _bias_cols(f32("L0_ffn_rms_weight")), _bias_cols(f32("L1_ffn_bsu")),
         _bias_cols(f32("L1_ffn_bsa")), _bias_cols(f32("L1_ffn_bsd"))], 1))
    sh["rt0"] = np.ascontiguousarray(f32("L0_attn_rand_matrix").T)
    sh["rw0"] = _fc_major(f32("L0_ffn_router_w"))
    for e in range(NE0):
        sh[f"wg0_{e}"] = _oc_major(f32("L0_ffn_w_gate")[e])
        sh[f"wu0_{e}"] = _oc_major(f32("L0_ffn_w_up")[e])
        sh[f"wd0_{e}"] = _oc_major(f32("L0_ffn_w_down")[e])
    sh["swg"] = _oc_major(f32("L0_ffn_sw_gate"))
    sh["swu"] = _oc_major(f32("L0_ffn_sw_up"))
    sh["swd"] = _oc_major(f32("L0_ffn_sw_down"))
    sh["taunu"] = np.array([[float(f32("L1_attn_tau").reshape(-1)[0]),
                             float(f32("L1_attn_nu").reshape(-1)[0])]], np.float32)
    for j in range(2):
        sh[f"wgate1_{j}"] = _fc_major(f32("L1_ffn_Wg")[j])
        sh[f"bg1_{j}"] = np.ascontiguousarray(f32("L1_ffn_bg")[j].reshape(8, 1))
    for n in range(NE1):
        sh[f"wup1_{n}"] = _oc_major(f32("L1_ffn_Wup")[n])
        sh[f"wsw1_{n}"] = _oc_major(f32("L1_ffn_Wsw")[n])
        sh[f"wdn1_{n}"] = _oc_major(f32("L1_ffn_Wdn")[n])
    sh["bup1"] = np.ascontiguousarray(
        np.concatenate([_bias_cols(f32("L1_ffn_bup")[n]) for n in range(NE1)], 1))
    sh["bsw1"] = np.ascontiguousarray(
        np.concatenate([_bias_cols(f32("L1_ffn_bsw")[n]) for n in range(NE1)], 1))
    sh["bdn1"] = np.ascontiguousarray(f32("L1_ffn_bdn"))
    sh["wsu1"] = _oc_major(f32("L1_ffn_Wsu"))
    sh["wsa1"] = _oc_major(f32("L1_ffn_Wsa"))
    sh["wsd1"] = _oc_major(f32("L1_ffn_Wsd"))
    wq1, wk1, wv1 = f32("L1_attn_wq").T, f32("L1_attn_wk").T, f32("L1_attn_wv").T
    maps = []
    for c in range(ncores):
        xo = x[c * TO:(c + 1) * TO]
        m = dict(sh)
        m["x_fm"] = np.ascontiguousarray(
            xo.T.reshape(FC, 128, TO).transpose(1, 0, 2).reshape(128, FC * TO))
        pos0 = (c % 2) * TO
        m["posrow"] = np.arange(pos0, pos0 + TO, dtype=np.float32).reshape(1, TO)
        hf = c % 2
        m["halfsel"] = np.array([[1.0 - hf, float(hf)]], np.float32)
        hs = slice(hf * H1 * HD, (hf + 1) * H1 * HD)
        m["wq1"] = _oc_major(wq1[:, hs])
        m["wk1"] = _oc_major(wk1[:, hs])
        m["wv1"] = _oc_major(wv1[:, hs])
        maps.append(m)
    return maps


ENGS = ("tensor", "vector", "scalar", "gpsimd", "sync")


class P:
    def __init__(self, nc, sems):
        self.nc = nc
        self.sems = sems
        self.streams = {e: [] for e in ENGS}
        self.val = {n: 0 for n in sems}
        self.wm = {e: {n: 0 for n in sems} for e in ENGS}
        self.own = {"tensor": "pe", "vector": "dve", "scalar": "act",
                    "gpsimd": "gpc", "sync": None}

    def raw(self, eng, fn):
        self.streams[eng].append(fn)

    def wait(self, eng, tok):
        if tok is None:
            return
        sem, v = tok
        if v <= self.wm[eng][sem]:
            return
        self.wm[eng][sem] = v
        self.raw(eng, lambda E, h=self.sems[sem], v=v: E.wait_ge(h, v))

    def op(self, eng, fn, waits=(), inc=None, n=1, selfwait=True, ring=False):
        own = self.own[eng]
        if selfwait and own is not None and self.val[own] > 0:
            self.wait(eng, (own, self.val[own]))
        if inc is None:
            inc = own
        if ring and inc is not False and self.val[inc] > 0:
            self.wait(eng, (inc, self.val[inc]))
        for t in waits:
            self.wait(eng, t)
        if inc is False:
            self.raw(eng, fn)
            return None
        self.val[inc] += n
        self.raw(eng, lambda E, fn=fn, h=self.sems[inc], n=n: fn(E).then_inc(h, n))
        return (inc, self.val[inc])

    def tok(self, sem):
        return (sem, self.val[sem])


def cdiv(a, b):
    return (a + b - 1) // b


def build_program(TO, ncores, dbg=()):
    SEQ = 2 * TO
    TB = TO // 128
    SB = SEQ // 128
    NT5 = cdiv(TO, 512)
    TCH = min(256, TO)
    NCH = TO // TCH
    nc = bass.Bass("TRN2", target_bir_lowering=False, debug=False)

    def din(name, shape, dtype=F32):
        return nc.dram_tensor(name, list(shape), dtype, kind="ExternalInput").ap()

    I = {"x_fm": din("x_fm", (128, FC * TO)), "posrow": din("posrow", (1, TO)),
         "biases": din("biases", (128, 104)), "halfsel": din("halfsel", (1, 2))}
    B_BQ, B_BK, B_BV, B_BO, B_RMS, B_BSU, B_BSA, B_BSD = 0, 16, 24, 32, 48, 64, 80, 88
    for nm, shp in [("wq0", (128, 16 * D)), ("wk0", (128, 8 * D)), ("wv0", (128, 8 * D)),
                    ("wo0", (128, 16 * D)), ("rt0", (128, 64)), ("rw0", (128, FC * 8)),
                    ("swg", (128, 8 * D)), ("swu", (128, 8 * D)), ("swd", (128, 16 * F0)),
                    ("taunu", (1, 2)), ("wq1", (128, 8 * D)), ("wk1", (128, 8 * D)),
                    ("wv1", (128, 8 * D)), ("wsu1", (128, 16 * D)),
                    ("wsa1", (128, 8 * E1)), ("wsd1", (128, 16 * D)),
                    ("bup1", (128, NE1 * 8)), ("bsw1", (128, NE1 * 8)),
                    ("bdn1", (NE1, D))]:
        I[nm] = din(nm, shp)
    for e in range(NE0):
        I[f"wg0_{e}"] = din(f"wg0_{e}", (128, 8 * D))
        I[f"wu0_{e}"] = din(f"wu0_{e}", (128, 8 * D))
        I[f"wd0_{e}"] = din(f"wd0_{e}", (128, 16 * F0))
    for j in range(2):
        I[f"wgate1_{j}"] = din(f"wgate1_{j}", (128, FC * 8))
        I[f"bg1_{j}"] = din(f"bg1_{j}", (8, 1))
    for n in range(NE1):
        I[f"wup1_{n}"] = din(f"wup1_{n}", (128, 8 * D))
        I[f"wsw1_{n}"] = din(f"wsw1_{n}", (128, 8 * E1))
        I[f"wdn1_{n}"] = din(f"wdn1_{n}", (128, 16 * E1))

    y = nc.dram_tensor("y", [TO, D], F32, kind="ExternalOutput").ap()

    def dt_(name, shape, dtype):
        return nc.dram_tensor(name, list(shape), dtype)

    xn0_d = dt_("xn0_d", (128, FC * TO), BF16)
    hbf_d = dt_("hbf_d", (128, FC * TO), BF16)
    a0_d = dt_("a0_d", (128, FC * TO), BF16)
    x1_d = dt_("x1_d", (128, FC * TO), F32)
    xt_d = dt_("xt_d", (128, FC * TO), BF16)
    moe_d = dt_("moe_d", (128, FC * TO), F32)
    wfm_d = dt_("wfm_d", (8, TO), F32)
    wfm1_d = dt_("wfm1_d", (8, TO), F32)
    xn1_own = dt_("xn1_own", (128, FC * TO), BF16)
    xn1_full = dt_("xn1_full", (2, 128, FC * TO), BF16)
    q1_d = dt_("q1_d", (H1, 128, SEQ), BF16)
    k1_d = dt_("k1_d", (H1, 128, SEQ), BF16)
    v1_d = dt_("v1_d", (SB, 128, H1 * HD), BF16)
    mixo_d = dt_("mixo_d", (128, HQ * TO), BF16)
    mixo_full = dt_("mixo_full", (2, 128, HQ * TO), BF16)
    att_own = dt_("att_own", (H1, 128, SEQ), BF16)
    att_full = dt_("att_full", (2, H1, 128, SEQ), BF16)

    SBT = {}
    def sbuf(name, shape, dtype=F32):
        SBT[name] = nc.alloc_sbuf_tensor(name, list(shape), dtype)
        return SBT[name]

    costab = sbuf("costab", (128, TO))
    sintab = sbuf("sintab", (128, TO))
    ident = sbuf("ident", (128, 128), BF16)
    identf = sbuf("identf", (128, 128), F32)
    onesbf = sbuf("onesbf", (128, 128), BF16)
    onesdiv = sbuf("onesdiv", (128, 128), BF16)
    onesfr = sbuf("onesfr", (1, 128))
    prot = sbuf("prot", (128, 128), BF16)
    rt0 = sbuf("s_rt0", (128, 64), BF16)
    rw0 = sbuf("s_rw0", (128, FC * 8), BF16)
    biases = sbuf("s_biases", (128, 104))
    bup1 = sbuf("s_bup1", (128, NE1 * 8))
    bsw1 = sbuf("s_bsw1", (128, NE1 * 8))
    bdn1 = sbuf("s_bdn1", (8, D))
    w6 = sbuf("w6", (8, 512))
    bg1 = sbuf("s_bg1", (8, 2))
    cst = sbuf("cst", (128, 8))
    taunu = sbuf("s_taunu", (128, 2))
    hsel = sbuf("hsel", (128, 2))
    lrow2 = sbuf("lrow2", (1, 512))
    lrow = sbuf("lrow", (1, 512))
    NWS = 5
    ws = [sbuf(f"ws{i}", (128, 2048), BF16) for i in range(NWS)]
    bufA = sbuf("bufA", (128, 32768), BF16)
    bufB = sbuf("bufB", (128, 16384), BF16)
    bufC = sbuf("bufC", (128, 2048))
    xch = sbuf("xch", (128, FC * 512))
    stage = sbuf("stage", (128, 1024), BF16)
    tmp = [sbuf(f"tmp{i}", (128, 512)) for i in range(4)]
    tmpi = sbuf("tmpi", (128, 512), mybir.dt.int32)
    cmask = sbuf("cmask", (128, 128), BF16)
    stgf = sbuf("stgf", (128, 1024))
    vhp = sbuf("vhp", (8, 32 * 128), BF16)

    pb = [nc.alloc_psum_tensor(f"pb{i}", [128, 512], F32) for i in range(7)]
    pT = nc.alloc_psum_tensor("pT", [128, 1024], BF16)

    stackx = contextlib.ExitStack()
    NWSEM = NWS
    semnames = (["pe", "dve", "act", "gpc", "cc", "xd", "xd2", "sp", "od", "yd", "gd"] +
                [f"wd{i}" for i in range(NWSEM)])
    sems = {n: stackx.enter_context(nc.semaphore(n)) for n in semnames}
    p = P(nc, sems)

    def mm(waits, out_ap, lhsT_ap, rhs_ap, start=True, stop=True, tp=None):
        return p.op("tensor",
                    lambda E: E.matmul(out_ap, lhsT_ap, rhs_ap, start=start, stop=stop,
                                       tile_position=tp),
                    waits=waits, inc=("pe" if stop else False), selfwait=False)

    def tr(waits, out_ap, in_ap, idap):
        return p.op("tensor", lambda E: E.transpose(out_ap, in_ap, idap),
                    waits=waits, inc="pe", selfwait=False)

    def dve(fn, waits=()):
        return p.op("vector", fn, waits=waits)

    def act(fn, waits=()):
        return p.op("scalar", fn, waits=waits)

    def gp(fn, waits=()):
        return p.op("gpsimd", fn, waits=waits)

    def sy(fn, waits=(), inc="xd"):
        return p.op("sync", fn, waits=waits, inc=inc, n=16, selfwait=False, ring=True)

    def gdma(fn, waits=(), inc="gd"):
        return p.op("gpsimd", fn, waits=waits, inc=inc, n=16, selfwait=False, ring=True)

    wst = {"free": [None] * NWS, "seq": 0}

    def wload(src_ap, cols):
        i = wst["seq"] % NWS
        wst["seq"] += 1
        t = p.op("gpsimd",
                 lambda E, i=i, cols=cols, src_ap=src_ap:
                 E.dma_start(out=ws[i][:, :cols], in_=src_ap),
                 waits=([wst["free"][i]] if wst["free"][i] else []),
                 inc=f"wd{i}", n=16, selfwait=False, ring=True)
        return i, t

    def wfree(i, tok):
        wst["free"][i] = tok

    bank_rd = [None] * 7
    pT_rd = [None]

    def engtoks():
        return [p.tok("pe"), p.tok("dve"), p.tok("act")]

    dbg_aps = {}
    def dbg_out(name, src_ap, waits):
        if name in dbg:
            ap = nc.dram_tensor("dbg_" + name, list(src_ap.shape), src_ap.dtype,
                                kind="ExternalOutput").ap()
            dbg_aps[name] = ap
            sy(lambda E: E.dma_start(out=ap, in_=src_ap), waits=waits, inc="od")

    t_b = sy(lambda E: E.dma_start(out=biases[:, :], in_=I["biases"]))
    t_tn0 = sy(lambda E: E.dma_start(out=taunu[:1, :2], in_=I["taunu"]))
    t_hs0 = sy(lambda E: E.dma_start(out=hsel[:1, :2], in_=I["halfsel"]))
    t_bup = sy(lambda E: E.dma_start(out=bup1[:, :], in_=I["bup1"]))
    t_bsw = sy(lambda E: E.dma_start(out=bsw1[:, :], in_=I["bsw1"]))
    t_bg0 = sy(lambda E: E.dma_start(out=bg1[:8, 0:1], in_=I["bg1_0"]))
    t_bg1 = sy(lambda E: E.dma_start(out=bg1[:8, 1:2], in_=I["bg1_1"]))
    t_bdn = sy(lambda E: E.dma_start(out=bdn1[:NE1, :], in_=I["bdn1"]), inc="xd2")
    t_rt = gdma(lambda E: E.dma_start(out=rt0[:, :], in_=I["rt0"]))
    t_rw = gdma(lambda E: E.dma_start(out=rw0[:, :], in_=I["rw0"]))
    t_ones = dve(lambda E: E.memset(onesbf[:, :], 1.0))
    t_odiv = dve(lambda E: E.memset(onesdiv[:, :], 1.0 / 2048.0))
    t_ofr = dve(lambda E: E.memset(onesfr[:, :], 1.0))
    t_c = dve(lambda E: E.memset(cst[:, :], 0.0))
    t_c0 = dve(lambda E: E.memset(cst[:, 0:1], 1e-5), waits=[t_c])
    t_c2 = dve(lambda E: E.memset(cst[:, 2:3], 1e-6), waits=[t_c])
    t_of = dve(lambda E: E.memset(stgf[:, :128], 1.0))
    t_n1 = dve(lambda E: E.memset(stgf[:, 128:256], -1.0))
    t_id = gp(lambda E: E.affine_select(ident[:, :], onesbf[:, :], pattern=[[1, 128]],
                                        compare_op=ALU.is_equal, fill=0.0, base=0,
                                        channel_multiplier=-1), waits=[t_ones])
    t_idf = gp(lambda E: E.affine_select(identf[:, :], stgf[:, :128], pattern=[[1, 128]],
                                         compare_op=ALU.is_equal, fill=0.0, base=0,
                                         channel_multiplier=-1), waits=[t_of])
    t_cmi = gp(lambda E: E.iota(stgf[:, 256:384], pattern=[[1, 128]], base=0,
                                channel_multiplier=-1,
                                allow_small_or_imprecise_dtypes=True))
    t_cm = dve(lambda E: E.tensor_scalar(cmask[:, :], stgf[:, 256:384], 0.5, None,
                                         ALU.is_lt), waits=[t_cmi])
    t_pa = gp(lambda E: E.affine_select(prot[:, :], onesbf[:, :], pattern=[[1, 128]],
                                        compare_op=ALU.is_equal, fill=0.0, base=-64,
                                        channel_multiplier=-1), waits=[t_ones])
    t_pb = gp(lambda E: E.affine_select(stage[:, :128], stgf[:, 128:256],
                                        pattern=[[1, 128]], compare_op=ALU.is_equal,
                                        fill=0.0, base=64, channel_multiplier=-1),
              waits=[t_n1])
    t_prot = dve(lambda E: E.tensor_add(prot[:, :], prot[:, :], stage[:, :128]),
                 waits=[t_pa, t_pb])
    t_tnm = mm([t_tn0, t_ofr, bank_rd[6]], pb[6][:, :2], onesfr[:1, :],
               taunu[:1, :2])
    t_tn = dve(lambda E: E.tensor_copy(taunu[:, :2], pb[6][:, :2]), waits=[t_tnm])
    t_hsm = mm([t_hs0, t_ofr, t_tn], pb[6][:, 2:4], onesfr[:1, :], hsel[:1, :2])
    t_hsel = dve(lambda E: E.tensor_copy(hsel[:, :2], pb[6][:, 2:4]), waits=[t_hsm])
    bank_rd[6] = t_hsel

    xn0v = xn0_d.ap().rearrange("p (c t) -> p c t", t=TO)
    hbfv = hbf_d.ap().rearrange("p (c t) -> p c t", t=TO)
    a0v = a0_d.ap().rearrange("p (c t) -> p c t", t=TO)
    x1v = x1_d.ap().rearrange("p (c t) -> p c t", t=TO)
    xfmv = I["x_fm"].rearrange("p (c t) -> p c t", t=TO)
    xn1v = xn1_own.ap().rearrange("p (c t) -> p c t", t=TO)
    xtv = xt_d.ap().rearrange("p (c t) -> p c t", t=TO)

    def ln_chunk(src_view, t0, w5, out_view, wait_src):
        t_xc = sy(lambda E: E.dma_start(
            out=xch[:, :FC * w5].rearrange("p (c t) -> p c t", t=w5),
            in_=src_view[:, :, t0:t0 + w5]),
            waits=([wait_src] if wait_src else []) + engtoks())
        xv = xch[:, :FC * w5].rearrange("p (c t) -> p c t", t=w5)
        cps, sqs = [], []
        t_sp0 = p.tok("sp")
        for fc in range(FC):
            cps.append(dve(lambda E, fc=fc: E.tensor_copy(
                bufA[:, fc * w5:(fc + 1) * w5], xv[:, fc, :]),
                waits=[t_xc, t_sp0]))
            sqs.append(act(lambda E, fc=fc: E.activation(
                bufB[:, fc * w5:(fc + 1) * w5], xv[:, fc, :], AF.Square),
                waits=[t_xc, t_sp0]))
        for fc in range(FC):
            mm([cps[fc], t_odiv] + ([bank_rd[0]] if fc == 0 else []),
               pb[0][:, :w5], onesdiv[:, :], bufA[:, fc * w5:(fc + 1) * w5],
               start=(fc == 0), stop=(fc == FC - 1))
            mm([sqs[fc]] + ([bank_rd[1]] if fc == 0 else []),
               pb[1][:, :w5], onesdiv[:, :], bufB[:, fc * w5:(fc + 1) * w5],
               start=(fc == 0), stop=(fc == FC - 1))
        t_mm = p.tok("pe")
        t_m2 = act(lambda E: E.activation(stgf[:, :w5], pb[0][:, :w5], AF.Square),
                   waits=[t_mm])
        t_var = dve(lambda E: E.tensor_sub(tmp[0][:, :w5], pb[1][:, :w5],
                                           stgf[:, :w5]), waits=[t_mm, t_m2])
        bank_rd[1] = t_var
        t_ln = act(lambda E: E.activation(tmp[1][:, :w5], tmp[0][:, :w5], AF.Ln,
                                          bias=cst[:, 0:1]), waits=[t_var, t_c0])
        t_rstd = act(lambda E: E.activation(tmp[1][:, :w5], tmp[1][:, :w5], AF.Exp,
                                            scale=-0.5), waits=[t_ln])
        t_xnl = None
        for fc in range(FC):
            t_d = dve(lambda E, fc=fc: E.tensor_sub(tmp[2][:, :w5], xv[:, fc, :],
                                                    pb[0][:, :w5]),
                      waits=[t_mm, t_rstd, t_xc])
            t_xnl = dve(lambda E, fc=fc: E.tensor_mul(
                bufA[:, fc * w5:(fc + 1) * w5], tmp[2][:, :w5], tmp[1][:, :w5]),
                waits=[t_d])
        bank_rd[0] = t_xnl
        t_st = sy(lambda E: E.dma_start(
            out=out_view[:, :, t0:t0 + w5],
            in_=bufA[:, :FC * w5].rearrange("p (c t) -> p c t", t=w5)),
            waits=[t_xnl], inc="sp")
        return t_xnl, t_st

    for th in range(NT5):
        t0 = th * 512
        w5 = min(512, TO - t0)
        t_xnl, _ = ln_chunk(xfmv, t0, w5, xn0v, None)
        sqs = []
        t_sp1 = p.tok("sp")
        for fc in range(FC):
            sqs.append(act(lambda E, fc=fc: E.activation(
                bufB[:, fc * w5:(fc + 1) * w5], bufA[:, fc * w5:(fc + 1) * w5],
                AF.Square), waits=[t_xnl, t_sp1]))
        for fc in range(FC):
            mm([sqs[fc]] + ([bank_rd[2]] if fc == 0 else []),
               pb[2][:, :w5], onesdiv[:, :], bufB[:, fc * w5:(fc + 1) * w5],
               start=(fc == 0), stop=(fc == FC - 1))
        t_mm2 = p.tok("pe")
        t_l2 = act(lambda E: E.activation(tmp[0][:, :w5], pb[2][:, :w5], AF.Ln,
                                          bias=cst[:, 2:3]), waits=[t_mm2, t_c2])
        t_r2 = act(lambda E: E.activation(tmp[0][:, :w5], tmp[0][:, :w5], AF.Exp,
                                          scale=-0.5), waits=[t_l2])
        bank_rd[2] = t_r2
        t_hl = None
        for fc in range(FC):
            t_hl = dve(lambda E, fc=fc: E.scalar_tensor_tensor(
                bufB[:, fc * w5:(fc + 1) * w5], bufA[:, fc * w5:(fc + 1) * w5],
                biases[:, B_RMS + fc:B_RMS + fc + 1], tmp[0][:, :w5],
                ALU.mult, ALU.mult), waits=[t_r2, t_b])
        sy(lambda E, t0=t0, w5=w5: E.dma_start(
            out=hbfv[:, :, t0:t0 + w5],
            in_=bufB[:, :FC * w5].rearrange("p (c t) -> p c t", t=w5)),
            waits=[t_hl], inc="sp")
    t_ln_all = p.tok("sp")
    dbg_out("xn0", xn0_d.ap(), [t_ln_all])
    dbg_out("hbf", hbf_d.ap(), [t_ln_all])

    t_ior = gp(lambda E: E.iota(
        lrow[:1, :128].rearrange("o (a b) -> o a b", b=64), pattern=[[0, 2], [1, 64]],
        base=0, channel_multiplier=0, allow_small_or_imprecise_dtypes=True))
    t_if = act(lambda E: E.activation(lrow[:1, 128:256], lrow[:1, :128], AF.Exp,
                                      scale=-LN10K / 64.0), waits=[t_ior])
    for j5 in range(NT5):
        a = j5 * 512
        w5 = min(512, TO - a)
        t_posc = sy(lambda E, a=a, w5=w5: E.dma_start(out=lrow2[:1, :w5],
                                                      in_=I["posrow"][:, a:a + w5]),
                    waits=[p.tok("pe")])
        t_am = mm([t_if, t_posc, bank_rd[3]], pb[3][:, :w5], lrow[:1, 128:256],
                  lrow2[:1, :w5])
        t_y = dve(lambda E, w5=w5: E.tensor_scalar(tmp[0][:, :w5], pb[3][:, :w5],
                                                   1.0 / TWO_PI, 24.5, ALU.mult,
                                                   ALU.add), waits=[t_am])
        t_ki = dve(lambda E, w5=w5: E.tensor_copy(tmpi[:, :w5], tmp[0][:, :w5]),
                   waits=[t_y])
        t_kf = dve(lambda E, w5=w5: E.tensor_copy(tmp[2][:, :w5], tmpi[:, :w5]),
                   waits=[t_ki])
        t_d = dve(lambda E, w5=w5: E.tensor_sub(tmp[0][:, :w5], tmp[0][:, :w5],
                                                tmp[2][:, :w5]), waits=[t_kf])
        t_g = dve(lambda E, w5=w5: E.tensor_scalar(tmp[0][:, :w5], tmp[0][:, :w5],
                                                   0.0, None, ALU.is_ge),
                  waits=[t_d])
        t_kk = dve(lambda E, w5=w5: E.scalar_tensor_tensor(
            tmp[0][:, :w5], tmp[0][:, :w5], -25.0, tmp[2][:, :w5], ALU.add, ALU.add),
            waits=[t_g])
        t_r1 = dve(lambda E, w5=w5: E.scalar_tensor_tensor(
            tmp[1][:, :w5], tmp[0][:, :w5], -_C1, pb[3][:, :w5], ALU.mult, ALU.add),
            waits=[t_kk, t_am])
        bank_rd[3] = t_r1
        t_r2 = dve(lambda E, w5=w5: E.scalar_tensor_tensor(
            tmp[1][:, :w5], tmp[0][:, :w5], -_C2, tmp[1][:, :w5], ALU.mult, ALU.add),
            waits=[t_r1])
        t_red2 = dve(lambda E, w5=w5: E.scalar_tensor_tensor(
            tmp[1][:, :w5], tmp[0][:, :w5], -_C3, tmp[1][:, :w5], ALU.mult, ALU.add),
            waits=[t_r2])
        act(lambda E, a=a, w5=w5: E.activation(sintab[:, a:a + w5], tmp[1][:, :w5],
                                               AF.Sin), waits=[t_red2])
        t_s2 = act(lambda E, w5=w5: E.activation(tmp[2][:, :w5], tmp[1][:, :w5],
                                                 AF.Sin, scale=0.5), waits=[t_red2])
        t_sq = act(lambda E, w5=w5: E.activation(tmp[2][:, :w5], tmp[2][:, :w5],
                                                 AF.Square), waits=[t_s2])
        dve(lambda E, a=a, w5=w5: E.tensor_scalar(costab[:, a:a + w5], tmp[2][:, :w5],
                                                  -2.0, 1.0, ALU.mult, ALU.add),
            waits=[t_sq])
    t_tabs = [p.tok("dve"), p.tok("act")]
    dbg_out("sintab", sintab[:, :], t_tabs)
    dbg_out("costab", costab[:, :], t_tabs)

    for tb in range(TB):
        tt0 = tb * 128
        t_h = sy(lambda E, tt0=tt0: E.dma_start(
            out=bufB[:, :FC * 128].rearrange("p (c t) -> p c t", t=128),
            in_=hbfv[:, :, tt0:tt0 + 128]), waits=[t_ln_all] + engtoks())
        hv = bufB[:, :FC * 128].rearrange("p (c t) -> p c t", t=128)
        for fc in range(FC):
            mm([t_h, t_rw] + ([bank_rd[4]] if fc == 0 else []), pb[4][:8, :128],
               rw0[:, fc * 8:(fc + 1) * 8], hv[:, fc, :],
               start=(fc == 0), stop=(fc == FC - 1))
        t_lg = p.tok("pe")
        t_lb = dve(lambda E: E.tensor_copy(stgf[:8, :128], pb[4][:8, :128]),
                   waits=[t_lg])
        bank_rd[4] = t_lb
        t_tr = tr([t_lb, t_idf, bank_rd[5]], pb[5][:, :8], stgf[:8, :128],
                  identf[:8, :8])
        t_tm = dve(lambda E: E.tensor_copy(stgf[:, 512:520], pb[5][:, :8]),
                   waits=[t_tr])
        bank_rd[5] = t_tm
        lg = stgf[:, 512:520]
        t_v1 = dve(lambda E: E.reduce_max(stgf[:, 520:521], lg, axis=AX.X),
                   waits=[t_tm])
        t_e1 = dve(lambda E: E.tensor_scalar(tmp[0][:, :8], lg, stgf[:, 520:521],
                                             None, ALU.is_ge), waits=[t_v1])
        t_x2 = dve(lambda E: E.scalar_tensor_tensor(
            tmp[1][:, :8], tmp[0][:, :8], -1e30, lg, ALU.mult, ALU.add),
            waits=[t_e1])
        t_v2 = dve(lambda E: E.reduce_max(stgf[:, 521:522], tmp[1][:, :8], axis=AX.X),
                   waits=[t_x2])
        t_e2 = dve(lambda E: E.tensor_scalar(tmp[1][:, :8], tmp[1][:, :8],
                                             stgf[:, 521:522], None, ALU.is_ge),
                   waits=[t_v2])
        t_dl = dve(lambda E: E.tensor_sub(stgf[:, 522:523], stgf[:, 520:521],
                                          stgf[:, 521:522]), waits=[t_v2])
        t_th = act(lambda E: E.activation(stgf[:, 523:524], stgf[:, 522:523],
                                          AF.Tanh, scale=0.5), waits=[t_dl])
        t_w1 = dve(lambda E: E.tensor_scalar(stgf[:, 523:524], stgf[:, 523:524],
                                             0.5, 0.5, ALU.mult, ALU.add),
                   waits=[t_th])
        t_w2 = dve(lambda E: E.tensor_scalar(stgf[:, 524:525], stgf[:, 523:524],
                                             -1.0, 1.0, ALU.mult, ALU.add),
                   waits=[t_w1])
        t_a1 = dve(lambda E: E.tensor_scalar(tmp[0][:, :8], tmp[0][:, :8],
                                             stgf[:, 523:524], None, ALU.mult),
                   waits=[t_w1, t_e1])
        t_wall = dve(lambda E: E.scalar_tensor_tensor(
            stgf[:, 528:536], tmp[1][:, :8], stgf[:, 524:525], tmp[0][:, :8],
            ALU.mult, ALU.add), waits=[t_w2, t_a1, t_e2])
        t_tw = tr([t_wall, t_idf, bank_rd[6]], pb[6][:8, :128], stgf[:, 528:536],
                  identf[:, :128])
        t_we = dve(lambda E: E.tensor_copy(stgf[:8, 640:768], pb[6][:8, :128]),
                   waits=[t_tw, p.tok("sp")])
        bank_rd[6] = t_we
        sy(lambda E, tt0=tt0: E.dma_start(out=wfm_d.ap()[:8, tt0:tt0 + 128],
                                          in_=stgf[:8, 640:768]),
           waits=[t_we], inc="sp")
    t_wfst = p.tok("sp")
    dbg_out("wfm", wfm_d.ap(), [t_wfst])


    A_HB, A_HE, A_WBC = 16384, 24576, 28672
    moev = moe_d.ap().rearrange("p (c t) -> p c t", t=TO)
    s4st = {"spill": None, "guctr": 0, "spills": []}

    def mk_group(a5, w5):
        hvv = bufA[:, A_HB:A_HB + FC * w5].rearrange("p (c t) -> p c t", t=w5)
        st = {"t_hb": None, "t_wbc": None, "t_h": None}
        units = []

        def u_load():
            st["t_hb"] = sy(lambda E: E.dma_start(
                out=bufA[:, A_HB:A_HB + FC * w5].rearrange("p (c t) -> p c t",
                                                           t=w5),
                in_=hbfv[:, :, a5:a5 + w5]),
                waits=[t_ln_all, p.tok("pe")], inc="xd2")
        units.append(u_load)

        def u_wbc(e):
            t_rr = sy(lambda E, e=e: E.dma_start(
                out=lrow[:1, :w5], in_=wfm_d.ap()[e:e + 1, a5:a5 + w5]),
                waits=[t_wfst, p.tok("pe")], inc="xd2")
            t_bc = mm([t_rr, t_ofr, bank_rd[5]], pb[5][:, :w5], onesfr[:1, :],
                      lrow[:1, :w5])
            st["t_wbc"] = dve(lambda E: E.tensor_copy(
                bufA[:, A_WBC:A_WBC + w5], pb[5][:, :w5]), waits=[t_bc])
            bank_rd[5] = st["t_wbc"]

        def u_gu(wg_ap, wu_ap, oc8, scaled):
            bg_ = [3, 4][s4st["guctr"] % 2]
            bu_ = [5, 6][s4st["guctr"] % 2]
            s4st["guctr"] += 1
            sg, t_wg = wload(wg_ap, D)
            su, t_wu = wload(wu_ap, D)
            wgv = ws[sg][:, :D].rearrange("p (c o) -> p c o", o=128)
            wuv = ws[su][:, :D].rearrange("p (c o) -> p c o", o=128)
            for fc in range(FC):
                mm([t_wg, st["t_hb"]] + ([bank_rd[bg_]] if fc == 0 else []),
                   pb[bg_][:, :w5], wgv[:, fc, :], hvv[:, fc, :],
                   start=(fc == 0), stop=(fc == FC - 1))
                mm([t_wu] + ([bank_rd[bu_]] if fc == 0 else []),
                   pb[bu_][:, :w5], wuv[:, fc, :], hvv[:, fc, :],
                   start=(fc == 0), stop=(fc == FC - 1))
            t_gu = p.tok("pe")
            wfree(sg, t_gu)
            wfree(su, t_gu)
            t_s = act(lambda E, bg_=bg_: E.activation(bufC[:, :w5],
                                                      pb[bg_][:, :w5], AF.Silu),
                      waits=[t_gu, p.tok("dve")])
            bank_rd[bg_] = t_s
            t_u = act(lambda E, bu_=bu_: E.activation(bufC[:, 512:512 + w5],
                                                      pb[bu_][:, :w5],
                                                      AF.Identity),
                      waits=[t_gu])
            bank_rd[bu_] = t_u
            dst = bufA[:, A_HE + oc8 * 512: A_HE + oc8 * 512 + w5]
            if not scaled:
                st["t_h"] = dve(lambda E, dst=dst: E.tensor_mul(
                    dst, bufC[:, :w5], bufC[:, 512:512 + w5]), waits=[t_s, t_u])
            else:
                t_h0 = dve(lambda E: E.tensor_mul(
                    bufC[:, 1024:1024 + w5], bufC[:, :w5], bufC[:, 512:512 + w5]),
                    waits=[t_s, t_u])
                st["t_h"] = dve(lambda E, dst=dst: E.tensor_mul(
                    dst, bufC[:, 1024:1024 + w5], bufA[:, A_WBC:A_WBC + w5]),
                    waits=[t_h0, st["t_wbc"]])

        def u_down(w_ap, oc, fresh):
            bi = [3, 4][oc % 2]
            sd, t_d = wload(w_ap, F0)
            wdv = ws[sd][:, :F0].rearrange("p (c o) -> p c o", o=128)
            for fcf in range(8):
                mm([t_d, st["t_h"]] + ([bank_rd[bi]] if fcf == 0 else []),
                   pb[bi][:, :w5], wdv[:, fcf, :],
                   bufA[:, A_HE + fcf * 512: A_HE + fcf * 512 + w5],
                   start=(fcf == 0), stop=(fcf == 7))
            t_dn = p.tok("pe")
            wfree(sd, t_dn)
            if fresh:
                t_ac = dve(lambda E, oc=oc, bi=bi: E.tensor_copy(
                    xch[:, oc * w5:(oc + 1) * w5], pb[bi][:, :w5]),
                    waits=[t_dn] + ([s4st["spill"]] if s4st["spill"] else []))
            else:
                t_ac = dve(lambda E, oc=oc, bi=bi: E.tensor_add(
                    xch[:, oc * w5:(oc + 1) * w5],
                    xch[:, oc * w5:(oc + 1) * w5], pb[bi][:, :w5]),
                    waits=[t_dn])
            bank_rd[bi] = t_ac

        for e in range(NE0):
            units.append(lambda e=e: u_wbc(e))
            for oc8 in range(8):
                units.append(lambda e=e, oc8=oc8: u_gu(
                    I[f"wg0_{e}"][:, oc8 * D:(oc8 + 1) * D],
                    I[f"wu0_{e}"][:, oc8 * D:(oc8 + 1) * D], oc8, True))
            for oc in range(FC):
                units.append(lambda e=e, oc=oc: u_down(
                    I[f"wd0_{e}"][:, oc * F0:(oc + 1) * F0], oc, e == 0))
        for oc8 in range(8):
            units.append(lambda oc8=oc8: u_gu(
                I["swg"][:, oc8 * D:(oc8 + 1) * D],
                I["swu"][:, oc8 * D:(oc8 + 1) * D], oc8, False))
        for oc in range(FC):
            units.append(lambda oc=oc: u_down(
                I["swd"][:, oc * F0:(oc + 1) * F0], oc, False))

        def u_spill():
            t_acc = p.tok("dve")
            s4st["spill"] = sy(lambda E: E.dma_start(
                out=moev[:, :, a5:a5 + w5],
                in_=xch[:, :FC * w5].rearrange("p (c t) -> p c t", t=w5)),
                waits=[t_acc], inc="sp")
            s4st["spills"].append(s4st["spill"])
        units.append(u_spill)
        return units

    s4_units = []
    for th5 in range(NT5):
        s4_units += mk_group(th5 * 512, min(512, TO - th5 * 512))

    O_QSB, O_QP, O_KP, O_VSB = 0, 24 * TCH, 40 * TCH, 48 * TCH
    O_MIXO, O_INVN, O_XN = 0, 16 * TCH, 40 * TCH
    s2_units = []
    for ch in range(NCH):
        c0 = ch * TCH
        cs = {"t_mm2_prev": None}
        xnv = bufB[:, O_XN:O_XN + FC * TCH].rearrange("p (c t) -> p c t", t=TCH)

        def u_xn(c0=c0, cs=cs):
            cs["t_xn"] = sy(lambda E, c0=c0: E.dma_start(
                out=bufB[:, O_XN:O_XN + FC * TCH].rearrange("p (c t) -> p c t",
                                                            t=TCH),
                in_=xn0v[:, :, c0:c0 + TCH]), waits=[t_ln_all] + engtoks())
        s2_units.append(u_xn)

        def u_proj(wnm, hh, bcol, obase, hidx, cs=cs, xnv=xnv):
            si, t_w = wload(I[wnm][:, hh * D:(hh + 1) * D], D)
            wtv = ws[si][:, :D].rearrange("p (c o) -> p c o", o=128)
            bank = pb[hh % 2]
            for fc in range(FC):
                mm([t_w, cs["t_xn"]] + ([bank_rd[hh % 2]] if fc == 0 else []),
                   bank[:, :TCH], wtv[:, fc, :], xnv[:, fc, :],
                   start=(fc == 0), stop=(fc == FC - 1))
            t_mm = p.tok("pe")
            wfree(si, t_mm)
            dst = bufA[:, obase + hh * TCH: obase + (hh + 1) * TCH]
            t_ev = act(lambda E, dst=dst, bank=bank, bcol=bcol, hh=hh:
                       E.activation(dst, bank[:, :TCH], AF.Identity,
                                    bias=biases[:, bcol + hh:bcol + hh + 1]),
                       waits=[t_mm, t_b])
            bank_rd[hh % 2] = t_ev
            if hidx is not None:
                t_sq = act(lambda E, dst=dst: E.activation(
                    stage[:, :TCH], dst, AF.Square), waits=[t_ev])
                t_n2 = mm([t_sq, t_ones, bank_rd[2]], pb[2][:, :TCH],
                          onesbf[:, :], stage[:, :TCH])
                t_mx = dve(lambda E: E.tensor_scalar(
                    tmp[0][:, :TCH], pb[2][:, :TCH], 1e-12, None, ALU.max),
                    waits=[t_n2])
                bank_rd[2] = t_mx
                t_l = act(lambda E: E.activation(tmp[0][:, :TCH], tmp[0][:, :TCH],
                                                 AF.Ln), waits=[t_mx])
                act(lambda E, hidx=hidx: E.activation(
                    bufB[:, O_INVN + hidx * TCH: O_INVN + (hidx + 1) * TCH],
                    tmp[0][:, :TCH], AF.Exp, scale=-0.5), waits=[t_l])

        for pname, wnm, nh, bcol, obase in [
                ("q", "wq0", HQ, B_BQ, O_QSB),
                ("k", "wk0", HKV, B_BK, O_QSB + HQ * TCH),
                ("v", "wv0", HKV, B_BV, O_VSB)]:
            for hh in range(nh):
                hidx = (hh if pname == "q" else HQ + hh) if pname != "v" else None
                s2_units.append(lambda wnm=wnm, hh=hh, bcol=bcol, obase=obase,
                                hidx=hidx, f=u_proj: f(wnm, hh, bcol, obase,
                                                       hidx))

        def u_projbar(cs=cs):
            cs["t_proj"] = [p.tok("act"), p.tok("dve")]
        s2_units.append(u_projbar)

        def u_feat(hh, cs=cs, c0=c0):
            t_proj = cs["t_proj"]
            src = bufA[:, O_QSB + hh * TCH: O_QSB + (hh + 1) * TCH]
            t_rot = mm(t_proj + [t_prot, bank_rd[2]], pb[2][:, :TCH], prot[:, :],
                       src)
            t_qc = dve(lambda E, src=src, c0=c0: E.tensor_mul(
                tmp[0][:, :TCH], src, costab[:, c0:c0 + TCH]),
                waits=t_tabs + t_proj)
            t_rs = dve(lambda E, c0=c0: E.tensor_mul(
                tmp[1][:, :TCH], pb[2][:, :TCH], sintab[:, c0:c0 + TCH]),
                waits=[t_rot] + t_tabs)
            bank_rd[2] = t_rs
            t_qr = dve(lambda E: E.tensor_add(stage[:, :TCH], tmp[0][:, :TCH],
                                              tmp[1][:, :TCH]), waits=[t_qc, t_rs])
            t_p0 = mm([t_qr, t_rt, bank_rd[2]], pb[2][:64, :TCH], rt0[:, :],
                      stage[:, :TCH])
            t_ps = dve(lambda E, hh=hh: E.tensor_mul(
                tmp[0][:64, :TCH], pb[2][:64, :TCH],
                bufB[:64, O_INVN + hh * TCH: O_INVN + (hh + 1) * TCH]),
                waits=[t_p0])
            bank_rd[2] = t_ps
            t_y = dve(lambda E: E.tensor_scalar(tmp[1][:64, :TCH], tmp[0][:64, :TCH],
                                                1.0 / TWO_PI, 24.5, ALU.mult,
                                                ALU.add),
                      waits=[t_ps])
            t_ki = dve(lambda E: E.tensor_copy(tmpi[:64, :TCH], tmp[1][:64, :TCH]),
                       waits=[t_y])
            t_kf = dve(lambda E: E.tensor_copy(tmp[3][:64, :TCH], tmpi[:64, :TCH]),
                       waits=[t_ki])
            t_d = dve(lambda E: E.tensor_sub(tmp[1][:64, :TCH], tmp[1][:64, :TCH],
                                             tmp[3][:64, :TCH]), waits=[t_kf])
            t_g = dve(lambda E: E.tensor_scalar(tmp[1][:64, :TCH], tmp[1][:64, :TCH],
                                                0.0, None, ALU.is_ge), waits=[t_d])
            t_k = dve(lambda E: E.scalar_tensor_tensor(
                tmp[1][:64, :TCH], tmp[1][:64, :TCH], -25.0, tmp[3][:64, :TCH],
                ALU.add, ALU.add), waits=[t_g])
            t_q1r = dve(lambda E: E.scalar_tensor_tensor(
                tmp[2][:64, :TCH], tmp[1][:64, :TCH], -_C1, tmp[0][:64, :TCH],
                ALU.mult, ALU.add), waits=[t_k])
            t_q2r = dve(lambda E: E.scalar_tensor_tensor(
                tmp[2][:64, :TCH], tmp[1][:64, :TCH], -_C2, tmp[2][:64, :TCH],
                ALU.mult, ALU.add), waits=[t_q1r])
            t_red = dve(lambda E: E.scalar_tensor_tensor(
                tmp[0][:64, :TCH], tmp[1][:64, :TCH], -_C3, tmp[2][:64, :TCH],
                ALU.mult, ALU.add), waits=[t_q2r])
            dq = (bufA[:, O_QP + hh * TCH: O_QP + (hh + 1) * TCH] if hh < HQ else
                  bufA[:, O_KP + (hh - HQ) * TCH: O_KP + (hh - HQ + 1) * TCH])
            act(lambda E, dq=dq: E.activation(dq[:64, :], tmp[0][:64, :TCH],
                                              AF.Sin),
                waits=[t_red])
            t_s2 = act(lambda E: E.activation(tmp[3][:64, :TCH], tmp[0][:64, :TCH],
                                              AF.Sin, scale=0.5), waits=[t_red])
            t_sqq = act(lambda E: E.activation(tmp[3][:64, :TCH], tmp[3][:64, :TCH],
                                               AF.Square), waits=[t_s2])
            dve(lambda E, dq=dq: E.tensor_scalar(dq[64:128, :], tmp[3][:64, :TCH],
                                                 -2.0, 1.0, ALU.mult, ALU.add),
                waits=[t_sqq])

        for hh in range(HQ + HKV):
            s2_units.append(lambda hh=hh, f=u_feat: f(hh))

        def u_featbar(cs=cs):
            cs["t_feats"] = [p.tok("dve"), p.tok("act")]
        s2_units.append(u_featbar)

        def u_mix(sb32, cs=cs):
            t_feats = cs["t_feats"]
            qpv = bufA[:, O_QP:O_QP + HQ * TCH].rearrange("p (h t) -> p h t",
                                                          t=TCH)
            kpv = bufA[:, O_KP:O_KP + HKV * TCH].rearrange("p (h t) -> p h t",
                                                           t=TCH)
            tl0 = sb32 * 32
            for i in range(HKV):
                t_t = tr(t_feats + [t_id, pT_rd[0]], pT[:32, :128],
                         bufA[:, O_VSB + i * TCH + tl0: O_VSB + i * TCH + tl0 + 32],
                         ident[:, :])
                t_e = dve(lambda E: E.tensor_scalar(stage[:32, :128], pT[:32, :128],
                                                    2.0 / 128.0, None, ALU.mult),
                          waits=[t_t, p.tok("gd")])
                pT_rd[0] = t_e
                gdma(lambda E, i=i: E.dma_start(out=vhp[i:i + 1, :],
                                                in_=stage[:32, :128]),
                     waits=[t_e] + ([cs["t_mm2_prev"]] if cs["t_mm2_prev"]
                                    else []))
            t_vhp = p.tok("gd")
            for sl in range(32):
                s = tl0 + sl
                mm(t_feats + ([bank_rd[0]] if sl == 0 else []),
                   pb[0][:8, sl * 16:sl * 16 + 16], kpv[:, :, s], qpv[:, :, s])
            t_mm1 = p.tok("pe")
            t_ae = dve(lambda E: E.tensor_copy(stage[:8, 512:1024],
                                               pb[0][:8, :512]), waits=[t_mm1])
            bank_rd[0] = t_ae
            for sl in range(32):
                mm([t_ae, t_vhp] + ([bank_rd[1]] if sl == 0 else []),
                   pb[1][:, sl * 16:sl * 16 + 16],
                   vhp[:8, sl * 128:(sl + 1) * 128],
                   stage[:8, 512 + sl * 16:512 + sl * 16 + 16])
            t_mm2 = p.tok("pe")
            cs["t_mm2_prev"] = t_mm2
            t_me = dve(lambda E, tl0=tl0: E.tensor_copy(
                bufB[:, O_MIXO:O_MIXO + HQ * TCH].rearrange(
                    "p (h t) -> p t h", h=HQ)[:, tl0: tl0 + 32, :],
                pb[1][:, :512].rearrange("p (t h) -> p t h", h=16)),
                waits=[t_mm2, p.tok("sp")])
            bank_rd[1] = t_me

        for sb32 in range(TCH // 32):
            s2_units.append(lambda sb32=sb32, f=u_mix: f(sb32))

        def u_store(c0=c0):
            sy(lambda E, c0=c0: E.dma_start(
                out=mixo_d.ap().rearrange("p (h t) -> p h t", t=TO)[:, :,
                                                                   c0:c0 + TCH],
                in_=bufB[:, O_MIXO:O_MIXO + HQ * TCH].rearrange(
                    "p (h t) -> p h t", t=TCH)),
                waits=[p.tok("dve")], inc="sp")
        s2_units.append(u_store)

    nf, ns = len(s4_units), len(s2_units)
    fi = 0
    for si, u in enumerate(s2_units):
        u()
        want = (si + 1) * nf // ns
        while fi < want:
            s4_units[fi]()
            fi += 1
    while fi < nf:
        s4_units[fi]()
        fi += 1
    t_mixd = p.tok("sp")

    groups = [[2 * i, 2 * i + 1] for i in range(max(1, ncores // 2))]
    t_cc3 = p.op("gpsimd", lambda E: E.collective_compute(
        "AllGather", ALU.bypass, replica_groups=groups,
        ins=[mixo_d.ap().opt()], outs=[mixo_full.ap().opt()]),
        waits=[t_mixd], inc="cc", selfwait=False)

    HBLK = SEQ // 16
    for i in range(TO // HBLK):
        S2T = 2 * TO
        t_lA = sy(lambda E, i=i: E.dma_start(
            out=bufA[:, :S2T].rearrange("p (r t) -> p r t", r=2),
            in_=mixo_full.ap().rearrange("r p (h t) -> r p h t", t=TO)
            [:, :, i, :].rearrange("r p t -> p r t")),
            waits=[t_cc3] + engtoks())
        t_lB = sy(lambda E, i=i: E.dma_start(
            out=bufA[:, S2T:2 * S2T].rearrange("p (r t) -> p r t", r=2),
            in_=mixo_full.ap().rearrange("r p (h t) -> r p h t", t=TO)
            [:, :, 8 + i, :].rearrange("r p t -> p r t")),
            waits=[t_cc3] + engtoks(), inc="xd2")
        t_se = dve(lambda E: E.tensor_scalar(bufA[:, 2 * S2T:3 * S2T],
                                             bufA[:, :S2T],
                                             hsel[:, 0:1], None, ALU.mult),
                   waits=[t_lA, t_hsel])
        t_cb = dve(lambda E: E.scalar_tensor_tensor(
            bufA[:, 2 * S2T:3 * S2T], bufA[:, S2T:2 * S2T], hsel[:, 1:2],
            bufA[:, 2 * S2T:3 * S2T], ALU.mult, ALU.add), waits=[t_lB, t_se])
        t_pm = dve(lambda E, i=i: E.tensor_copy(
            bufB[:, :HQ * TO].rearrange("p (h t) -> p h t", t=TO)
            [:, :, i * HBLK:(i + 1) * HBLK],
            bufA[:, 2 * S2T:3 * S2T].rearrange("p (u h) -> p h u", h=16)),
            waits=[t_cb])
    t_a0in = p.tok("dve")
    for oc in range(FC):
        si, t_w = wload(I["wo0"][:, oc * D:(oc + 1) * D], D)
        wtv = ws[si][:, :D].rearrange("p (c o) -> p c o", o=128)
        for j5 in range(NT5):
            a5 = j5 * 512
            w5 = min(512, TO - a5)
            bank = pb[j5 % 2]
            for fc in range(FC):
                mm([t_w, t_a0in] + ([bank_rd[j5 % 2]] if fc == 0 else []),
                   bank[:, :w5], wtv[:, fc, :],
                   bufB[:, fc * TO + a5: fc * TO + a5 + w5],
                   start=(fc == 0), stop=(fc == FC - 1))
            t_mm = p.tok("pe")
            t_ev = act(lambda E, bank=bank, oc=oc, w5=w5: E.activation(
                stage[:, :w5], bank[:, :w5], AF.Identity,
                bias=biases[:, B_BO + oc:B_BO + oc + 1]),
                waits=[t_mm, t_b, p.tok("sp")])
            bank_rd[j5 % 2] = t_ev
            sy(lambda E, oc=oc, a5=a5, w5=w5: E.dma_start(
                out=a0v[:, oc, a5:a5 + w5], in_=stage[:, :w5]),
                waits=[t_ev], inc="sp")
        wfree(si, p.tok("pe"))
    t_a0 = p.tok("sp")
    dbg_out("a0", a0_d.ap(), [t_a0])


    for th5 in range(NT5):
        a5 = th5 * 512
        w5 = min(512, TO - a5)
        for fc in range(FC):
            t_xl = sy(lambda E, fc=fc, a5=a5, w5=w5: E.dma_start(
                out=stgf[:, :w5], in_=xfmv[:, fc, a5:a5 + w5]),
                inc="xd2", waits=engtoks())
            t_al = sy(lambda E, fc=fc, a5=a5, w5=w5: E.dma_start(
                out=stage[:, :w5], in_=a0v[:, fc, a5:a5 + w5]),
                waits=[t_a0] + engtoks())
            t_ml = sy(lambda E, fc=fc, a5=a5, w5=w5: E.dma_start(
                out=tmp[1][:, :w5], in_=moev[:, fc, a5:a5 + w5]),
                waits=[s4st["spills"][th5]] + engtoks())
            t_s1 = dve(lambda E, w5=w5: E.tensor_add(
                tmp[0][:, :w5], stgf[:, :w5], stage[:, :w5]),
                waits=[t_xl, t_al])
            t_s2 = dve(lambda E, w5=w5: E.tensor_add(
                stgf[:, 512:512 + w5], tmp[0][:, :w5], tmp[1][:, :w5]),
                waits=[t_s1, t_ml, p.tok("sp")])
            sy(lambda E, fc=fc, a5=a5, w5=w5: E.dma_start(
                out=x1v[:, fc, a5:a5 + w5], in_=stgf[:, 512:512 + w5]),
                waits=[t_s2], inc="sp")
    t_x1 = p.tok("sp")
    dbg_out("x1", x1_d.ap(), [t_x1])

    for th in range(NT5):
        t0 = th * 512
        w5 = min(512, TO - t0)
        ln_chunk(x1v, t0, w5, xn1v, t_x1)
    t_xn1 = p.tok("sp")
    dbg_out("xn1", xn1_own.ap(), [t_xn1])

    t_cc1 = p.op("gpsimd", lambda E: E.collective_compute(
        "AllGather", ALU.bypass, replica_groups=groups,
        ins=[xn1_own.ap().opt()], outs=[xn1_full.ap().opt()]),
        waits=[t_xn1], inc="cc", selfwait=False)

    t_xf = sy(lambda E: E.dma_start(
        out=bufA[:, :2 * FC * TO].rearrange("p (r n) -> p r n", r=2),
        in_=xn1_full.ap().rearrange("r p n -> p r n")),
        waits=[t_cc1] + engtoks())

    def xn1tile(fc, j5):
        r = (j5 * 512) // TO
        tt = (j5 * 512) % TO
        base = r * FC * TO + fc * TO + tt
        return bufA[:, base: base + 512]

    for wi, (wnm, dnm) in enumerate([("wq1", q1_d), ("wk1", k1_d), ("wv1", v1_d)]):
        for hh in range(H1):
            si, t_w = wload(I[wnm][:, hh * D:(hh + 1) * D], D)
            wtv = ws[si][:, :D].rearrange("p (c o) -> p c o", o=128)
            for j5 in range(SEQ // 512):
                bank = pb[j5 % 2]
                for fc in range(FC):
                    mm([t_w, t_xf] + ([bank_rd[j5 % 2]] if fc == 0 else []),
                       bank[:, :512], wtv[:, fc, :], xn1tile(fc, j5),
                       start=(fc == 0), stop=(fc == FC - 1))
                t_mm = p.tok("pe")
                t_ev = dve(lambda E, hh=hh, j5=j5, bank=bank: E.tensor_copy(
                    bufB[:, hh * SEQ + j5 * 512: hh * SEQ + j5 * 512 + 512],
                    bank[:, :512]), waits=[t_mm, p.tok("sp")])
                bank_rd[j5 % 2] = t_ev
            wfree(si, p.tok("pe"))
        t_evs = p.tok("dve")
        for hh in range(H1):
            qh = bufB[:, hh * SEQ:(hh + 1) * SEQ]
            for j5 in range(SEQ // 512):
                sl = qh[:, j5 * 512:(j5 + 1) * 512]
                t_sq = act(lambda E, sl=sl: E.activation(stage[:, :512], sl,
                                                         AF.Square), waits=[t_evs])
                t_n2 = mm([t_sq, t_ones, bank_rd[2]], pb[2][:, :512], onesbf[:, :],
                          stage[:, :512])
                t_mx = dve(lambda E: E.tensor_scalar(tmp[0][:, :512], pb[2][:, :512],
                                                     1e-38, None, ALU.max),
                           waits=[t_n2])
                bank_rd[2] = t_mx
                t_l = act(lambda E: E.activation(tmp[0][:, :512], tmp[0][:, :512],
                                                 AF.Ln), waits=[t_mx])
                t_n = act(lambda E: E.activation(tmp[0][:, :512], tmp[0][:, :512],
                                                 AF.Exp, scale=0.5), waits=[t_l])
                t_pe2 = dve(lambda E: E.tensor_scalar(tmp[0][:, :512],
                                                      tmp[0][:, :512], 1e-8, None,
                                                      ALU.add), waits=[t_n])
                t_l2 = act(lambda E: E.activation(tmp[0][:, :512], tmp[0][:, :512],
                                                  AF.Ln), waits=[t_pe2])
                t_iv = act(lambda E: E.activation(tmp[0][:, :512], tmp[0][:, :512],
                                                  AF.Exp, scale=-1.0), waits=[t_l2])
                if wi == 1:
                    dve(lambda E, sl=sl: E.scalar_tensor_tensor(
                        sl, sl, taunu[:, 0:1], tmp[0][:, :512], ALU.mult, ALU.mult),
                        waits=[t_iv, t_tn])
                else:
                    dve(lambda E, sl=sl: E.tensor_mul(sl, sl, tmp[0][:, :512]),
                        waits=[t_iv])
        t_nrm = p.tok("dve")
        if wi < 2:
            sy(lambda E, dnm=dnm: E.dma_start(
                out=dnm.ap().rearrange("h p t -> p h t"),
                in_=bufB[:, :H1 * SEQ].rearrange("p (h t) -> p h t", h=H1)),
                waits=[t_nrm], inc="sp")
        else:
            for tb2 in range(SB):
                for hh in range(H1):
                    tr([t_nrm, t_id] + ([pT_rd[0]] if hh == 0 else []),
                       pT[:, hh * 128:(hh + 1) * 128],
                       bufB[:, hh * SEQ + tb2 * 128: hh * SEQ + tb2 * 128 + 128],
                       ident[:, :])
                t_tt = p.tok("pe")
                t_pe3 = dve(lambda E: E.tensor_copy(stage[:, :1024], pT[:, :1024]),
                            waits=[t_tt, p.tok("sp")])
                pT_rd[0] = t_pe3
                sy(lambda E, tb2=tb2: E.dma_start(out=v1_d.ap()[tb2],
                                                  in_=stage[:, :1024]),
                   waits=[t_pe3], inc="sp")
    t_qkv1 = p.tok("sp")
    dbg_out("q1", q1_d.ap(), [t_qkv1])
    dbg_out("k1", k1_d.ap(), [t_qkv1])
    dbg_out("v1", v1_d.ap(), [t_qkv1])

    E0pe = p.tok("pe")
    E0act = p.tok("act")
    E0dve = p.tok("dve")
    hslot_rd = [[E0pe, E0dve, E0act], [E0pe, E0dve, E0act]]
    qslot_rd = [[E0pe, E0act, t_qkv1], [E0pe, E0act, t_qkv1]]
    dgslot_rd = [[E0pe, E0act, t_qkv1], [E0pe, E0act, t_qkv1]]
    outslot_rd = [[E0dve], [E0dve]]
    eslot_rd = [[E0pe, E0dve, t_qkv1], [E0pe, E0dve, t_qkv1]]
    etslot_rd = [[E0pe], [E0pe]]
    dstg_rd = [[t_qkv1, E0dve], [t_qkv1, E0dve]]
    pz_rd = [[E0act], [E0act]]
    hload = [None, None]
    iters8 = [(h, qb) for h in range(H1) for qb in range(SB)]

    def phaseA(i):
        h, qb = iters8[i]
        ip = i % 2
        hp = h % 2
        abase = 8192 + hp * 4096
        if qb == 0:
            t_kl = sy(lambda E, h=h, abase=abase: E.dma_start(
                out=bufA[:, abase:abase + SEQ], in_=k1_d.ap()[h]),
                waits=[t_qkv1] + hslot_rd[hp])
            t_vl = sy(lambda E, h=h, abase=abase: E.dma_start(
                out=bufA[:, abase + SEQ:abase + SEQ + SB * 128].rearrange(
                    "p (b t) -> p b t", b=SB),
                in_=v1_d.ap()[:, :, h * 128:(h + 1) * 128].rearrange(
                    "b p t -> p b t")),
                waits=[t_qkv1] + hslot_rd[hp], inc="xd2")
            hload[hp] = (t_kl, t_vl)
        t_kl, t_vl = hload[hp]
        qoff = ip * 128
        t_ql = sy(lambda E, h=h, qb=qb, qoff=qoff: E.dma_start(
            out=stage[:, qoff:qoff + 128],
            in_=q1_d.ap()[h][:, qb * 128:(qb + 1) * 128]),
            waits=[t_qkv1] + qslot_rd[ip])
        pcol = ip * 8
        t_z = dve(lambda E, pcol=pcol: E.memset(tmp[3][:, pcol:pcol + 8], 0.0),
                  waits=pz_rd[ip])
        nk = qb + 1
        ebase = 16384 + ip * 4096
        t_sums = [t_z]
        t_diag = None
        for kc in range(cdiv(nk * 128, 512)):
            k0 = kc * 512
            wk = min(512, nk * 128 - k0)
            t_sc = mm([t_ql, t_kl, bank_rd[ip]], pb[ip][:, :wk],
                      stage[:, qoff:qoff + 128],
                      bufA[:, abase + k0:abase + k0 + wk])
            ndiag = 128 if k0 + wk == nk * 128 else 0
            wfull = wk - ndiag
            if wfull > 0:
                t_ex = act(lambda E, ip=ip, k0=k0, wfull=wfull, kc=kc,
                           ebase=ebase, pcol=pcol:
                           E.activation(bufA[:, ebase + k0:ebase + k0 + wfull],
                                        pb[ip][:, :wfull], AF.Exp,
                                        accum_out=tmp[3][:, pcol + kc:
                                                         pcol + kc + 1]),
                           waits=[t_sc, t_z] + eslot_rd[ip])
                t_sums.append(t_ex)
                bank_rd[ip] = t_ex
            if ndiag:
                soff = ip * 128
                t_ed = act(lambda E, ip=ip, wfull=wfull, soff=soff:
                           E.activation(stgf[:, soff:soff + 128],
                                        pb[ip][:, wfull:wfull + 128], AF.Exp),
                           waits=[t_sc] + dstg_rd[ip])
                bank_rd[ip] = t_ed
                t_diag = (t_ed, k0, wfull)
        qslot_rd[ip] = [p.tok("pe")]
        return {"h": h, "qb": qb, "ip": ip, "nk": nk, "ebase": ebase,
                "abase": abase, "pcol": pcol, "t_sums": t_sums, "t_vl": t_vl,
                "t_z": t_z, "diag": t_diag}

    def phaseB(C):
        h, qb, ip = C["h"], C["qb"], C["ip"]
        nk, ebase, abase, pcol = C["nk"], C["ebase"], C["abase"], C["pcol"]
        t_sums = C["t_sums"]
        t_ed, k0d, wfd = C["diag"]
        t_mk = dve(lambda E, k0=k0d, wfull=wfd, ebase=ebase, ip=ip:
                   E.tensor_mul(bufA[:, ebase + k0 + wfull:
                                     ebase + k0 + wfull + 128],
                                stgf[:, ip * 128:ip * 128 + 128], cmask[:, :]),
                   waits=[t_ed, t_cm] + eslot_rd[ip])
        dstg_rd[ip] = [t_mk]
        t_ds = dve(lambda E, k0=k0d, wfull=wfd, ebase=ebase, pcol=pcol:
                   E.reduce_sum(tmp[3][:, pcol + 7:pcol + 8],
                                bufA[:, ebase + k0 + wfull:
                                     ebase + k0 + wfull + 128],
                                axis=AX.X), waits=[t_mk, C["t_z"]])
        t_sums = t_sums + [t_ds]
        pz_rd[ip] = list(t_sums)
        rcol = 3 * ip
        t_rs = dve(lambda E, pcol=pcol, rcol=rcol: E.reduce_sum(
            tmp[2][:, rcol:rcol + 1], tmp[3][:, pcol:pcol + 8], axis=AX.X),
            waits=t_sums)
        t_iv = dve(lambda E, rcol=rcol: E.reciprocal(
            tmp[2][:, rcol + 1:rcol + 2], tmp[2][:, rcol:rcol + 1]),
            waits=[t_rs])
        t_sc2 = dve(lambda E, rcol=rcol: E.tensor_scalar(
            tmp[2][:, rcol + 2:rcol + 3], tmp[2][:, rcol + 1:rcol + 2],
            taunu[:, 1:2], None, ALU.mult), waits=[t_iv, t_tn])
        dgoff = 256 + ip * 128
        t_dg = dve(lambda E, rcol=rcol, dgoff=dgoff: E.tensor_scalar(
            stage[:, dgoff:dgoff + 128], ident[:, :],
            tmp[2][:, rcol + 2:rcol + 3], None, ALU.mult),
            waits=[t_sc2, t_id] + dgslot_rd[ip])
        t_pe4s = []
        for kb4 in range(cdiv(nk, 4)):
            nkb = min(4, nk - kb4 * 4)
            for kk in range(nkb):
                kb = kb4 * 4 + kk
                mm([t_dg] + ([bank_rd[2]] if kk == 0 else []),
                   pb[2][:, kk * 128:(kk + 1) * 128],
                   bufA[:, ebase + kb * 128:ebase + (kb + 1) * 128],
                   stage[:, dgoff:dgoff + 128])
            t_et = p.tok("pe")
            t_pe4 = dve(lambda E, kb4=kb4, nkb=nkb, ebase=ebase: E.tensor_copy(
                bufA[:, ebase + 2048 + kb4 * 512:
                     ebase + 2048 + kb4 * 512 + nkb * 128],
                pb[2][:, :nkb * 128]),
                waits=[t_et] + etslot_rd[ip])
            etslot_rd[ip] = []
            bank_rd[2] = t_pe4
            t_pe4s.append(t_pe4)
        t_eTend = p.tok("pe")
        dgslot_rd[ip] = [t_eTend]
        eslot_rd[ip] = [t_eTend]
        for kb in range(nk):
            mm(t_pe4s + [C["t_vl"]] + ([bank_rd[ip]] if kb == 0 else []),
               pb[ip][:, :128],
               bufA[:, abase + SEQ + kb * 128:abase + SEQ + (kb + 1) * 128],
               bufA[:, ebase + 2048 + kb * 128:ebase + 2048 + (kb + 1) * 128],
               start=(kb == 0), stop=(kb == nk - 1))
        t_pv = p.tok("pe")
        etslot_rd[ip] = [t_pv]
        if qb == SB - 1:
            hslot_rd[h % 2] = [t_pv]
        ooff = 24576 + ip * 128
        t_oe = dve(lambda E, ooff=ooff, ip=ip: E.tensor_copy(
            bufA[:, ooff:ooff + 128], pb[ip][:, :128]),
            waits=[t_pv] + outslot_rd[ip])
        bank_rd[ip] = t_oe
        t_od = sy(lambda E, h=h, qb=qb, ooff=ooff: E.dma_start(
            out=att_own.ap()[h][:, qb * 128:(qb + 1) * 128],
            in_=bufA[:, ooff:ooff + 128]), waits=[t_oe], inc="sp")
        outslot_rd[ip] = [t_od]

    t_xt0 = gdma(lambda E: E.dma_start(out=xt_d.ap()[:, :], in_=xn1_own.ap()[:, :]),
                 waits=[t_xn1])
    G_LG, SC, G_WE = 256, 776, 800
    s10 = {"round_done": t_xt0, "bb": [E0pe, E0dve, t_qkv1], "res_rd": [E0pe],
           "t_xl": None, "t_res": None, "t_w6": None, "t_le": None,
           "t_gate": None, "t_w1st": None, "t_wbc": None, "t_h1a": None,
           "t_hx": None, "t_sua": None, "t_hsh": None, "first": True}
    O_H1b, O_H2b = FC * 512, FC * 512 + 8 * 512
    s10_units = []

    def s10_build():
        units = []
        for rnd in range(2):
            for j5 in range(NT5):
                a5 = j5 * 512
                w5 = min(512, TO - a5)

                def u_glog(rnd=rnd, a5=a5, w5=w5):
                    t_xl = sy(lambda E: E.dma_start(
                        out=bufB[:, :FC * w5].rearrange("p (c t) -> p c t", t=w5),
                        in_=xtv[:, :, a5:a5 + w5]),
                        waits=[s10["round_done"]] + s10["bb"])
                    xvv = bufB[:, :FC * w5].rearrange("p (c t) -> p c t", t=w5)
                    sg, t_wg = wload(I[f"wgate1_{rnd}"], FC * 8)
                    for fc in range(FC):
                        mm([t_wg, t_xl] + ([bank_rd[4]] if fc == 0 else []),
                           pb[4][:8, :w5], ws[sg][:, fc * 8:(fc + 1) * 8],
                           xvv[:, fc, :], start=(fc == 0), stop=(fc == FC - 1))
                    t_lg = p.tok("pe")
                    wfree(sg, t_lg)
                    s10["t_le"] = act(lambda E: E.activation(
                        stgf[:8, G_LG:G_LG + w5], pb[4][:8, :w5], AF.Identity,
                        bias=bg1[:8, rnd:rnd + 1]), waits=[t_lg, t_bg0, t_bg1])
                    bank_rd[4] = s10["t_le"]
                    s10["bb"] = [t_lg]
                units.append(u_glog)

                for tb2 in range(w5 // 128):
                    def u_gtb(tb2=tb2, a5=a5, w5=w5):
                        tt0 = a5 + tb2 * 128
                        t_le = s10["t_le"]
                        t_tr2 = tr([t_le, t_idf, bank_rd[5]], pb[5][:, :8],
                                   stgf[:8, G_LG + tb2 * 128:
                                        G_LG + (tb2 + 1) * 128],
                                   identf[:8, :8])
                        t_tm = dve(lambda E: E.tensor_copy(
                            stgf[:, SC:SC + 8], pb[5][:, :8]), waits=[t_tr2])
                        bank_rd[5] = t_tm
                        lg = stgf[:, SC:SC + 8]
                        t_m1 = dve(lambda E: E.tensor_reduce(
                            stgf[:, SC + 8:SC + 9], lg, AX.X, ALU.min),
                            waits=[t_tm])
                        t_e1 = dve(lambda E: E.tensor_scalar(
                            tmp[0][:, :8], lg, stgf[:, SC + 8:SC + 9], None,
                            ALU.is_le), waits=[t_m1])
                        t_xu = dve(lambda E: E.scalar_tensor_tensor(
                            tmp[1][:, :8], tmp[0][:, :8], 1e30, lg, ALU.mult,
                            ALU.add), waits=[t_e1])
                        t_m2 = dve(lambda E: E.tensor_reduce(
                            stgf[:, SC + 9:SC + 10], tmp[1][:, :8], AX.X,
                            ALU.min), waits=[t_xu])
                        t_top = dve(lambda E: E.tensor_scalar(
                            tmp[0][:, :8], lg, stgf[:, SC + 9:SC + 10], None,
                            ALU.is_gt), waits=[t_m2, t_tm])
                        t_mx = dve(lambda E: E.reduce_max(
                            stgf[:, SC + 10:SC + 11], lg, axis=AX.X),
                            waits=[t_tm])
                        t_mn = dve(lambda E: E.tensor_scalar(
                            stgf[:, SC + 11:SC + 12], stgf[:, SC + 10:SC + 11],
                            -1.0, None, ALU.mult), waits=[t_mx])
                        t_ee = act(lambda E: E.activation(
                            tmp[1][:, :8], lg, AF.Exp,
                            bias=stgf[:, SC + 11:SC + 12]), waits=[t_mn])
                        t_em = dve(lambda E: E.tensor_mul(
                            tmp[1][:, :8], tmp[1][:, :8], tmp[0][:, :8]),
                            waits=[t_ee, t_top])
                        t_ss = dve(lambda E: E.reduce_sum(
                            stgf[:, SC + 12:SC + 13], tmp[1][:, :8], axis=AX.X),
                            waits=[t_em])
                        t_rc = dve(lambda E: E.reciprocal(
                            stgf[:, SC + 13:SC + 14], stgf[:, SC + 12:SC + 13]),
                            waits=[t_ss])
                        t_wv = dve(lambda E: E.tensor_scalar(
                            tmp[1][:, :8], tmp[1][:, :8],
                            stgf[:, SC + 13:SC + 14], None, ALU.mult),
                            waits=[t_rc])
                        t_tw2 = tr([t_wv, t_idf, bank_rd[6]], pb[6][:8, :128],
                                   tmp[1][:, :8], identf[:, :128])
                        t_we2 = dve(lambda E: E.tensor_copy(
                            stgf[:8, G_WE:G_WE + 128], pb[6][:8, :128]),
                            waits=[t_tw2, p.tok("sp")])
                        bank_rd[6] = t_we2
                        sy(lambda E, tt0=tt0: E.dma_start(
                            out=wfm1_d.ap()[:8, tt0:tt0 + 128],
                            in_=stgf[:8, G_WE:G_WE + 128]),
                            waits=[t_we2, p.tok("od")], inc="sp")
                    units.append(u_gtb)

            def u_gbar():
                s10["t_gate"] = p.tok("dve")
                s10["t_w1st"] = p.tok("sp")
            units.append(u_gbar)

            for th5 in range(NT5):
                a5 = th5 * 512
                w5 = min(512, TO - a5)

                def u_xload(a5=a5, w5=w5):
                    s10["t_xl"] = sy(lambda E: E.dma_start(
                        out=bufB[:, :FC * w5].rearrange("p (c t) -> p c t",
                                                        t=w5),
                        in_=xtv[:, :, a5:a5 + w5]),
                        waits=[s10["round_done"], s10["t_gate"]] + s10["bb"])
                    s10["t_res"] = sy(lambda E: E.dma_start(
                        out=bufA[:, :FC * w5].rearrange("p (c t) -> p c t",
                                                        t=w5),
                        in_=xn1v[:, :, a5:a5 + w5]),
                        waits=[t_xn1] + s10["res_rd"], inc="xd2")
                    s10["t_w6"] = sy(lambda E: E.dma_start(
                        out=w6[:NE1, :w5], in_=wfm1_d.ap()[:NE1, a5:a5 + w5]),
                        waits=[s10["t_w1st"]] + s10["bb"], inc="xd2")
                    s10["first"] = True
                units.append(u_xload)
                xvv = bufB[:, :FC * w5].rearrange("p (c t) -> p c t", t=w5)

                def u_wbc1(n, a5=a5, w5=w5):
                    t_rr = sy(lambda E, n=n: E.dma_start(
                        out=lrow[:1, :w5],
                        in_=wfm1_d.ap()[n:n + 1, a5:a5 + w5]),
                        waits=[s10["t_w1st"], p.tok("pe")])
                    t_bc = mm([t_rr, t_ofr, bank_rd[3]], pb[3][:, :w5],
                              onesfr[:1, :], lrow[:1, :w5])
                    s10["t_wbc"] = dve(lambda E: E.tensor_copy(
                        bufC[:, :w5], pb[3][:, :w5]), waits=[t_bc])
                    bank_rd[3] = s10["t_wbc"]

                def u_up(n, oc8, w5=w5, xvv=xvv, wb=u_wbc1):
                    if oc8 == 0:
                        wb(n)
                    sw, t_w = wload(I[f"wup1_{n}"][:, oc8 * D:(oc8 + 1) * D], D)
                    wv_ = ws[sw][:, :D].rearrange("p (c o) -> p c o", o=128)
                    for fc in range(FC):
                        mm([t_w, s10["t_xl"]] + ([bank_rd[4]] if fc == 0
                                                 else []),
                           pb[4][:, :w5], wv_[:, fc, :], xvv[:, fc, :],
                           start=(fc == 0), stop=(fc == FC - 1))
                    t_mmu = p.tok("pe")
                    wfree(sw, t_mmu)
                    t_h1 = act(lambda E, oc8=oc8, n=n: E.activation(
                        bufB[:, O_H1b + oc8 * 512: O_H1b + oc8 * 512 + w5],
                        pb[4][:, :w5], AF.Identity,
                        bias=bup1[:, n * 8 + oc8:n * 8 + oc8 + 1]),
                        waits=[t_mmu, t_bup])
                    bank_rd[4] = t_h1

                def u_sw(n, oc8, w5=w5):
                    if oc8 == 0:
                        s10["t_h1a"] = p.tok("act")
                    sw, t_w = wload(I[f"wsw1_{n}"][:, oc8 * E1:(oc8 + 1) * E1],
                                    E1)
                    wv_ = ws[sw][:, :E1].rearrange("p (c o) -> p c o", o=128)
                    for fcf in range(8):
                        mm([t_w, s10["t_h1a"]] + ([bank_rd[5]] if fcf == 0
                                                  else []),
                           pb[5][:, :w5], wv_[:, fcf, :],
                           bufB[:, O_H1b + fcf * 512: O_H1b + fcf * 512 + w5],
                           start=(fcf == 0), stop=(fcf == 7))
                    t_mmw = p.tok("pe")
                    wfree(sw, t_mmw)
                    t_sl = act(lambda E, oc8=oc8, n=n: E.activation(
                        bufC[:, 512:512 + w5],
                        pb[5][:, :w5], AF.Silu,
                        bias=bsw1[:, n * 8 + oc8:n * 8 + oc8 + 1]),
                        waits=[t_mmw, t_bsw, p.tok("dve")])
                    bank_rd[5] = t_sl
                    t_h0 = dve(lambda E, oc8=oc8: E.tensor_mul(
                        bufC[:, 512:512 + w5], bufC[:, 512:512 + w5],
                        bufB[:, O_H1b + oc8 * 512: O_H1b + oc8 * 512 + w5]),
                        waits=[t_sl])
                    dve(lambda E, oc8=oc8: E.tensor_mul(
                        bufB[:, O_H2b + oc8 * 512: O_H2b + oc8 * 512 + w5],
                        bufC[:, 512:512 + w5],
                        bufC[:, :w5]), waits=[t_h0, s10["t_wbc"]])

                def u_dn(n, oc, w5=w5):
                    if oc == 0:
                        s10["t_hx"] = p.tok("dve")
                    sd, t_d = wload(I[f"wdn1_{n}"][:, oc * E1:(oc + 1) * E1],
                                    E1)
                    wdv = ws[sd][:, :E1].rearrange("p (c o) -> p c o", o=128)
                    for fcf in range(8):
                        mm([t_d, s10["t_hx"]] + ([bank_rd[6]] if fcf == 0
                                                 else []),
                           pb[6][:, :w5], wdv[:, fcf, :],
                           bufB[:, O_H2b + fcf * 512: O_H2b + fcf * 512 + w5],
                           start=(fcf == 0), stop=(fcf == 7))
                    t_dn = p.tok("pe")
                    wfree(sd, t_dn)
                    if s10["first"]:
                        t_ac = dve(lambda E, oc=oc: E.tensor_copy(
                            xch[:, oc * w5:(oc + 1) * w5], pb[6][:, :w5]),
                            waits=[t_dn])
                    else:
                        t_ac = dve(lambda E, oc=oc: E.tensor_add(
                            xch[:, oc * w5:(oc + 1) * w5],
                            xch[:, oc * w5:(oc + 1) * w5], pb[6][:, :w5]),
                            waits=[t_dn])
                    bank_rd[6] = t_ac

                for n in range(NE1):
                    for oc8 in range(8):
                        units.append(lambda n=n, oc8=oc8, f=u_up: f(n, oc8))
                    for oc8 in range(8):
                        units.append(lambda n=n, oc8=oc8, f=u_sw: f(n, oc8))
                    for oc in range(FC):
                        units.append(lambda n=n, oc=oc, f=u_dn: f(n, oc))
                    units.append(lambda: s10.__setitem__("first", False))

                def u_su(oc, w5=w5, xvv=xvv):
                    sw, t_w = wload(I["wsu1"][:, oc * D:(oc + 1) * D], D)
                    wv_ = ws[sw][:, :D].rearrange("p (c o) -> p c o", o=128)
                    for fc in range(FC):
                        mm([t_w, s10["t_xl"]] + ([bank_rd[4]] if fc == 0
                                                 else []),
                           pb[4][:, :w5], wv_[:, fc, :], xvv[:, fc, :],
                           start=(fc == 0), stop=(fc == FC - 1))
                    t_mmu = p.tok("pe")
                    wfree(sw, t_mmu)
                    t_su = act(lambda E, oc=oc: E.activation(
                        bufB[:, O_H1b + oc * 512: O_H1b + oc * 512 + w5]
                        if oc < 8 else
                        bufB[:, O_H2b + (oc - 8) * 512:
                             O_H2b + (oc - 8) * 512 + w5],
                        pb[4][:, :w5], AF.Identity,
                        bias=biases[:, B_BSU + oc:B_BSU + oc + 1]),
                        waits=[t_mmu, t_b])
                    bank_rd[4] = t_su

                def u_sa(s, oc8, w5=w5):
                    if s == 0 and oc8 == 0:
                        s10["t_sua"] = p.tok("act")
                    base = O_H1b if s == 0 else O_H2b
                    sw, t_w = wload(I["wsa1"][:, oc8 * E1:(oc8 + 1) * E1], E1)
                    wv_ = ws[sw][:, :E1].rearrange("p (c o) -> p c o", o=128)
                    for fcf in range(8):
                        mm([t_w, s10["t_sua"]] + ([bank_rd[5]] if fcf == 0
                                                  else []),
                           pb[5][:, :w5], wv_[:, fcf, :],
                           bufB[:, base + fcf * 512: base + fcf * 512 + w5],
                           start=(fcf == 0), stop=(fcf == 7))
                    t_mmw = p.tok("pe")
                    wfree(sw, t_mmw)
                    t_sl = act(lambda E, s=s, oc8=oc8, base=base: E.activation(
                        bufB[:, (s * 8 + oc8) * 512: (s * 8 + oc8) * 512 + w5],
                        pb[5][:, :w5], AF.Silu,
                        bias=biases[:, B_BSA + oc8:B_BSA + oc8 + 1]),
                        waits=[t_mmw, t_b, p.tok("dve")])
                    bank_rd[5] = t_sl
                    dve(lambda E, s=s, oc8=oc8, base=base: E.tensor_mul(
                        bufB[:, (s * 8 + oc8) * 512: (s * 8 + oc8) * 512 + w5],
                        bufB[:, (s * 8 + oc8) * 512: (s * 8 + oc8) * 512 + w5],
                        bufB[:, base + oc8 * 512: base + oc8 * 512 + w5]),
                        waits=[t_sl])

                def u_sd(oc, w5=w5):
                    if oc == 0:
                        s10["t_hsh"] = p.tok("dve")
                    sd, t_d = wload(I["wsd1"][:, oc * D:(oc + 1) * D], D)
                    wdv = ws[sd][:, :D].rearrange("p (c o) -> p c o", o=128)
                    for fcf in range(FC):
                        mm([t_d, s10["t_hsh"]] + ([bank_rd[6]] if fcf == 0
                                                  else []),
                           pb[6][:, :w5], wdv[:, fcf, :],
                           bufB[:, fcf * 512: fcf * 512 + w5],
                           start=(fcf == 0), stop=False)
                    mm([t_bdn, s10["t_w6"]], pb[6][:, :w5],
                       bdn1[:NE1, oc * 128:(oc + 1) * 128], w6[:NE1, :w5],
                       start=False, stop=True)
                    t_dn = p.tok("pe")
                    wfree(sd, t_dn)
                    t_ac = dve(lambda E, oc=oc: E.tensor_add(
                        xch[:, oc * w5:(oc + 1) * w5],
                        xch[:, oc * w5:(oc + 1) * w5], pb[6][:, :w5]),
                        waits=[t_dn])
                    bank_rd[6] = t_ac

                for oc in range(FC):
                    units.append(lambda oc=oc, f=u_su: f(oc))
                for s in range(2):
                    for oc8 in range(8):
                        units.append(lambda s=s, oc8=oc8, f=u_sa: f(s, oc8))
                for oc in range(FC):
                    units.append(lambda oc=oc, f=u_sd: f(oc))

                def u_bbar():
                    s10["bb"] = [p.tok("pe"), p.tok("act"), p.tok("dve")]
                units.append(u_bbar)

                def u_rend(fc4, a5=a5, w5=w5):
                    for fc in range(fc4 * 4, fc4 * 4 + 4):
                        t_n1r = dve(lambda E, fc=fc: E.scalar_tensor_tensor(
                            stage[:, 512:512 + w5],
                            xch[:, fc * w5:(fc + 1) * w5],
                            biases[:, B_BSD + fc:B_BSD + fc + 1],
                            bufA[:, fc * w5:(fc + 1) * w5], ALU.add, ALU.add),
                            waits=[s10["t_res"], t_b, p.tok("sp")])
                        sy(lambda E, fc=fc: E.dma_start(
                            out=xtv[:, fc, a5:a5 + w5],
                            in_=stage[:, 512:512 + w5]),
                            waits=[t_n1r, p.tok("od")], inc="sp")
                    if fc4 == 3:
                        s10["res_rd"] = [p.tok("dve")]
                        s10["round_done"] = p.tok("sp")
                for fc4 in range(4):
                    units.append(lambda fc4=fc4, f=u_rend: f(fc4))
            if rnd == 0:
                def u_dbg0():
                    dbg_out("xt0", xt_d.ap(), [s10["round_done"]])
                    dbg_out("wfm10", wfm1_d.ap(), [s10["t_w1st"]])
                units.append(u_dbg0)
        return units

    s10_units = s10_build()

    nf8 = len(s10_units)
    np8 = len(iters8)
    ctx8 = phaseA(0)
    fi8 = 0
    for i8 in range(np8):
        nxt = phaseA(i8 + 1) if i8 + 1 < np8 else None
        phaseB(ctx8)
        ctx8 = nxt
        want = (i8 + 1) * nf8 // np8
        while fi8 < want:
            s10_units[fi8]()
            fi8 += 1
    while fi8 < nf8:
        s10_units[fi8]()
        fi8 += 1
    t_att = p.tok("sp")
    dbg_out("att", att_own.ap(), [t_att])

    t_cc2 = p.op("gpsimd", lambda E: E.collective_compute(
        "AllGather", ALU.bypass, replica_groups=groups,
        ins=[att_own.ap().opt()], outs=[att_full.ap().opt()]),
        waits=[t_att], inc="cc", selfwait=False)
    t_ffn = p.tok("sp")
    dbg_out("xt", xt_d.ap(), [t_ffn])
    dbg_out("wfm1", wfm1_d.ap(), [t_ffn])

    for tb in range(TB):
        tt0 = tb * 128
        for fc in range(FC):
            gh, hl = fc // 8, fc % 8
            t_x1l = sy(lambda E, fc=fc, tt0=tt0: E.dma_start(
                out=stgf[:, :128], in_=x1v[:, fc, tt0:tt0 + 128]),
                waits=[t_x1] + engtoks())
            t_xtl = sy(lambda E, fc=fc, tt0=tt0: E.dma_start(
                out=stage[:, :128], in_=xtv[:, fc, tt0:tt0 + 128]),
                waits=[t_ffn] + engtoks(), inc="xd2")
            t_aL = sy(lambda E, gh=gh, hl=hl, tt0=tt0: E.dma_start(
                out=stage[:, 128:256],
                in_=att_full.ap()[gh, hl][:, tt0:tt0 + 128]),
                waits=[t_cc2] + engtoks(), inc="od")
            t_aH = sy(lambda E, gh=gh, hl=hl, tt0=tt0: E.dma_start(
                out=stage[:, 256:384],
                in_=att_full.ap()[gh, hl][:, TO + tt0:TO + tt0 + 128]),
                waits=[t_cc2] + engtoks(), inc="yd")
            t_s1 = dve(lambda E: E.tensor_scalar(tmp[0][:, :128], stage[:, 128:256],
                                                 hsel[:, 0:1], None, ALU.mult),
                       waits=[t_aL, t_hsel])
            t_s2 = dve(lambda E: E.scalar_tensor_tensor(
                tmp[0][:, :128], stage[:, 256:384], hsel[:, 1:2], tmp[0][:, :128],
                ALU.mult, ALU.add), waits=[t_aH, t_s1])
            t_s3 = dve(lambda E: E.tensor_add(tmp[1][:, :128], stgf[:, :128],
                                              stage[:, :128]),
                       waits=[t_x1l, t_xtl])
            t_s4 = dve(lambda E: E.tensor_add(stgf[:, 512:640], tmp[0][:, :128],
                                              tmp[1][:, :128]),
                       waits=[t_s2, t_s3, p.tok("pe")])
            bi = 2 + (fc % 2)
            t_tr3 = tr([t_s4, t_idf, bank_rd[bi]], pb[bi][:, :128],
                       stgf[:, 512:640], identf[:, :])
            t_ev3 = dve(lambda E, fc=fc, bi=bi: E.tensor_copy(
                bufC[:, fc * 128:(fc + 1) * 128], pb[bi][:, :128]),
                waits=[t_tr3, p.tok("yd")])
            bank_rd[bi] = t_ev3
        t_row = p.tok("dve")
        sy(lambda E, tt0=tt0: E.dma_start(out=y[tt0:tt0 + 128, :],
                                          in_=bufC[:, :2048]),
           waits=[t_row], inc="yd")
    p.raw("sync", lambda E, h=sems["yd"], v=p.val["yd"]: E.wait_ge(h, v))
    if p.val["od"] > 0:
        p.raw("sync", lambda E, h=sems["od"], v=p.val["od"]: E.wait_ge(h, v))
    p.raw("sync", lambda E, h=sems["sp"], v=p.val["sp"]: E.wait_ge(h, v))

    with nc.Block() as block:
        @block.tensor
        def _(E):
            for fn in p.streams["tensor"]:
                fn(E)

        @block.vector
        def _(E):
            for fn in p.streams["vector"]:
                fn(E)

        @block.scalar
        def _(E):
            for fn in p.streams["scalar"]:
                fn(E)

        @block.gpsimd
        def _(E):
            for fn in p.streams["gpsimd"]:
                fn(E)

        @block.sync
        def _(E):
            for fn in p.streams["sync"]:
                fn(E)

    return nc, stackx, dbg_aps


def kernel(**inputs):
    TO = 1024
    ncores = 8
    nc, stackx, _ = build_program(TO, ncores)
    maps = host_prepare(inputs, TO, ncores)
    res = run_bass_kernel_spmd(nc, maps, core_ids=list(range(ncores)))
    out = np.concatenate([np.asarray(r["y"]).reshape(TO, D) for r in res.results], 0)
    return out.reshape(4, 2048, D)


# version 15
# speedup vs baseline: 1.1682x; 1.1064x over previous
import math
import contextlib
import numpy as np
import ml_dtypes
import concourse.bass as bass
import concourse.mybir as mybir
from concourse.bass_utils import run_bass_kernel_spmd

F32 = mybir.dt.float32
BF16 = mybir.dt.bfloat16
AF = mybir.ActivationFunctionType
ALU = mybir.AluOpType
AX = mybir.AxisListType

D = 2048
FC = 16
HQ, HKV, HD = 16, 8, 128
NE0, F0 = 8, 1024
NE1, E1 = 6, 1024
H1 = 8
TWO_PI = 6.283185307179586
LN10K = math.log(10000.0)
PI32 = float(np.float32(np.pi))
_C1 = float(np.float32(np.ldexp(np.round(np.ldexp(TWO_PI, 9)), -9)))
_C2 = float(np.float32(np.ldexp(np.round(np.ldexp(TWO_PI - _C1, 21)), -21)))
_C3 = float(np.float32(TWO_PI - _C1 - _C2))


def _oc_major(w):
    i, o = w.shape
    return np.ascontiguousarray(
        w.reshape(i // 128, 128, o // 128, 128).transpose(1, 2, 0, 3).reshape(128, -1))


def _fc_major(w):
    i, o = w.shape
    return np.ascontiguousarray(
        w.reshape(i // 128, 128, o).transpose(1, 0, 2).reshape(128, -1))


def _bias_cols(b):
    o = b.shape[0]
    return np.ascontiguousarray(b.reshape(o // 128, 128).T)


def host_prepare(inputs, TO, ncores):
    f32 = lambda k: np.asarray(inputs[k], np.float32)
    x = f32("L0_attn_hidden_states").reshape(-1, D)
    sh = {}
    sh["wq0"] = _oc_major(f32("L0_attn_Wq").T)
    sh["wk0"] = _oc_major(f32("L0_attn_Wk").T)
    sh["wv0"] = _oc_major(f32("L0_attn_Wv").T)
    sh["wo0"] = _oc_major(f32("L0_attn_Wo").T)
    sh["biases"] = np.ascontiguousarray(np.concatenate(
        [_bias_cols(f32("L0_attn_bq")), _bias_cols(f32("L0_attn_bk")),
         _bias_cols(f32("L0_attn_bv")), _bias_cols(f32("L0_attn_bo")),
         _bias_cols(f32("L0_ffn_rms_weight")), _bias_cols(f32("L1_ffn_bsu")),
         _bias_cols(f32("L1_ffn_bsa")), _bias_cols(f32("L1_ffn_bsd"))], 1))
    sh["rt0"] = np.ascontiguousarray(f32("L0_attn_rand_matrix").T)
    sh["rw0"] = _fc_major(f32("L0_ffn_router_w"))
    for e in range(NE0):
        sh[f"wg0_{e}"] = _oc_major(f32("L0_ffn_w_gate")[e])
        sh[f"wu0_{e}"] = _oc_major(f32("L0_ffn_w_up")[e])
        sh[f"wd0_{e}"] = _oc_major(f32("L0_ffn_w_down")[e])
    sh["swg"] = _oc_major(f32("L0_ffn_sw_gate"))
    sh["swu"] = _oc_major(f32("L0_ffn_sw_up"))
    sh["swd"] = _oc_major(f32("L0_ffn_sw_down"))
    sh["taunu"] = np.array([[float(f32("L1_attn_tau").reshape(-1)[0]),
                             float(f32("L1_attn_nu").reshape(-1)[0])]], np.float32)
    for j in range(2):
        sh[f"wgate1_{j}"] = _fc_major(f32("L1_ffn_Wg")[j])
        sh[f"bg1_{j}"] = np.ascontiguousarray(f32("L1_ffn_bg")[j].reshape(8, 1))
    for n in range(NE1):
        sh[f"wup1_{n}"] = _oc_major(f32("L1_ffn_Wup")[n])
        sh[f"wsw1_{n}"] = _oc_major(f32("L1_ffn_Wsw")[n])
        sh[f"wdn1_{n}"] = _oc_major(f32("L1_ffn_Wdn")[n])
    sh["bup1"] = np.ascontiguousarray(
        np.concatenate([_bias_cols(f32("L1_ffn_bup")[n]) for n in range(NE1)], 1))
    sh["bsw1"] = np.ascontiguousarray(
        np.concatenate([_bias_cols(f32("L1_ffn_bsw")[n]) for n in range(NE1)], 1))
    sh["bdn1"] = np.ascontiguousarray(f32("L1_ffn_bdn"))
    sh["wsu1"] = _oc_major(f32("L1_ffn_Wsu"))
    sh["wsa1"] = _oc_major(f32("L1_ffn_Wsa"))
    sh["wsd1"] = _oc_major(f32("L1_ffn_Wsd"))
    wq1, wk1, wv1 = f32("L1_attn_wq").T, f32("L1_attn_wk").T, f32("L1_attn_wv").T
    maps = []
    for c in range(ncores):
        xo = x[c * TO:(c + 1) * TO]
        m = dict(sh)
        m["x_fm"] = np.ascontiguousarray(
            xo.T.reshape(FC, 128, TO).transpose(1, 0, 2).reshape(128, FC * TO))
        pos0 = (c % 2) * TO
        m["posrow"] = np.arange(pos0, pos0 + TO, dtype=np.float32).reshape(1, TO)
        hf = c % 2
        m["halfsel"] = np.array([[1.0 - hf, float(hf)]], np.float32)
        hs = slice(hf * H1 * HD, (hf + 1) * H1 * HD)
        m["wq1"] = _oc_major(wq1[:, hs])
        m["wk1"] = _oc_major(wk1[:, hs])
        m["wv1"] = _oc_major(wv1[:, hs])
        maps.append(m)
    return maps


ENGS = ("tensor", "vector", "scalar", "gpsimd", "sync")


class P:
    def __init__(self, nc, sems):
        self.nc = nc
        self.sems = sems
        self.streams = {e: [] for e in ENGS}
        self.val = {n: 0 for n in sems}
        self.wm = {e: {n: 0 for n in sems} for e in ENGS}
        self.own = {"tensor": "pe", "vector": "dve", "scalar": "act",
                    "gpsimd": "gpc", "sync": None}

    def raw(self, eng, fn):
        self.streams[eng].append(fn)

    def wait(self, eng, tok):
        if tok is None:
            return
        sem, v = tok
        if v <= self.wm[eng][sem]:
            return
        self.wm[eng][sem] = v
        self.raw(eng, lambda E, h=self.sems[sem], v=v: E.wait_ge(h, v))

    def op(self, eng, fn, waits=(), inc=None, n=1, selfwait=True, ring=False):
        own = self.own[eng]
        if selfwait and own is not None and self.val[own] > 0:
            self.wait(eng, (own, self.val[own]))
        if inc is None:
            inc = own
        if ring and inc is not False and self.val[inc] > 0:
            self.wait(eng, (inc, self.val[inc]))
        for t in waits:
            self.wait(eng, t)
        if inc is False:
            self.raw(eng, fn)
            return None
        self.val[inc] += n
        self.raw(eng, lambda E, fn=fn, h=self.sems[inc], n=n: fn(E).then_inc(h, n))
        return (inc, self.val[inc])

    def tok(self, sem):
        return (sem, self.val[sem])


def cdiv(a, b):
    return (a + b - 1) // b


def build_program(TO, ncores, dbg=()):
    SEQ = 2 * TO
    TB = TO // 128
    SB = SEQ // 128
    NT5 = cdiv(TO, 512)
    TCH = min(256, TO)
    NCH = TO // TCH
    nc = bass.Bass("TRN2", target_bir_lowering=False, debug=False)

    def din(name, shape, dtype=F32):
        return nc.dram_tensor(name, list(shape), dtype, kind="ExternalInput").ap()

    I = {"x_fm": din("x_fm", (128, FC * TO)), "posrow": din("posrow", (1, TO)),
         "biases": din("biases", (128, 104)), "halfsel": din("halfsel", (1, 2))}
    B_BQ, B_BK, B_BV, B_BO, B_RMS, B_BSU, B_BSA, B_BSD = 0, 16, 24, 32, 48, 64, 80, 88
    for nm, shp in [("wq0", (128, 16 * D)), ("wk0", (128, 8 * D)), ("wv0", (128, 8 * D)),
                    ("wo0", (128, 16 * D)), ("rt0", (128, 64)), ("rw0", (128, FC * 8)),
                    ("swg", (128, 8 * D)), ("swu", (128, 8 * D)), ("swd", (128, 16 * F0)),
                    ("taunu", (1, 2)), ("wq1", (128, 8 * D)), ("wk1", (128, 8 * D)),
                    ("wv1", (128, 8 * D)), ("wsu1", (128, 16 * D)),
                    ("wsa1", (128, 8 * E1)), ("wsd1", (128, 16 * D)),
                    ("bup1", (128, NE1 * 8)), ("bsw1", (128, NE1 * 8)),
                    ("bdn1", (NE1, D))]:
        I[nm] = din(nm, shp)
    for e in range(NE0):
        I[f"wg0_{e}"] = din(f"wg0_{e}", (128, 8 * D))
        I[f"wu0_{e}"] = din(f"wu0_{e}", (128, 8 * D))
        I[f"wd0_{e}"] = din(f"wd0_{e}", (128, 16 * F0))
    for j in range(2):
        I[f"wgate1_{j}"] = din(f"wgate1_{j}", (128, FC * 8))
        I[f"bg1_{j}"] = din(f"bg1_{j}", (8, 1))
    for n in range(NE1):
        I[f"wup1_{n}"] = din(f"wup1_{n}", (128, 8 * D))
        I[f"wsw1_{n}"] = din(f"wsw1_{n}", (128, 8 * E1))
        I[f"wdn1_{n}"] = din(f"wdn1_{n}", (128, 16 * E1))

    y = nc.dram_tensor("y", [TO, D], F32, kind="ExternalOutput").ap()

    def dt_(name, shape, dtype):
        return nc.dram_tensor(name, list(shape), dtype)

    xn0_d = dt_("xn0_d", (128, FC * TO), BF16)
    hbf_d = dt_("hbf_d", (128, FC * TO), BF16)
    a0_d = dt_("a0_d", (128, FC * TO), BF16)
    x1_d = dt_("x1_d", (128, FC * TO), F32)
    xt_d = dt_("xt_d", (128, FC * TO), BF16)
    moe_d = dt_("moe_d", (128, FC * TO), F32)
    wfm_d = dt_("wfm_d", (8, TO), F32)
    wfm1_d = dt_("wfm1_d", (8, TO), F32)
    xn1_own = dt_("xn1_own", (128, FC * TO), BF16)
    xn1_full = dt_("xn1_full", (2, 128, FC * TO), BF16)
    q1_d = dt_("q1_d", (H1, 128, SEQ), BF16)
    k1_d = dt_("k1_d", (H1, 128, SEQ), BF16)
    v1_d = dt_("v1_d", (SB, 128, H1 * HD), BF16)
    mixo_d = dt_("mixo_d", (128, HQ * TO), BF16)
    mixo_full = dt_("mixo_full", (2, 128, HQ * TO), BF16)
    att_own = dt_("att_own", (H1, 128, SEQ), BF16)
    att_full = dt_("att_full", (2, H1, 128, SEQ), BF16)

    SBT = {}
    def sbuf(name, shape, dtype=F32):
        SBT[name] = nc.alloc_sbuf_tensor(name, list(shape), dtype)
        return SBT[name]

    costab = sbuf("costab", (128, TO))
    sintab = sbuf("sintab", (128, TO))
    ident = sbuf("ident", (128, 128), BF16)
    identf = sbuf("identf", (128, 128), F32)
    onesbf = sbuf("onesbf", (128, 128), BF16)
    onesdiv = sbuf("onesdiv", (128, 128), BF16)
    onesfr = sbuf("onesfr", (1, 128))
    prot = sbuf("prot", (128, 128), BF16)
    rt0 = sbuf("s_rt0", (128, 64), BF16)
    rw0 = sbuf("s_rw0", (128, FC * 8), BF16)
    biases = sbuf("s_biases", (128, 104))
    bup1 = sbuf("s_bup1", (128, NE1 * 8))
    bsw1 = sbuf("s_bsw1", (128, NE1 * 8))
    bdn1 = sbuf("s_bdn1", (8, D))
    w6 = sbuf("w6", (8, 512))
    bg1 = sbuf("s_bg1", (8, 2))
    cst = sbuf("cst", (128, 8))
    taunu = sbuf("s_taunu", (128, 2))
    hsel = sbuf("hsel", (128, 2))
    lrow2 = sbuf("lrow2", (1, 512))
    lrow = sbuf("lrow", (1, 512))
    NWS = 5
    ws = [sbuf(f"ws{i}", (128, 2048), BF16) for i in range(NWS)]
    bufA = sbuf("bufA", (128, 32768), BF16)
    bufB = sbuf("bufB", (128, 16384), BF16)
    bufC = sbuf("bufC", (128, 2048))
    xch = sbuf("xch", (128, FC * 512))
    stage = sbuf("stage", (128, 1024), BF16)
    tmp = [sbuf(f"tmp{i}", (128, 512)) for i in range(4)]
    tmpi = sbuf("tmpi", (128, 512), mybir.dt.int32)
    cmask = sbuf("cmask", (128, 128), BF16)
    stgf = sbuf("stgf", (128, 1024))
    vhp = sbuf("vhp", (8, 32 * 128), BF16)

    pb = [nc.alloc_psum_tensor(f"pb{i}", [128, 512], F32) for i in range(7)]
    pT = nc.alloc_psum_tensor("pT", [128, 1024], BF16)

    stackx = contextlib.ExitStack()
    NWSEM = NWS
    semnames = (["pe", "dve", "act", "gpc", "cc", "xd", "xd2", "sp", "od", "yd", "gd"] +
                [f"wd{i}" for i in range(NWSEM)])
    sems = {n: stackx.enter_context(nc.semaphore(n)) for n in semnames}
    p = P(nc, sems)

    def mm(waits, out_ap, lhsT_ap, rhs_ap, start=True, stop=True, tp=None):
        return p.op("tensor",
                    lambda E: E.matmul(out_ap, lhsT_ap, rhs_ap, start=start, stop=stop,
                                       tile_position=tp),
                    waits=waits, inc=("pe" if stop else False), selfwait=False)

    def tr(waits, out_ap, in_ap, idap):
        return p.op("tensor", lambda E: E.transpose(out_ap, in_ap, idap),
                    waits=waits, inc="pe", selfwait=False)

    def dve(fn, waits=()):
        return p.op("vector", fn, waits=waits)

    def act(fn, waits=()):
        return p.op("scalar", fn, waits=waits)

    def gp(fn, waits=()):
        return p.op("gpsimd", fn, waits=waits)

    def sy(fn, waits=(), inc="xd"):
        return p.op("sync", fn, waits=waits, inc=inc, n=16, selfwait=False, ring=True)

    def gdma(fn, waits=(), inc="gd"):
        return p.op("gpsimd", fn, waits=waits, inc=inc, n=16, selfwait=False, ring=True)

    wst = {"free": [None] * NWS, "seq": 0}

    def wload(src_ap, cols):
        i = wst["seq"] % NWS
        wst["seq"] += 1
        t = p.op("gpsimd",
                 lambda E, i=i, cols=cols, src_ap=src_ap:
                 E.dma_start(out=ws[i][:, :cols], in_=src_ap),
                 waits=([wst["free"][i]] if wst["free"][i] else []),
                 inc=f"wd{i}", n=16, selfwait=False, ring=True)
        return i, t

    def wfree(i, tok):
        wst["free"][i] = tok

    bank_rd = [None] * 7
    pT_rd = [None]

    def engtoks():
        return [p.tok("pe"), p.tok("dve"), p.tok("act")]

    dbg_aps = {}
    def dbg_out(name, src_ap, waits):
        if name in dbg:
            ap = nc.dram_tensor("dbg_" + name, list(src_ap.shape), src_ap.dtype,
                                kind="ExternalOutput").ap()
            dbg_aps[name] = ap
            sy(lambda E: E.dma_start(out=ap, in_=src_ap), waits=waits, inc="od")

    t_b = sy(lambda E: E.dma_start(out=biases[:, :], in_=I["biases"]))
    t_tn0 = sy(lambda E: E.dma_start(out=taunu[:1, :2], in_=I["taunu"]))
    t_hs0 = sy(lambda E: E.dma_start(out=hsel[:1, :2], in_=I["halfsel"]))
    t_bup = sy(lambda E: E.dma_start(out=bup1[:, :], in_=I["bup1"]))
    t_bsw = sy(lambda E: E.dma_start(out=bsw1[:, :], in_=I["bsw1"]))
    t_bg0 = sy(lambda E: E.dma_start(out=bg1[:8, 0:1], in_=I["bg1_0"]))
    t_bg1 = sy(lambda E: E.dma_start(out=bg1[:8, 1:2], in_=I["bg1_1"]))
    t_bdn = sy(lambda E: E.dma_start(out=bdn1[:NE1, :], in_=I["bdn1"]), inc="xd2")
    t_rt = gdma(lambda E: E.dma_start(out=rt0[:, :], in_=I["rt0"]))
    t_rw = gdma(lambda E: E.dma_start(out=rw0[:, :], in_=I["rw0"]))
    t_ones = dve(lambda E: E.memset(onesbf[:, :], 1.0))
    t_odiv = dve(lambda E: E.memset(onesdiv[:, :], 1.0 / 2048.0))
    t_ofr = dve(lambda E: E.memset(onesfr[:, :], 1.0))
    t_c = dve(lambda E: E.memset(cst[:, :], 0.0))
    t_c0 = dve(lambda E: E.memset(cst[:, 0:1], 1e-5), waits=[t_c])
    t_c2 = dve(lambda E: E.memset(cst[:, 2:3], 1e-6), waits=[t_c])
    t_of = dve(lambda E: E.memset(stgf[:, :128], 1.0))
    t_n1 = dve(lambda E: E.memset(stgf[:, 128:256], -1.0))
    t_id = gp(lambda E: E.affine_select(ident[:, :], onesbf[:, :], pattern=[[1, 128]],
                                        compare_op=ALU.is_equal, fill=0.0, base=0,
                                        channel_multiplier=-1), waits=[t_ones])
    t_idf = gp(lambda E: E.affine_select(identf[:, :], stgf[:, :128], pattern=[[1, 128]],
                                         compare_op=ALU.is_equal, fill=0.0, base=0,
                                         channel_multiplier=-1), waits=[t_of])
    t_cmi = gp(lambda E: E.iota(stgf[:, 256:384], pattern=[[1, 128]], base=0,
                                channel_multiplier=-1,
                                allow_small_or_imprecise_dtypes=True))
    t_cm = dve(lambda E: E.tensor_scalar(cmask[:, :], stgf[:, 256:384], 0.5, None,
                                         ALU.is_lt), waits=[t_cmi])
    t_pa = gp(lambda E: E.affine_select(prot[:, :], onesbf[:, :], pattern=[[1, 128]],
                                        compare_op=ALU.is_equal, fill=0.0, base=-64,
                                        channel_multiplier=-1), waits=[t_ones])
    t_pb = gp(lambda E: E.affine_select(stage[:, :128], stgf[:, 128:256],
                                        pattern=[[1, 128]], compare_op=ALU.is_equal,
                                        fill=0.0, base=64, channel_multiplier=-1),
              waits=[t_n1])
    t_prot = dve(lambda E: E.tensor_add(prot[:, :], prot[:, :], stage[:, :128]),
                 waits=[t_pa, t_pb])
    t_tnm = mm([t_tn0, t_ofr, bank_rd[6]], pb[6][:, :2], onesfr[:1, :],
               taunu[:1, :2])
    t_tn = dve(lambda E: E.tensor_copy(taunu[:, :2], pb[6][:, :2]), waits=[t_tnm])
    t_hsm = mm([t_hs0, t_ofr, t_tn], pb[6][:, 2:4], onesfr[:1, :], hsel[:1, :2])
    t_hsel = dve(lambda E: E.tensor_copy(hsel[:, :2], pb[6][:, 2:4]), waits=[t_hsm])
    bank_rd[6] = t_hsel

    xn0v = xn0_d.ap().rearrange("p (c t) -> p c t", t=TO)
    hbfv = hbf_d.ap().rearrange("p (c t) -> p c t", t=TO)
    a0v = a0_d.ap().rearrange("p (c t) -> p c t", t=TO)
    x1v = x1_d.ap().rearrange("p (c t) -> p c t", t=TO)
    xfmv = I["x_fm"].rearrange("p (c t) -> p c t", t=TO)
    xn1v = xn1_own.ap().rearrange("p (c t) -> p c t", t=TO)
    xtv = xt_d.ap().rearrange("p (c t) -> p c t", t=TO)

    def ln_chunk(src_view, t0, w5, out_view, wait_src):
        t_xc = sy(lambda E: E.dma_start(
            out=xch[:, :FC * w5].rearrange("p (c t) -> p c t", t=w5),
            in_=src_view[:, :, t0:t0 + w5]),
            waits=([wait_src] if wait_src else []) + engtoks())
        xv = xch[:, :FC * w5].rearrange("p (c t) -> p c t", t=w5)
        cps, sqs = [], []
        for fc in range(FC):
            cps.append(dve(lambda E, fc=fc: E.tensor_copy(
                bufA[:, fc * w5:(fc + 1) * w5], xv[:, fc, :]), waits=[t_xc]))
            sqs.append(act(lambda E, fc=fc: E.activation(
                bufB[:, fc * w5:(fc + 1) * w5], xv[:, fc, :], AF.Square),
                waits=[t_xc]))
        for fc in range(FC):
            mm([cps[fc], t_odiv] + ([bank_rd[0]] if fc == 0 else []),
               pb[0][:, :w5], onesdiv[:, :], bufA[:, fc * w5:(fc + 1) * w5],
               start=(fc == 0), stop=(fc == FC - 1))
            mm([sqs[fc]] + ([bank_rd[1]] if fc == 0 else []),
               pb[1][:, :w5], onesdiv[:, :], bufB[:, fc * w5:(fc + 1) * w5],
               start=(fc == 0), stop=(fc == FC - 1))
        t_mm = p.tok("pe")
        t_m2 = act(lambda E: E.activation(stgf[:, :w5], pb[0][:, :w5], AF.Square),
                   waits=[t_mm])
        t_var = dve(lambda E: E.tensor_sub(tmp[0][:, :w5], pb[1][:, :w5],
                                           stgf[:, :w5]), waits=[t_mm, t_m2])
        bank_rd[1] = t_var
        t_ln = act(lambda E: E.activation(tmp[1][:, :w5], tmp[0][:, :w5], AF.Ln,
                                          bias=cst[:, 0:1]), waits=[t_var, t_c0])
        t_rstd = act(lambda E: E.activation(tmp[1][:, :w5], tmp[1][:, :w5], AF.Exp,
                                            scale=-0.5), waits=[t_ln])
        t_xnl = None
        for fc in range(FC):
            t_d = dve(lambda E, fc=fc: E.tensor_sub(tmp[2][:, :w5], xv[:, fc, :],
                                                    pb[0][:, :w5]),
                      waits=[t_mm, t_rstd, t_xc])
            t_xnl = dve(lambda E, fc=fc: E.tensor_mul(
                bufA[:, fc * w5:(fc + 1) * w5], tmp[2][:, :w5], tmp[1][:, :w5]),
                waits=[t_d])
        bank_rd[0] = t_xnl
        t_st = sy(lambda E: E.dma_start(
            out=out_view[:, :, t0:t0 + w5],
            in_=bufA[:, :FC * w5].rearrange("p (c t) -> p c t", t=w5)),
            waits=[t_xnl], inc="sp")
        return t_xnl, t_st

    for th in range(NT5):
        t0 = th * 512
        w5 = min(512, TO - t0)
        t_xnl, _ = ln_chunk(xfmv, t0, w5, xn0v, None)
        sqs = []
        for fc in range(FC):
            sqs.append(act(lambda E, fc=fc: E.activation(
                bufB[:, fc * w5:(fc + 1) * w5], bufA[:, fc * w5:(fc + 1) * w5],
                AF.Square), waits=[t_xnl]))
        for fc in range(FC):
            mm([sqs[fc]] + ([bank_rd[2]] if fc == 0 else []),
               pb[2][:, :w5], onesdiv[:, :], bufB[:, fc * w5:(fc + 1) * w5],
               start=(fc == 0), stop=(fc == FC - 1))
        t_mm2 = p.tok("pe")
        t_l2 = act(lambda E: E.activation(tmp[0][:, :w5], pb[2][:, :w5], AF.Ln,
                                          bias=cst[:, 2:3]), waits=[t_mm2, t_c2])
        t_r2 = act(lambda E: E.activation(tmp[0][:, :w5], tmp[0][:, :w5], AF.Exp,
                                          scale=-0.5), waits=[t_l2])
        bank_rd[2] = t_r2
        t_hl = None
        for fc in range(FC):
            t_hl = dve(lambda E, fc=fc: E.scalar_tensor_tensor(
                bufB[:, fc * w5:(fc + 1) * w5], bufA[:, fc * w5:(fc + 1) * w5],
                biases[:, B_RMS + fc:B_RMS + fc + 1], tmp[0][:, :w5],
                ALU.mult, ALU.mult), waits=[t_r2, t_b])
        sy(lambda E, t0=t0, w5=w5: E.dma_start(
            out=hbfv[:, :, t0:t0 + w5],
            in_=bufB[:, :FC * w5].rearrange("p (c t) -> p c t", t=w5)),
            waits=[t_hl], inc="sp")
    t_ln_all = p.tok("sp")
    dbg_out("xn0", xn0_d.ap(), [t_ln_all])
    dbg_out("hbf", hbf_d.ap(), [t_ln_all])

    t_ior = gp(lambda E: E.iota(
        lrow[:1, :128].rearrange("o (a b) -> o a b", b=64), pattern=[[0, 2], [1, 64]],
        base=0, channel_multiplier=0, allow_small_or_imprecise_dtypes=True))
    t_if = act(lambda E: E.activation(lrow[:1, 128:256], lrow[:1, :128], AF.Exp,
                                      scale=-LN10K / 64.0), waits=[t_ior])
    for j5 in range(NT5):
        a = j5 * 512
        w5 = min(512, TO - a)
        t_posc = sy(lambda E, a=a, w5=w5: E.dma_start(out=lrow2[:1, :w5],
                                                      in_=I["posrow"][:, a:a + w5]),
                    waits=[p.tok("pe")])
        t_am = mm([t_if, t_posc, bank_rd[3]], pb[3][:, :w5], lrow[:1, 128:256],
                  lrow2[:1, :w5])
        t_y = dve(lambda E, w5=w5: E.tensor_scalar(tmp[0][:, :w5], pb[3][:, :w5],
                                                   1.0 / TWO_PI, 24.5, ALU.mult,
                                                   ALU.add), waits=[t_am])
        t_ki = dve(lambda E, w5=w5: E.tensor_copy(tmpi[:, :w5], tmp[0][:, :w5]),
                   waits=[t_y])
        t_kf = dve(lambda E, w5=w5: E.tensor_copy(tmp[2][:, :w5], tmpi[:, :w5]),
                   waits=[t_ki])
        t_d = dve(lambda E, w5=w5: E.tensor_sub(tmp[0][:, :w5], tmp[0][:, :w5],
                                                tmp[2][:, :w5]), waits=[t_kf])
        t_g = dve(lambda E, w5=w5: E.tensor_scalar(tmp[0][:, :w5], tmp[0][:, :w5],
                                                   0.0, None, ALU.is_ge),
                  waits=[t_d])
        t_kk = dve(lambda E, w5=w5: E.scalar_tensor_tensor(
            tmp[0][:, :w5], tmp[0][:, :w5], -25.0, tmp[2][:, :w5], ALU.add, ALU.add),
            waits=[t_g])
        t_r1 = dve(lambda E, w5=w5: E.scalar_tensor_tensor(
            tmp[1][:, :w5], tmp[0][:, :w5], -_C1, pb[3][:, :w5], ALU.mult, ALU.add),
            waits=[t_kk, t_am])
        bank_rd[3] = t_r1
        t_r2 = dve(lambda E, w5=w5: E.scalar_tensor_tensor(
            tmp[1][:, :w5], tmp[0][:, :w5], -_C2, tmp[1][:, :w5], ALU.mult, ALU.add),
            waits=[t_r1])
        t_red2 = dve(lambda E, w5=w5: E.scalar_tensor_tensor(
            tmp[1][:, :w5], tmp[0][:, :w5], -_C3, tmp[1][:, :w5], ALU.mult, ALU.add),
            waits=[t_r2])
        act(lambda E, a=a, w5=w5: E.activation(sintab[:, a:a + w5], tmp[1][:, :w5],
                                               AF.Sin), waits=[t_red2])
        t_s2 = act(lambda E, w5=w5: E.activation(tmp[2][:, :w5], tmp[1][:, :w5],
                                                 AF.Sin, scale=0.5), waits=[t_red2])
        t_sq = act(lambda E, w5=w5: E.activation(tmp[2][:, :w5], tmp[2][:, :w5],
                                                 AF.Square), waits=[t_s2])
        dve(lambda E, a=a, w5=w5: E.tensor_scalar(costab[:, a:a + w5], tmp[2][:, :w5],
                                                  -2.0, 1.0, ALU.mult, ALU.add),
            waits=[t_sq])
    t_tabs = [p.tok("dve"), p.tok("act")]
    dbg_out("sintab", sintab[:, :], t_tabs)
    dbg_out("costab", costab[:, :], t_tabs)

    for tb in range(TB):
        tt0 = tb * 128
        t_h = sy(lambda E, tt0=tt0: E.dma_start(
            out=bufB[:, :FC * 128].rearrange("p (c t) -> p c t", t=128),
            in_=hbfv[:, :, tt0:tt0 + 128]), waits=[t_ln_all] + engtoks())
        hv = bufB[:, :FC * 128].rearrange("p (c t) -> p c t", t=128)
        for fc in range(FC):
            mm([t_h, t_rw] + ([bank_rd[4]] if fc == 0 else []), pb[4][:8, :128],
               rw0[:, fc * 8:(fc + 1) * 8], hv[:, fc, :],
               start=(fc == 0), stop=(fc == FC - 1))
        t_lg = p.tok("pe")
        t_lb = dve(lambda E: E.tensor_copy(stgf[:8, :128], pb[4][:8, :128]),
                   waits=[t_lg])
        bank_rd[4] = t_lb
        t_tr = tr([t_lb, t_idf, bank_rd[5]], pb[5][:, :8], stgf[:8, :128],
                  identf[:8, :8])
        t_tm = dve(lambda E: E.tensor_copy(stgf[:, 512:520], pb[5][:, :8]),
                   waits=[t_tr])
        bank_rd[5] = t_tm
        lg = stgf[:, 512:520]
        t_v1 = dve(lambda E: E.reduce_max(stgf[:, 520:521], lg, axis=AX.X),
                   waits=[t_tm])
        t_e1 = dve(lambda E: E.tensor_scalar(tmp[0][:, :8], lg, stgf[:, 520:521],
                                             None, ALU.is_ge), waits=[t_v1])
        t_x2 = dve(lambda E: E.scalar_tensor_tensor(
            tmp[1][:, :8], tmp[0][:, :8], -1e30, lg, ALU.mult, ALU.add),
            waits=[t_e1])
        t_v2 = dve(lambda E: E.reduce_max(stgf[:, 521:522], tmp[1][:, :8], axis=AX.X),
                   waits=[t_x2])
        t_e2 = dve(lambda E: E.tensor_scalar(tmp[1][:, :8], tmp[1][:, :8],
                                             stgf[:, 521:522], None, ALU.is_ge),
                   waits=[t_v2])
        t_dl = dve(lambda E: E.tensor_sub(stgf[:, 522:523], stgf[:, 520:521],
                                          stgf[:, 521:522]), waits=[t_v2])
        t_th = act(lambda E: E.activation(stgf[:, 523:524], stgf[:, 522:523],
                                          AF.Tanh, scale=0.5), waits=[t_dl])
        t_w1 = dve(lambda E: E.tensor_scalar(stgf[:, 523:524], stgf[:, 523:524],
                                             0.5, 0.5, ALU.mult, ALU.add),
                   waits=[t_th])
        t_w2 = dve(lambda E: E.tensor_scalar(stgf[:, 524:525], stgf[:, 523:524],
                                             -1.0, 1.0, ALU.mult, ALU.add),
                   waits=[t_w1])
        t_a1 = dve(lambda E: E.tensor_scalar(tmp[0][:, :8], tmp[0][:, :8],
                                             stgf[:, 523:524], None, ALU.mult),
                   waits=[t_w1, t_e1])
        t_wall = dve(lambda E: E.scalar_tensor_tensor(
            stgf[:, 528:536], tmp[1][:, :8], stgf[:, 524:525], tmp[0][:, :8],
            ALU.mult, ALU.add), waits=[t_w2, t_a1, t_e2])
        t_tw = tr([t_wall, t_idf, bank_rd[6]], pb[6][:8, :128], stgf[:, 528:536],
                  identf[:, :128])
        t_we = dve(lambda E: E.tensor_copy(stgf[:8, 640:768], pb[6][:8, :128]),
                   waits=[t_tw, p.tok("sp")])
        bank_rd[6] = t_we
        sy(lambda E, tt0=tt0: E.dma_start(out=wfm_d.ap()[:8, tt0:tt0 + 128],
                                          in_=stgf[:8, 640:768]),
           waits=[t_we], inc="sp")
    t_wfst = p.tok("sp")
    dbg_out("wfm", wfm_d.ap(), [t_wfst])


    O_QSB, O_QP, O_KP, O_VSB = 0, 24 * TCH, 40 * TCH, 48 * TCH
    O_MIXO, O_INVN, O_XN = 0, 16 * TCH, 40 * TCH
    for ch in range(NCH):
        c0 = ch * TCH
        t_xn = sy(lambda E, c0=c0: E.dma_start(
            out=bufB[:, O_XN:O_XN + FC * TCH].rearrange("p (c t) -> p c t", t=TCH),
            in_=xn0v[:, :, c0:c0 + TCH]), waits=[t_ln_all] + engtoks())
        xnv = bufB[:, O_XN:O_XN + FC * TCH].rearrange("p (c t) -> p c t", t=TCH)
        for pname, wnm, nh, bcol, obase in [
                ("q", "wq0", HQ, B_BQ, O_QSB),
                ("k", "wk0", HKV, B_BK, O_QSB + HQ * TCH),
                ("v", "wv0", HKV, B_BV, O_VSB)]:
            for hh in range(nh):
                si, t_w = wload(I[wnm][:, hh * D:(hh + 1) * D], D)
                wtv = ws[si][:, :D].rearrange("p (c o) -> p c o", o=128)
                bank = pb[hh % 2]
                for fc in range(FC):
                    mm([t_w, t_xn] + ([bank_rd[hh % 2]] if fc == 0 else []),
                       bank[:, :TCH], wtv[:, fc, :], xnv[:, fc, :],
                       start=(fc == 0), stop=(fc == FC - 1))
                t_mm = p.tok("pe")
                wfree(si, t_mm)
                dst = bufA[:, obase + hh * TCH: obase + (hh + 1) * TCH]
                t_ev = act(lambda E, dst=dst, bank=bank, bcol=bcol, hh=hh:
                           E.activation(dst, bank[:, :TCH], AF.Identity,
                                        bias=biases[:, bcol + hh:bcol + hh + 1]),
                           waits=[t_mm, t_b])
                bank_rd[hh % 2] = t_ev
                if pname != "v":
                    hidx = hh if pname == "q" else HQ + hh
                    t_sq = act(lambda E, dst=dst: E.activation(
                        stage[:, :TCH], dst, AF.Square), waits=[t_ev])
                    t_n2 = mm([t_sq, t_ones, bank_rd[2]], pb[2][:, :TCH],
                              onesbf[:, :], stage[:, :TCH])
                    t_mx = dve(lambda E: E.tensor_scalar(
                        tmp[0][:, :TCH], pb[2][:, :TCH], 1e-12, None, ALU.max),
                        waits=[t_n2])
                    bank_rd[2] = t_mx
                    t_l = act(lambda E: E.activation(tmp[0][:, :TCH], tmp[0][:, :TCH],
                                                     AF.Ln), waits=[t_mx])
                    act(lambda E, hidx=hidx: E.activation(
                        bufB[:, O_INVN + hidx * TCH: O_INVN + (hidx + 1) * TCH],
                        tmp[0][:, :TCH], AF.Exp, scale=-0.5), waits=[t_l])
        t_proj = [p.tok("act"), p.tok("dve")]

        for hh in range(HQ + HKV):
            src = bufA[:, O_QSB + hh * TCH: O_QSB + (hh + 1) * TCH]
            t_rot = mm(t_proj + [t_prot, bank_rd[2]], pb[2][:, :TCH], prot[:, :], src)
            t_qc = dve(lambda E, src=src, c0=c0: E.tensor_mul(
                tmp[0][:, :TCH], src, costab[:, c0:c0 + TCH]),
                waits=t_tabs + t_proj)
            t_rs = dve(lambda E, c0=c0: E.tensor_mul(
                tmp[1][:, :TCH], pb[2][:, :TCH], sintab[:, c0:c0 + TCH]),
                waits=[t_rot] + t_tabs)
            bank_rd[2] = t_rs
            t_qr = dve(lambda E: E.tensor_add(stage[:, :TCH], tmp[0][:, :TCH],
                                              tmp[1][:, :TCH]), waits=[t_qc, t_rs])
            t_p0 = mm([t_qr, t_rt, bank_rd[2]], pb[2][:64, :TCH], rt0[:, :],
                      stage[:, :TCH])
            t_ps = dve(lambda E, hh=hh: E.tensor_mul(
                tmp[0][:64, :TCH], pb[2][:64, :TCH],
                bufB[:64, O_INVN + hh * TCH: O_INVN + (hh + 1) * TCH]),
                waits=[t_p0])
            bank_rd[2] = t_ps
            t_y = dve(lambda E: E.tensor_scalar(tmp[1][:64, :TCH], tmp[0][:64, :TCH],
                                                1.0 / TWO_PI, 24.5, ALU.mult, ALU.add),
                      waits=[t_ps])
            t_ki = dve(lambda E: E.tensor_copy(tmpi[:64, :TCH], tmp[1][:64, :TCH]),
                       waits=[t_y])
            t_kf = dve(lambda E: E.tensor_copy(tmp[3][:64, :TCH], tmpi[:64, :TCH]),
                       waits=[t_ki])
            t_d = dve(lambda E: E.tensor_sub(tmp[1][:64, :TCH], tmp[1][:64, :TCH],
                                             tmp[3][:64, :TCH]), waits=[t_kf])
            t_g = dve(lambda E: E.tensor_scalar(tmp[1][:64, :TCH], tmp[1][:64, :TCH],
                                                0.0, None, ALU.is_ge), waits=[t_d])
            t_k = dve(lambda E: E.scalar_tensor_tensor(
                tmp[1][:64, :TCH], tmp[1][:64, :TCH], -25.0, tmp[3][:64, :TCH],
                ALU.add, ALU.add), waits=[t_g])
            t_q1r = dve(lambda E: E.scalar_tensor_tensor(
                tmp[2][:64, :TCH], tmp[1][:64, :TCH], -_C1, tmp[0][:64, :TCH],
                ALU.mult, ALU.add), waits=[t_k])
            t_q2r = dve(lambda E: E.scalar_tensor_tensor(
                tmp[2][:64, :TCH], tmp[1][:64, :TCH], -_C2, tmp[2][:64, :TCH],
                ALU.mult, ALU.add), waits=[t_q1r])
            t_red = dve(lambda E: E.scalar_tensor_tensor(
                tmp[0][:64, :TCH], tmp[1][:64, :TCH], -_C3, tmp[2][:64, :TCH],
                ALU.mult, ALU.add), waits=[t_q2r])
            dq = (bufA[:, O_QP + hh * TCH: O_QP + (hh + 1) * TCH] if hh < HQ else
                  bufA[:, O_KP + (hh - HQ) * TCH: O_KP + (hh - HQ + 1) * TCH])
            act(lambda E, dq=dq: E.activation(dq[:64, :], tmp[0][:64, :TCH], AF.Sin),
                waits=[t_red])
            t_s2 = act(lambda E: E.activation(tmp[3][:64, :TCH], tmp[0][:64, :TCH],
                                              AF.Sin, scale=0.5), waits=[t_red])
            t_sqq = act(lambda E: E.activation(tmp[3][:64, :TCH], tmp[3][:64, :TCH],
                                               AF.Square), waits=[t_s2])
            dve(lambda E, dq=dq: E.tensor_scalar(dq[64:128, :], tmp[3][:64, :TCH],
                                                 -2.0, 1.0, ALU.mult, ALU.add),
                waits=[t_sqq])
        t_feats = [p.tok("dve"), p.tok("act")]

        qpv = bufA[:, O_QP:O_QP + HQ * TCH].rearrange("p (h t) -> p h t", t=TCH)
        kpv = bufA[:, O_KP:O_KP + HKV * TCH].rearrange("p (h t) -> p h t", t=TCH)
        t_mm2_prev = None
        for sb32 in range(TCH // 32):
            tl0 = sb32 * 32
            for i in range(HKV):
                t_t = tr(t_feats + [t_id, pT_rd[0]], pT[:32, :128],
                         bufA[:, O_VSB + i * TCH + tl0: O_VSB + i * TCH + tl0 + 32],
                         ident[:, :])
                t_e = dve(lambda E: E.tensor_scalar(stage[:32, :128], pT[:32, :128],
                                                    2.0 / 128.0, None, ALU.mult),
                          waits=[t_t, p.tok("gd")])
                pT_rd[0] = t_e
                gdma(lambda E, i=i: E.dma_start(out=vhp[i:i + 1, :],
                                                in_=stage[:32, :128]),
                     waits=[t_e] + ([t_mm2_prev] if t_mm2_prev else []))
            t_vhp = p.tok("gd")
            for sl in range(32):
                s = tl0 + sl
                mm(t_feats + ([bank_rd[0]] if sl == 0 else []),
                   pb[0][:8, sl * 16:sl * 16 + 16], kpv[:, :, s], qpv[:, :, s])
            t_mm1 = p.tok("pe")
            t_ae = dve(lambda E: E.tensor_copy(stage[:8, 512:1024],
                                               pb[0][:8, :512]), waits=[t_mm1])
            bank_rd[0] = t_ae
            for sl in range(32):
                mm([t_ae, t_vhp] + ([bank_rd[1]] if sl == 0 else []),
                   pb[1][:, sl * 16:sl * 16 + 16],
                   vhp[:8, sl * 128:(sl + 1) * 128],
                   stage[:8, 512 + sl * 16:512 + sl * 16 + 16])
            t_mm2 = p.tok("pe")
            t_mm2_prev = t_mm2
            t_me = dve(lambda E, tl0=tl0: E.tensor_copy(
                bufB[:, O_MIXO:O_MIXO + HQ * TCH].rearrange(
                    "p (h t) -> p t h", h=HQ)[:, tl0: tl0 + 32, :],
                pb[1][:, :512].rearrange("p (t h) -> p t h", h=16)),
                waits=[t_mm2])
            bank_rd[1] = t_me
        t_mix = [p.tok("dve")]
        sy(lambda E, c0=c0: E.dma_start(
            out=mixo_d.ap().rearrange("p (h t) -> p h t", t=TO)[:, :, c0:c0 + TCH],
            in_=bufB[:, O_MIXO:O_MIXO + HQ * TCH].rearrange("p (h t) -> p h t",
                                                            t=TCH)),
            waits=t_mix, inc="sp")
    t_mixd = p.tok("sp")

    groups = [[2 * i, 2 * i + 1] for i in range(max(1, ncores // 2))]
    t_cc3 = p.op("gpsimd", lambda E: E.collective_compute(
        "AllGather", ALU.bypass, replica_groups=groups,
        ins=[mixo_d.ap().opt()], outs=[mixo_full.ap().opt()]),
        waits=[t_mixd], inc="cc", selfwait=False)

    HBLK = SEQ // 16
    for i in range(TO // HBLK):
        S2T = 2 * TO
        t_lA = sy(lambda E, i=i: E.dma_start(
            out=bufA[:, :S2T].rearrange("p (r t) -> p r t", r=2),
            in_=mixo_full.ap().rearrange("r p (h t) -> r p h t", t=TO)
            [:, :, i, :].rearrange("r p t -> p r t")),
            waits=[t_cc3] + engtoks())
        t_lB = sy(lambda E, i=i: E.dma_start(
            out=bufA[:, S2T:2 * S2T].rearrange("p (r t) -> p r t", r=2),
            in_=mixo_full.ap().rearrange("r p (h t) -> r p h t", t=TO)
            [:, :, 8 + i, :].rearrange("r p t -> p r t")),
            waits=[t_cc3] + engtoks(), inc="xd2")
        t_se = dve(lambda E: E.tensor_scalar(bufA[:, 2 * S2T:3 * S2T],
                                             bufA[:, :S2T],
                                             hsel[:, 0:1], None, ALU.mult),
                   waits=[t_lA, t_hsel])
        t_cb = dve(lambda E: E.scalar_tensor_tensor(
            bufA[:, 2 * S2T:3 * S2T], bufA[:, S2T:2 * S2T], hsel[:, 1:2],
            bufA[:, 2 * S2T:3 * S2T], ALU.mult, ALU.add), waits=[t_lB, t_se])
        t_pm = dve(lambda E, i=i: E.tensor_copy(
            bufB[:, :HQ * TO].rearrange("p (h t) -> p h t", t=TO)
            [:, :, i * HBLK:(i + 1) * HBLK],
            bufA[:, 2 * S2T:3 * S2T].rearrange("p (u h) -> p h u", h=16)),
            waits=[t_cb])
    t_a0in = p.tok("dve")
    for oc in range(FC):
        si, t_w = wload(I["wo0"][:, oc * D:(oc + 1) * D], D)
        wtv = ws[si][:, :D].rearrange("p (c o) -> p c o", o=128)
        for j5 in range(NT5):
            a5 = j5 * 512
            w5 = min(512, TO - a5)
            bank = pb[j5 % 2]
            for fc in range(FC):
                mm([t_w, t_a0in] + ([bank_rd[j5 % 2]] if fc == 0 else []),
                   bank[:, :w5], wtv[:, fc, :],
                   bufB[:, fc * TO + a5: fc * TO + a5 + w5],
                   start=(fc == 0), stop=(fc == FC - 1))
            t_mm = p.tok("pe")
            t_ev = act(lambda E, bank=bank, oc=oc, w5=w5: E.activation(
                stage[:, :w5], bank[:, :w5], AF.Identity,
                bias=biases[:, B_BO + oc:B_BO + oc + 1]),
                waits=[t_mm, t_b, p.tok("sp")])
            bank_rd[j5 % 2] = t_ev
            sy(lambda E, oc=oc, a5=a5, w5=w5: E.dma_start(
                out=a0v[:, oc, a5:a5 + w5], in_=stage[:, :w5]),
                waits=[t_ev], inc="sp")
        wfree(si, p.tok("pe"))
    t_a0 = p.tok("sp")
    dbg_out("a0", a0_d.ap(), [t_a0])

    A_HB, A_HE, A_WBC = 16384, 24576, 28672
    moev = moe_d.ap().rearrange("p (c t) -> p c t", t=TO)
    s4st = {"spill": None, "guctr": 0, "spills": []}

    def mk_group(a5, w5):
        hvv = bufA[:, A_HB:A_HB + FC * w5].rearrange("p (c t) -> p c t", t=w5)
        st = {"t_hb": None, "t_wbc": None, "t_h": None}
        units = []

        def u_load():
            st["t_hb"] = sy(lambda E: E.dma_start(
                out=bufA[:, A_HB:A_HB + FC * w5].rearrange("p (c t) -> p c t",
                                                           t=w5),
                in_=hbfv[:, :, a5:a5 + w5]),
                waits=[t_ln_all, p.tok("pe")], inc="xd2")
        units.append(u_load)

        def u_wbc(e):
            t_rr = sy(lambda E, e=e: E.dma_start(
                out=lrow[:1, :w5], in_=wfm_d.ap()[e:e + 1, a5:a5 + w5]),
                waits=[t_wfst, p.tok("pe")], inc="xd2")
            t_bc = mm([t_rr, t_ofr, bank_rd[5]], pb[5][:, :w5], onesfr[:1, :],
                      lrow[:1, :w5])
            st["t_wbc"] = dve(lambda E: E.tensor_copy(
                bufA[:, A_WBC:A_WBC + w5], pb[5][:, :w5]), waits=[t_bc])
            bank_rd[5] = st["t_wbc"]

        def u_gu(wg_ap, wu_ap, oc8, scaled):
            bg_ = [3, 4][s4st["guctr"] % 2]
            bu_ = [5, 6][s4st["guctr"] % 2]
            s4st["guctr"] += 1
            sg, t_wg = wload(wg_ap, D)
            su, t_wu = wload(wu_ap, D)
            wgv = ws[sg][:, :D].rearrange("p (c o) -> p c o", o=128)
            wuv = ws[su][:, :D].rearrange("p (c o) -> p c o", o=128)
            for fc in range(FC):
                mm([t_wg, st["t_hb"]] + ([bank_rd[bg_]] if fc == 0 else []),
                   pb[bg_][:, :w5], wgv[:, fc, :], hvv[:, fc, :],
                   start=(fc == 0), stop=(fc == FC - 1))
                mm([t_wu] + ([bank_rd[bu_]] if fc == 0 else []),
                   pb[bu_][:, :w5], wuv[:, fc, :], hvv[:, fc, :],
                   start=(fc == 0), stop=(fc == FC - 1))
            t_gu = p.tok("pe")
            wfree(sg, t_gu)
            wfree(su, t_gu)
            t_s = act(lambda E, bg_=bg_: E.activation(bufC[:, :w5],
                                                      pb[bg_][:, :w5], AF.Silu),
                      waits=[t_gu, p.tok("dve")])
            bank_rd[bg_] = t_s
            t_u = act(lambda E, bu_=bu_: E.activation(bufC[:, 512:512 + w5],
                                                      pb[bu_][:, :w5],
                                                      AF.Identity),
                      waits=[t_gu])
            bank_rd[bu_] = t_u
            dst = bufA[:, A_HE + oc8 * 512: A_HE + oc8 * 512 + w5]
            if not scaled:
                st["t_h"] = dve(lambda E, dst=dst: E.tensor_mul(
                    dst, bufC[:, :w5], bufC[:, 512:512 + w5]), waits=[t_s, t_u])
            else:
                t_h0 = dve(lambda E: E.tensor_mul(
                    bufC[:, 1024:1024 + w5], bufC[:, :w5], bufC[:, 512:512 + w5]),
                    waits=[t_s, t_u])
                st["t_h"] = dve(lambda E, dst=dst: E.tensor_mul(
                    dst, bufC[:, 1024:1024 + w5], bufA[:, A_WBC:A_WBC + w5]),
                    waits=[t_h0, st["t_wbc"]])

        def u_down(w_ap, oc, fresh):
            bi = [3, 4][oc % 2]
            sd, t_d = wload(w_ap, F0)
            wdv = ws[sd][:, :F0].rearrange("p (c o) -> p c o", o=128)
            for fcf in range(8):
                mm([t_d, st["t_h"]] + ([bank_rd[bi]] if fcf == 0 else []),
                   pb[bi][:, :w5], wdv[:, fcf, :],
                   bufA[:, A_HE + fcf * 512: A_HE + fcf * 512 + w5],
                   start=(fcf == 0), stop=(fcf == 7))
            t_dn = p.tok("pe")
            wfree(sd, t_dn)
            if fresh:
                t_ac = dve(lambda E, oc=oc, bi=bi: E.tensor_copy(
                    xch[:, oc * w5:(oc + 1) * w5], pb[bi][:, :w5]),
                    waits=[t_dn] + ([s4st["spill"]] if s4st["spill"] else []))
            else:
                t_ac = dve(lambda E, oc=oc, bi=bi: E.tensor_add(
                    xch[:, oc * w5:(oc + 1) * w5],
                    xch[:, oc * w5:(oc + 1) * w5], pb[bi][:, :w5]),
                    waits=[t_dn])
            bank_rd[bi] = t_ac

        for e in range(NE0):
            units.append(lambda e=e: u_wbc(e))
            for oc8 in range(8):
                units.append(lambda e=e, oc8=oc8: u_gu(
                    I[f"wg0_{e}"][:, oc8 * D:(oc8 + 1) * D],
                    I[f"wu0_{e}"][:, oc8 * D:(oc8 + 1) * D], oc8, True))
            for oc in range(FC):
                units.append(lambda e=e, oc=oc: u_down(
                    I[f"wd0_{e}"][:, oc * F0:(oc + 1) * F0], oc, e == 0))
        for oc8 in range(8):
            units.append(lambda oc8=oc8: u_gu(
                I["swg"][:, oc8 * D:(oc8 + 1) * D],
                I["swu"][:, oc8 * D:(oc8 + 1) * D], oc8, False))
        for oc in range(FC):
            units.append(lambda oc=oc: u_down(
                I["swd"][:, oc * F0:(oc + 1) * F0], oc, False))

        def u_spill():
            t_acc = p.tok("dve")
            s4st["spill"] = sy(lambda E: E.dma_start(
                out=moev[:, :, a5:a5 + w5],
                in_=xch[:, :FC * w5].rearrange("p (c t) -> p c t", t=w5)),
                waits=[t_acc], inc="sp")
            s4st["spills"].append(s4st["spill"])
        units.append(u_spill)
        return units

    s4_units = []
    for th5 in range(NT5):
        s4_units += mk_group(th5 * 512, min(512, TO - th5 * 512))
    for u in s4_units:
        u()

    for th5 in range(NT5):
        a5 = th5 * 512
        w5 = min(512, TO - a5)
        for fc in range(FC):
            t_xl = sy(lambda E, fc=fc, a5=a5, w5=w5: E.dma_start(
                out=stgf[:, :w5], in_=xfmv[:, fc, a5:a5 + w5]),
                inc="xd2", waits=[p.tok("dve"), p.tok("act")])
            t_al = sy(lambda E, fc=fc, a5=a5, w5=w5: E.dma_start(
                out=stage[:, :w5], in_=a0v[:, fc, a5:a5 + w5]),
                waits=[t_a0, p.tok("dve")])
            t_ml = sy(lambda E, fc=fc, a5=a5, w5=w5: E.dma_start(
                out=tmp[1][:, :w5], in_=moev[:, fc, a5:a5 + w5]),
                waits=[s4st["spills"][th5], p.tok("dve")])
            t_s1 = dve(lambda E, w5=w5: E.tensor_add(
                tmp[0][:, :w5], stgf[:, :w5], stage[:, :w5]),
                waits=[t_xl, t_al])
            t_s2 = dve(lambda E, w5=w5: E.tensor_add(
                stgf[:, 512:512 + w5], tmp[0][:, :w5], tmp[1][:, :w5]),
                waits=[t_s1, t_ml, p.tok("sp")])
            sy(lambda E, fc=fc, a5=a5, w5=w5: E.dma_start(
                out=x1v[:, fc, a5:a5 + w5], in_=stgf[:, 512:512 + w5]),
                waits=[t_s2], inc="sp")
    t_x1 = p.tok("sp")
    dbg_out("x1", x1_d.ap(), [t_x1])

    for th in range(NT5):
        t0 = th * 512
        w5 = min(512, TO - t0)
        ln_chunk(x1v, t0, w5, xn1v, t_x1)
    t_xn1 = p.tok("sp")
    dbg_out("xn1", xn1_own.ap(), [t_xn1])

    t_cc1 = p.op("gpsimd", lambda E: E.collective_compute(
        "AllGather", ALU.bypass, replica_groups=groups,
        ins=[xn1_own.ap().opt()], outs=[xn1_full.ap().opt()]),
        waits=[t_xn1], inc="cc", selfwait=False)

    t_xf = sy(lambda E: E.dma_start(
        out=bufA[:, :2 * FC * TO].rearrange("p (r n) -> p r n", r=2),
        in_=xn1_full.ap().rearrange("r p n -> p r n")),
        waits=[t_cc1] + engtoks())

    def xn1tile(fc, j5):
        r = (j5 * 512) // TO
        tt = (j5 * 512) % TO
        base = r * FC * TO + fc * TO + tt
        return bufA[:, base: base + 512]

    for wi, (wnm, dnm) in enumerate([("wq1", q1_d), ("wk1", k1_d), ("wv1", v1_d)]):
        for hh in range(H1):
            si, t_w = wload(I[wnm][:, hh * D:(hh + 1) * D], D)
            wtv = ws[si][:, :D].rearrange("p (c o) -> p c o", o=128)
            for j5 in range(SEQ // 512):
                bank = pb[j5 % 2]
                for fc in range(FC):
                    mm([t_w, t_xf] + ([bank_rd[j5 % 2]] if fc == 0 else []),
                       bank[:, :512], wtv[:, fc, :], xn1tile(fc, j5),
                       start=(fc == 0), stop=(fc == FC - 1))
                t_mm = p.tok("pe")
                t_ev = dve(lambda E, hh=hh, j5=j5, bank=bank: E.tensor_copy(
                    bufB[:, hh * SEQ + j5 * 512: hh * SEQ + j5 * 512 + 512],
                    bank[:, :512]), waits=[t_mm, p.tok("sp")])
                bank_rd[j5 % 2] = t_ev
            wfree(si, p.tok("pe"))
        t_evs = p.tok("dve")
        for hh in range(H1):
            qh = bufB[:, hh * SEQ:(hh + 1) * SEQ]
            for j5 in range(SEQ // 512):
                sl = qh[:, j5 * 512:(j5 + 1) * 512]
                t_sq = act(lambda E, sl=sl: E.activation(stage[:, :512], sl,
                                                         AF.Square), waits=[t_evs])
                t_n2 = mm([t_sq, t_ones, bank_rd[2]], pb[2][:, :512], onesbf[:, :],
                          stage[:, :512])
                t_mx = dve(lambda E: E.tensor_scalar(tmp[0][:, :512], pb[2][:, :512],
                                                     1e-38, None, ALU.max),
                           waits=[t_n2])
                bank_rd[2] = t_mx
                t_l = act(lambda E: E.activation(tmp[0][:, :512], tmp[0][:, :512],
                                                 AF.Ln), waits=[t_mx])
                t_n = act(lambda E: E.activation(tmp[0][:, :512], tmp[0][:, :512],
                                                 AF.Exp, scale=0.5), waits=[t_l])
                t_pe2 = dve(lambda E: E.tensor_scalar(tmp[0][:, :512],
                                                      tmp[0][:, :512], 1e-8, None,
                                                      ALU.add), waits=[t_n])
                t_l2 = act(lambda E: E.activation(tmp[0][:, :512], tmp[0][:, :512],
                                                  AF.Ln), waits=[t_pe2])
                t_iv = act(lambda E: E.activation(tmp[0][:, :512], tmp[0][:, :512],
                                                  AF.Exp, scale=-1.0), waits=[t_l2])
                if wi == 1:
                    dve(lambda E, sl=sl: E.scalar_tensor_tensor(
                        sl, sl, taunu[:, 0:1], tmp[0][:, :512], ALU.mult, ALU.mult),
                        waits=[t_iv, t_tn])
                else:
                    dve(lambda E, sl=sl: E.tensor_mul(sl, sl, tmp[0][:, :512]),
                        waits=[t_iv])
        t_nrm = p.tok("dve")
        if wi < 2:
            sy(lambda E, dnm=dnm: E.dma_start(
                out=dnm.ap().rearrange("h p t -> p h t"),
                in_=bufB[:, :H1 * SEQ].rearrange("p (h t) -> p h t", h=H1)),
                waits=[t_nrm], inc="sp")
        else:
            for tb2 in range(SB):
                for hh in range(H1):
                    tr([t_nrm, t_id] + ([pT_rd[0]] if hh == 0 else []),
                       pT[:, hh * 128:(hh + 1) * 128],
                       bufB[:, hh * SEQ + tb2 * 128: hh * SEQ + tb2 * 128 + 128],
                       ident[:, :])
                t_tt = p.tok("pe")
                t_pe3 = dve(lambda E: E.tensor_copy(stage[:, :1024], pT[:, :1024]),
                            waits=[t_tt, p.tok("sp")])
                pT_rd[0] = t_pe3
                sy(lambda E, tb2=tb2: E.dma_start(out=v1_d.ap()[tb2],
                                                  in_=stage[:, :1024]),
                   waits=[t_pe3], inc="sp")
    t_qkv1 = p.tok("sp")
    dbg_out("q1", q1_d.ap(), [t_qkv1])
    dbg_out("k1", k1_d.ap(), [t_qkv1])
    dbg_out("v1", v1_d.ap(), [t_qkv1])

    E0pe = p.tok("pe")
    E0act = p.tok("act")
    hslot_rd = [[E0pe], [E0pe]]
    qslot_rd = [[E0pe, t_qkv1], [E0pe, t_qkv1]]
    dgslot_rd = [[E0pe, t_qkv1], [E0pe, t_qkv1]]
    outslot_rd = [[t_qkv1], [t_qkv1]]
    eslot_rd = [[E0pe, t_qkv1], [E0pe, t_qkv1]]
    etslot_rd = [[E0pe], [E0pe]]
    dstg_rd = [[t_qkv1], [t_qkv1]]
    pz_rd = [[E0act], [E0act]]
    hload = [None, None]
    HSLOT = SEQ + SB * 128
    iters8 = [(h, qb) for h in range(H1) for qb in range(SB)]

    def phaseA(i):
        h, qb = iters8[i]
        ip = i % 2
        hp = h % 2
        abase = hp * HSLOT
        if qb == 0:
            t_kl = sy(lambda E, h=h, abase=abase: E.dma_start(
                out=bufA[:, abase:abase + SEQ], in_=k1_d.ap()[h]),
                waits=[t_qkv1] + hslot_rd[hp])
            t_vl = sy(lambda E, h=h, abase=abase: E.dma_start(
                out=bufA[:, abase + SEQ:abase + HSLOT].rearrange(
                    "p (b t) -> p b t", b=SB),
                in_=v1_d.ap()[:, :, h * 128:(h + 1) * 128].rearrange(
                    "b p t -> p b t")),
                waits=[t_qkv1] + hslot_rd[hp], inc="xd2")
            hload[hp] = (t_kl, t_vl)
        t_kl, t_vl = hload[hp]
        qoff = ip * 128
        t_ql = sy(lambda E, h=h, qb=qb, qoff=qoff: E.dma_start(
            out=stage[:, qoff:qoff + 128],
            in_=q1_d.ap()[h][:, qb * 128:(qb + 1) * 128]),
            waits=[t_qkv1] + qslot_rd[ip])
        pcol = ip * 8
        t_z = dve(lambda E, pcol=pcol: E.memset(tmp[3][:, pcol:pcol + 8], 0.0),
                  waits=pz_rd[ip])
        nk = qb + 1
        ebase = ip * 4096
        t_sums = [t_z]
        t_mk_last = None
        for kc in range(cdiv(nk * 128, 512)):
            k0 = kc * 512
            wk = min(512, nk * 128 - k0)
            bi = 2 * ip + kc % 2
            bank = pb[bi]
            t_sc = mm([t_ql, t_kl, bank_rd[bi]], bank[:, :wk],
                      stage[:, qoff:qoff + 128],
                      bufA[:, abase + k0:abase + k0 + wk])
            ndiag = 128 if k0 + wk == nk * 128 else 0
            wfull = wk - ndiag
            if wfull > 0:
                t_ex = act(lambda E, bank=bank, k0=k0, wfull=wfull, kc=kc,
                           ebase=ebase, pcol=pcol:
                           E.activation(bufB[:, ebase + k0:ebase + k0 + wfull],
                                        bank[:, :wfull], AF.Exp,
                                        accum_out=tmp[3][:, pcol + kc:
                                                         pcol + kc + 1]),
                           waits=[t_sc, t_z] + eslot_rd[ip])
                t_sums.append(t_ex)
                bank_rd[bi] = t_ex
            if ndiag:
                soff = ip * 128
                t_ed = act(lambda E, bank=bank, wfull=wfull, soff=soff:
                           E.activation(stgf[:, soff:soff + 128],
                                        bank[:, wfull:wfull + 128], AF.Exp),
                           waits=[t_sc] + dstg_rd[ip])
                bank_rd[bi] = t_ed
                t_mk_last = (t_ed, k0, wfull)
        qslot_rd[ip] = [p.tok("pe")]
        return {"h": h, "qb": qb, "ip": ip, "nk": nk, "ebase": ebase,
                "abase": abase, "pcol": pcol, "t_sums": t_sums, "t_vl": t_vl,
                "t_z": t_z, "diag": t_mk_last}

    def phaseB(C):
        h, qb, ip = C["h"], C["qb"], C["ip"]
        nk, ebase, abase, pcol = C["nk"], C["ebase"], C["abase"], C["pcol"]
        t_sums = C["t_sums"]
        t_ed, k0d, wfd = C["diag"]
        t_mk = dve(lambda E, k0=k0d, wfull=wfd, ebase=ebase, ip=ip:
                   E.tensor_mul(bufB[:, ebase + k0 + wfull:
                                     ebase + k0 + wfull + 128],
                                stgf[:, ip * 128:ip * 128 + 128], cmask[:, :]),
                   waits=[t_ed, t_cm] + eslot_rd[ip])
        dstg_rd[ip] = [t_mk]
        t_ds = dve(lambda E, k0=k0d, wfull=wfd, ebase=ebase, pcol=pcol:
                   E.reduce_sum(tmp[3][:, pcol + 7:pcol + 8],
                                bufB[:, ebase + k0 + wfull:
                                     ebase + k0 + wfull + 128],
                                axis=AX.X), waits=[t_mk, C["t_z"]])
        t_sums = t_sums + [t_ds]
        pz_rd[ip] = list(t_sums)
        rcol = 3 * ip
        t_rs = dve(lambda E, pcol=pcol, rcol=rcol: E.reduce_sum(
            tmp[2][:, rcol:rcol + 1], tmp[3][:, pcol:pcol + 8], axis=AX.X),
            waits=t_sums)
        t_iv = dve(lambda E, rcol=rcol: E.reciprocal(
            tmp[2][:, rcol + 1:rcol + 2], tmp[2][:, rcol:rcol + 1]),
            waits=[t_rs])
        t_sc2 = dve(lambda E, rcol=rcol: E.tensor_scalar(
            tmp[2][:, rcol + 2:rcol + 3], tmp[2][:, rcol + 1:rcol + 2],
            taunu[:, 1:2], None, ALU.mult), waits=[t_iv, t_tn])
        dgoff = 256 + ip * 256
        t_dg = dve(lambda E, rcol=rcol, dgoff=dgoff: E.tensor_scalar(
            stage[:, dgoff:dgoff + 128], ident[:, :],
            tmp[2][:, rcol + 2:rcol + 3], None, ALU.mult),
            waits=[t_sc2, t_id] + dgslot_rd[ip])
        t_pe4s = []
        for kb4 in range(cdiv(nk, 4)):
            nkb = min(4, nk - kb4 * 4)
            for kk in range(nkb):
                kb = kb4 * 4 + kk
                mm([t_dg] + ([bank_rd[5 + kb4 % 2]] if kk == 0 else []),
                   pb[5 + kb4 % 2][:, kk * 128:(kk + 1) * 128],
                   bufB[:, ebase + kb * 128:ebase + (kb + 1) * 128],
                   stage[:, dgoff:dgoff + 128])
            t_et = p.tok("pe")
            t_pe4 = dve(lambda E, kb4=kb4, nkb=nkb, ebase=ebase: E.tensor_copy(
                bufB[:, ebase + 2048 + kb4 * 512:
                     ebase + 2048 + kb4 * 512 + nkb * 128],
                pb[5 + kb4 % 2][:, :nkb * 128]),
                waits=[t_et] + etslot_rd[ip])
            etslot_rd[ip] = []
            bank_rd[5 + kb4 % 2] = t_pe4
            t_pe4s.append(t_pe4)
        t_eTend = p.tok("pe")
        dgslot_rd[ip] = [t_eTend]
        eslot_rd[ip] = [t_eTend]
        pvb = 4 if ip == 0 else 2
        for kb in range(nk):
            mm(t_pe4s + [C["t_vl"]] + ([bank_rd[pvb]] if kb == 0 else []),
               pb[pvb][:, :128],
               bufA[:, abase + SEQ + kb * 128:abase + SEQ + (kb + 1) * 128],
               bufB[:, ebase + 2048 + kb * 128:ebase + 2048 + (kb + 1) * 128],
               start=(kb == 0), stop=(kb == nk - 1))
        t_pv = p.tok("pe")
        etslot_rd[ip] = [t_pv]
        if qb == SB - 1:
            hslot_rd[h % 2] = [t_pv]
        ooff = 384 + ip * 256
        t_oe = dve(lambda E, ooff=ooff, pvb=pvb: E.tensor_copy(
            stage[:, ooff:ooff + 128], pb[pvb][:, :128]),
            waits=[t_pv] + outslot_rd[ip])
        bank_rd[pvb] = t_oe
        t_od = sy(lambda E, h=h, qb=qb, ooff=ooff: E.dma_start(
            out=att_own.ap()[h][:, qb * 128:(qb + 1) * 128],
            in_=stage[:, ooff:ooff + 128]), waits=[t_oe], inc="sp")
        outslot_rd[ip] = [t_od]

    ctx8 = phaseA(0)
    for i8 in range(len(iters8)):
        nxt = phaseA(i8 + 1) if i8 + 1 < len(iters8) else None
        phaseB(ctx8)
        ctx8 = nxt
    t_att = p.tok("sp")
    dbg_out("att", att_own.ap(), [t_att])

    t_cc2 = p.op("gpsimd", lambda E: E.collective_compute(
        "AllGather", ALU.bypass, replica_groups=groups,
        ins=[att_own.ap().opt()], outs=[att_full.ap().opt()]),
        waits=[t_att], inc="cc", selfwait=False)

    t_xt0 = gdma(lambda E: E.dma_start(out=xt_d.ap()[:, :], in_=xn1_own.ap()[:, :]),
                 waits=[t_xn1])
    t_round_done = t_xt0
    t_w6all = [None]
    for rnd in range(2):
        for j5 in range(NT5):
            a5 = j5 * 512
            w5 = min(512, TO - a5)
            t_xl = sy(lambda E, a5=a5, w5=w5: E.dma_start(
                out=bufB[:, :FC * w5].rearrange("p (c t) -> p c t", t=w5),
                in_=xtv[:, :, a5:a5 + w5]), waits=[t_round_done] + engtoks())
            xvv = bufB[:, :FC * w5].rearrange("p (c t) -> p c t", t=w5)
            sg, t_wg = wload(I[f"wgate1_{rnd}"], FC * 8)
            for fc in range(FC):
                mm([t_wg, t_xl] + ([bank_rd[4]] if fc == 0 else []), pb[4][:8, :w5],
                   ws[sg][:, fc * 8:(fc + 1) * 8], xvv[:, fc, :],
                   start=(fc == 0), stop=(fc == FC - 1))
            t_lg = p.tok("pe")
            wfree(sg, t_lg)
            t_le = act(lambda E, w5=w5, rnd=rnd: E.activation(
                stgf[:8, :w5], pb[4][:8, :w5], AF.Identity,
                bias=bg1[:8, rnd:rnd + 1]), waits=[t_lg, t_bg0, t_bg1])
            bank_rd[4] = t_le
            for tb2 in range(w5 // 128):
                tt0 = a5 + tb2 * 128
                t_tr2 = tr([t_le, t_idf, bank_rd[5]], pb[5][:, :8],
                           stgf[:8, tb2 * 128:(tb2 + 1) * 128], identf[:8, :8])
                t_tm = dve(lambda E: E.tensor_copy(stgf[:, 512:520], pb[5][:, :8]),
                           waits=[t_tr2])
                bank_rd[5] = t_tm
                lg = stgf[:, 512:520]
                t_m1 = dve(lambda E: E.tensor_reduce(stgf[:, 520:521], lg, AX.X,
                                                     ALU.min), waits=[t_tm])
                t_e1 = dve(lambda E: E.tensor_scalar(tmp[0][:, :8], lg,
                                                     stgf[:, 520:521], None,
                                                     ALU.is_le), waits=[t_m1])
                t_xu = dve(lambda E: E.scalar_tensor_tensor(
                    tmp[1][:, :8], tmp[0][:, :8], 1e30, lg, ALU.mult, ALU.add),
                    waits=[t_e1])
                t_m2 = dve(lambda E: E.tensor_reduce(stgf[:, 521:522], tmp[1][:, :8],
                                                     AX.X, ALU.min), waits=[t_xu])
                t_top = dve(lambda E: E.tensor_scalar(tmp[0][:, :8], lg,
                                                      stgf[:, 521:522], None,
                                                      ALU.is_gt),
                            waits=[t_m2, t_tm])
                t_mx = dve(lambda E: E.reduce_max(stgf[:, 522:523], lg, axis=AX.X),
                           waits=[t_tm])
                t_mn = dve(lambda E: E.tensor_scalar(stgf[:, 523:524],
                                                     stgf[:, 522:523], -1.0, None,
                                                     ALU.mult), waits=[t_mx])
                t_ee = act(lambda E: E.activation(tmp[1][:, :8], lg, AF.Exp,
                                                  bias=stgf[:, 523:524]),
                           waits=[t_mn])
                t_em = dve(lambda E: E.tensor_mul(tmp[1][:, :8], tmp[1][:, :8],
                                                  tmp[0][:, :8]),
                           waits=[t_ee, t_top])
                t_ss = dve(lambda E: E.reduce_sum(stgf[:, 524:525], tmp[1][:, :8],
                                                  axis=AX.X), waits=[t_em])
                t_rc = dve(lambda E: E.reciprocal(stgf[:, 525:526], stgf[:, 524:525]),
                           waits=[t_ss])
                t_wv = dve(lambda E: E.tensor_scalar(tmp[1][:, :8], tmp[1][:, :8],
                                                     stgf[:, 525:526], None,
                                                     ALU.mult), waits=[t_rc])
                t_tw2 = tr([t_wv, t_idf, bank_rd[6]], pb[6][:8, :128],
                           tmp[1][:, :8], identf[:, :128])
                t_we2 = dve(lambda E: E.tensor_copy(stgf[:8, 640:768],
                                                    pb[6][:8, :128]),
                            waits=[t_tw2, p.tok("sp")])
                bank_rd[6] = t_we2
                sy(lambda E, tt0=tt0: E.dma_start(
                    out=wfm1_d.ap()[:8, tt0:tt0 + 128], in_=stgf[:8, 640:768]),
                    waits=[t_we2], inc="sp")
        t_gate = p.tok("dve")
        t_w1st = p.tok("sp")

        for th5 in range(NT5):
            a5 = th5 * 512
            w5 = min(512, TO - a5)
            t_xl = sy(lambda E, a5=a5, w5=w5: E.dma_start(
                out=bufB[:, :FC * w5].rearrange("p (c t) -> p c t", t=w5),
                in_=xtv[:, :, a5:a5 + w5]),
                waits=[t_round_done, t_gate] + engtoks())
            xvv = bufB[:, :FC * w5].rearrange("p (c t) -> p c t", t=w5)
            t_res = sy(lambda E, a5=a5, w5=w5: E.dma_start(
                out=bufA[:, :FC * w5].rearrange("p (c t) -> p c t", t=w5),
                in_=xn1v[:, :, a5:a5 + w5]), waits=[t_xn1] + engtoks(), inc="xd2")
            t_w6 = sy(lambda E, a5=a5, w5=w5: E.dma_start(
                out=w6[:NE1, :w5], in_=wfm1_d.ap()[:NE1, a5:a5 + w5]),
                waits=[t_w1st, p.tok("pe")], inc="xd2")
            first = [True]
            O_H1b, O_H2b = FC * 512, FC * 512 + 8 * 512
            for n in range(NE1):
                t_rr = sy(lambda E, n=n, a5=a5, w5=w5: E.dma_start(
                    out=lrow[:1, :w5], in_=wfm1_d.ap()[n:n + 1, a5:a5 + w5]),
                    waits=[t_w1st, p.tok("pe"), p.tok("dve")])
                t_bc = mm([t_rr, t_ofr, bank_rd[3]], pb[3][:, :w5], onesfr[:1, :],
                          lrow[:1, :w5])
                t_wbc = dve(lambda E, w5=w5: E.tensor_copy(bufC[:, :w5],
                                                           pb[3][:, :w5]),
                            waits=[t_bc])
                bank_rd[3] = t_wbc
                for oc8 in range(8):
                    bu_ = [0, 4][oc8 % 2]
                    sw, t_w = wload(I[f"wup1_{n}"][:, oc8 * D:(oc8 + 1) * D], D)
                    wv_ = ws[sw][:, :D].rearrange("p (c o) -> p c o", o=128)
                    for fc in range(FC):
                        mm([t_w, t_xl] + ([bank_rd[bu_]] if fc == 0 else []),
                           pb[bu_][:, :w5], wv_[:, fc, :], xvv[:, fc, :],
                           start=(fc == 0), stop=(fc == FC - 1))
                    t_mmu = p.tok("pe")
                    wfree(sw, t_mmu)
                    t_h1 = act(lambda E, oc8=oc8, n=n, w5=w5, bu_=bu_: E.activation(
                        bufB[:, O_H1b + oc8 * 512: O_H1b + oc8 * 512 + w5],
                        pb[bu_][:, :w5], AF.Identity,
                        bias=bup1[:, n * 8 + oc8:n * 8 + oc8 + 1]),
                        waits=[t_mmu, t_bup])
                    bank_rd[bu_] = t_h1
                t_h1a = p.tok("act")
                for oc8 in range(8):
                    bw_ = [1, 5][oc8 % 2]
                    sw, t_w = wload(I[f"wsw1_{n}"][:, oc8 * E1:(oc8 + 1) * E1], E1)
                    wv_ = ws[sw][:, :E1].rearrange("p (c o) -> p c o", o=128)
                    for fcf in range(8):
                        mm([t_w, t_h1a] + ([bank_rd[bw_]] if fcf == 0 else []),
                           pb[bw_][:, :w5], wv_[:, fcf, :],
                           bufB[:, O_H1b + fcf * 512: O_H1b + fcf * 512 + w5],
                           start=(fcf == 0), stop=(fcf == 7))
                    t_mmw = p.tok("pe")
                    wfree(sw, t_mmw)
                    t_sl = act(lambda E, oc8=oc8, n=n, w5=w5, bw_=bw_: E.activation(
                        stage[:, :w5], pb[bw_][:, :w5], AF.Silu,
                        bias=bsw1[:, n * 8 + oc8:n * 8 + oc8 + 1]),
                        waits=[t_mmw, t_bsw, p.tok("dve")])
                    bank_rd[bw_] = t_sl
                    t_h0 = dve(lambda E, oc8=oc8, w5=w5: E.tensor_mul(
                        tmp[0][:, :w5], stage[:, :w5],
                        bufB[:, O_H1b + oc8 * 512: O_H1b + oc8 * 512 + w5]),
                        waits=[t_sl])
                    dve(lambda E, oc8=oc8, w5=w5: E.tensor_mul(
                        bufB[:, O_H2b + oc8 * 512: O_H2b + oc8 * 512 + w5],
                        tmp[0][:, :w5], bufC[:, :w5]), waits=[t_h0, t_wbc])
                t_hx = p.tok("dve")
                for oc in range(FC):
                    bi = 2 + (oc % 2)
                    bank = pb[bi]
                    sd, t_d = wload(I[f"wdn1_{n}"][:, oc * E1:(oc + 1) * E1], E1)
                    wdv = ws[sd][:, :E1].rearrange("p (c o) -> p c o", o=128)
                    for fcf in range(8):
                        mm([t_d, t_hx] + ([bank_rd[bi]] if fcf == 0 else []),
                           bank[:, :w5], wdv[:, fcf, :],
                           bufB[:, O_H2b + fcf * 512: O_H2b + fcf * 512 + w5],
                           start=(fcf == 0), stop=(fcf == 7))
                    t_dn = p.tok("pe")
                    wfree(sd, t_dn)
                    if first[0]:
                        t_ac = dve(lambda E, oc=oc, w5=w5, bank=bank: E.tensor_copy(
                            xch[:, oc * w5:(oc + 1) * w5], bank[:, :w5]),
                            waits=[t_dn])
                    else:
                        t_ac = dve(lambda E, oc=oc, w5=w5, bank=bank: E.tensor_add(
                            xch[:, oc * w5:(oc + 1) * w5],
                            xch[:, oc * w5:(oc + 1) * w5], bank[:, :w5]),
                            waits=[t_dn])
                    bank_rd[bi] = t_ac
                first[0] = False
            for oc in range(FC):
                sw, t_w = wload(I["wsu1"][:, oc * D:(oc + 1) * D], D)
                wv_ = ws[sw][:, :D].rearrange("p (c o) -> p c o", o=128)
                bi = [0, 4][oc % 2]
                for fc in range(FC):
                    mm([t_w, t_xl] + ([bank_rd[bi]] if fc == 0 else []),
                       pb[bi][:, :w5], wv_[:, fc, :], xvv[:, fc, :],
                       start=(fc == 0), stop=(fc == FC - 1))
                t_mmu = p.tok("pe")
                wfree(sw, t_mmu)
                t_su = act(lambda E, oc=oc, w5=w5, bi=bi: E.activation(
                    bufB[:, O_H1b + oc * 512: O_H1b + oc * 512 + w5]
                    if oc < 8 else
                    bufB[:, O_H2b + (oc - 8) * 512: O_H2b + (oc - 8) * 512 + w5],
                    pb[bi][:, :w5], AF.Identity,
                    bias=biases[:, B_BSU + oc:B_BSU + oc + 1]),
                    waits=[t_mmu, t_b])
                bank_rd[bi] = t_su
            t_sua = p.tok("act")
            for s in range(2):
                base = O_H1b if s == 0 else O_H2b
                for oc8 in range(8):
                    bw_ = [1, 5][oc8 % 2]
                    sw, t_w = wload(I["wsa1"][:, oc8 * E1:(oc8 + 1) * E1], E1)
                    wv_ = ws[sw][:, :E1].rearrange("p (c o) -> p c o", o=128)
                    for fcf in range(8):
                        mm([t_w, t_sua] + ([bank_rd[bw_]] if fcf == 0 else []),
                           pb[bw_][:, :w5], wv_[:, fcf, :],
                           bufB[:, base + fcf * 512: base + fcf * 512 + w5],
                           start=(fcf == 0), stop=(fcf == 7))
                    t_mmw = p.tok("pe")
                    wfree(sw, t_mmw)
                    t_sl = act(lambda E, oc8=oc8, w5=w5, bw_=bw_: E.activation(
                        stage[:, :w5], pb[bw_][:, :w5], AF.Silu,
                        bias=biases[:, B_BSA + oc8:B_BSA + oc8 + 1]),
                        waits=[t_mmw, t_b, p.tok("dve")])
                    bank_rd[bw_] = t_sl
                    dve(lambda E, s=s, oc8=oc8, base=base, w5=w5: E.tensor_mul(
                        bufB[:, (s * 8 + oc8) * 512: (s * 8 + oc8) * 512 + w5],
                        stage[:, :w5],
                        bufB[:, base + oc8 * 512: base + oc8 * 512 + w5]),
                        waits=[t_sl])
            t_hsh = p.tok("dve")
            for oc in range(FC):
                bi = 2 + (oc % 2)
                bank = pb[bi]
                sd, t_d = wload(I["wsd1"][:, oc * D:(oc + 1) * D], D)
                wdv = ws[sd][:, :D].rearrange("p (c o) -> p c o", o=128)
                for fcf in range(FC):
                    mm([t_d, t_hsh] + ([bank_rd[bi]] if fcf == 0 else []),
                       bank[:, :w5], wdv[:, fcf, :],
                       bufB[:, fcf * 512: fcf * 512 + w5],
                       start=(fcf == 0), stop=False)
                mm([t_bdn, t_w6], bank[:, :w5], bdn1[:NE1, oc * 128:(oc + 1) * 128],
                   w6[:NE1, :w5], start=False, stop=True)
                t_dn = p.tok("pe")
                wfree(sd, t_dn)
                t_ac = dve(lambda E, oc=oc, w5=w5, bank=bank: E.tensor_add(
                    xch[:, oc * w5:(oc + 1) * w5], xch[:, oc * w5:(oc + 1) * w5],
                    bank[:, :w5]), waits=[t_dn])
                bank_rd[bi] = t_ac
            t_wr = None
            for fc in range(FC):
                t_n1r = dve(lambda E, fc=fc, w5=w5: E.scalar_tensor_tensor(
                    stage[:, 512:512 + w5], xch[:, fc * w5:(fc + 1) * w5],
                    biases[:, B_BSD + fc:B_BSD + fc + 1],
                    bufA[:, fc * w5:(fc + 1) * w5], ALU.add, ALU.add),
                    waits=[t_res, t_b, p.tok("sp")])
                t_wr = sy(lambda E, fc=fc, a5=a5, w5=w5: E.dma_start(
                    out=xtv[:, fc, a5:a5 + w5], in_=stage[:, 512:512 + w5]),
                    waits=[t_n1r], inc="sp")
        t_round_done = p.tok("sp")
    t_ffn = t_round_done

    for tb in range(TB):
        tt0 = tb * 128
        for fc in range(FC):
            gh, hl = fc // 8, fc % 8
            t_x1l = sy(lambda E, fc=fc, tt0=tt0: E.dma_start(
                out=stgf[:, :128], in_=x1v[:, fc, tt0:tt0 + 128]),
                waits=[t_x1] + engtoks())
            t_xtl = sy(lambda E, fc=fc, tt0=tt0: E.dma_start(
                out=stage[:, :128], in_=xtv[:, fc, tt0:tt0 + 128]),
                waits=[t_ffn] + engtoks(), inc="xd2")
            t_aL = sy(lambda E, gh=gh, hl=hl, tt0=tt0: E.dma_start(
                out=stage[:, 128:256],
                in_=att_full.ap()[gh, hl][:, tt0:tt0 + 128]),
                waits=[t_cc2] + engtoks(), inc="od")
            t_aH = sy(lambda E, gh=gh, hl=hl, tt0=tt0: E.dma_start(
                out=stage[:, 256:384],
                in_=att_full.ap()[gh, hl][:, TO + tt0:TO + tt0 + 128]),
                waits=[t_cc2] + engtoks(), inc="yd")
            t_s1 = dve(lambda E: E.tensor_scalar(tmp[0][:, :128], stage[:, 128:256],
                                                 hsel[:, 0:1], None, ALU.mult),
                       waits=[t_aL, t_hsel])
            t_s2 = dve(lambda E: E.scalar_tensor_tensor(
                tmp[0][:, :128], stage[:, 256:384], hsel[:, 1:2], tmp[0][:, :128],
                ALU.mult, ALU.add), waits=[t_aH, t_s1])
            t_s3 = dve(lambda E: E.tensor_add(tmp[1][:, :128], stgf[:, :128],
                                              stage[:, :128]),
                       waits=[t_x1l, t_xtl])
            t_s4 = dve(lambda E: E.tensor_add(stgf[:, 512:640], tmp[0][:, :128],
                                              tmp[1][:, :128]),
                       waits=[t_s2, t_s3, p.tok("pe")])
            bi = 2 + (fc % 2)
            t_tr3 = tr([t_s4, t_idf, bank_rd[bi]], pb[bi][:, :128],
                       stgf[:, 512:640], identf[:, :])
            t_ev3 = dve(lambda E, fc=fc, bi=bi: E.tensor_copy(
                bufC[:, fc * 128:(fc + 1) * 128], pb[bi][:, :128]),
                waits=[t_tr3, p.tok("yd")])
            bank_rd[bi] = t_ev3
        t_row = p.tok("dve")
        sy(lambda E, tt0=tt0: E.dma_start(out=y[tt0:tt0 + 128, :],
                                          in_=bufC[:, :2048]),
           waits=[t_row], inc="yd")
    p.raw("sync", lambda E, h=sems["yd"], v=p.val["yd"]: E.wait_ge(h, v))
    if p.val["od"] > 0:
        p.raw("sync", lambda E, h=sems["od"], v=p.val["od"]: E.wait_ge(h, v))
    p.raw("sync", lambda E, h=sems["sp"], v=p.val["sp"]: E.wait_ge(h, v))

    with nc.Block() as block:
        @block.tensor
        def _(E):
            for fn in p.streams["tensor"]:
                fn(E)

        @block.vector
        def _(E):
            for fn in p.streams["vector"]:
                fn(E)

        @block.scalar
        def _(E):
            for fn in p.streams["scalar"]:
                fn(E)

        @block.gpsimd
        def _(E):
            for fn in p.streams["gpsimd"]:
                fn(E)

        @block.sync
        def _(E):
            for fn in p.streams["sync"]:
                fn(E)

    return nc, stackx, dbg_aps


def kernel(**inputs):
    TO = 1024
    ncores = 8
    nc, stackx, _ = build_program(TO, ncores)
    maps = host_prepare(inputs, TO, ncores)
    res = run_bass_kernel_spmd(nc, maps, core_ids=list(range(ncores)))
    out = np.concatenate([np.asarray(r["y"]).reshape(TO, D) for r in res.results], 0)
    return out.reshape(4, 2048, D)
